# version 07
# baseline (speedup 1.0000x reference)
import numpy as np
import concourse.bass as bass
import concourse.mybir as mybir
from concourse.bass_utils import run_bass_kernel_spmd

F32 = mybir.dt.float32
BF16 = mybir.dt.bfloat16
I32 = mybir.dt.int32
ALU = mybir.AluOpType
ACT = mybir.ActivationFunctionType
AX = mybir.AxisListType

D = 1024
S = 4096
DC = 8
NF = 3072
F1 = 4096
NEG = -1.0e30
EPS = 1e-5
QA = 32767.0
QW = 7.0
MAGIC = 12582912.0

ENGINES = ("sync", "tensor", "vector", "scalar", "gpsimd")


class Builder:
    def __init__(self, nc):
        self.nc = nc
        self.prog = {e: [] for e in ENGINES}
        self.sems = {}
        self.cnt = {}
        for k in ("LD", "GP", "TE", "VE", "SE", "CC"):
            self.sems[k] = nc.alloc_semaphore(name="sem" + k)
            self.cnt[k] = 0
        self.order_owner = {"TE": "tensor", "VE": "vector", "SE": "scalar"}
        self.lastw = {e: {} for e in ENGINES}
        self.free = {}
        self.bar = []

    def barrier(self):
        self.bar = [(s, self.cnt[s]) for s in self.cnt]
        self.free = {}

    def F(self, key):
        return self.free.get(key, self.bar)

    def setF(self, key, tok):
        self.free[key] = tok

    def emit(self, eng, fn):
        self.prog[eng].append(fn)

    def tok(self, s):
        return (s, self.cnt[s])

    def all_dma_tok(self):
        return [(s, self.cnt[s]) for s in self.cnt
                if s not in ("TE", "VE", "SE", "CC", "LD")]

    def bump(self, s, n=1):
        self.cnt[s] += n
        return (s, self.cnt[s])

    def _wait1(self, eng, tok):
        s, v = tok
        if v <= 0:
            return
        if self.order_owner.get(s) == eng:
            return
        if self.lastw[eng].get(s, 0) >= v:
            return
        self.lastw[eng][s] = v
        h = self.sems[s]
        self.emit(eng, lambda e, h=h, v=v: e.wait_ge(h, v))

    def waits(self, eng, toks):
        stack = [toks]
        while stack:
            t = stack.pop()
            if t is None:
                continue
            if isinstance(t, list):
                stack.extend(t)
            else:
                self._wait1(eng, t)

    def op(self, eng, sem, fn, deps=(), inc=True):
        self.waits(eng, list(deps))
        if inc or eng in ("vector", "scalar", "gpsimd"):
            t = self.bump(sem)
            h = self.sems[sem]
            self.emit(eng, lambda e, fn=fn, h=h: fn(e).then_inc(h, 1))
            if eng in ("vector", "scalar", "gpsimd"):
                n = t[1]
                self.lastw[eng][sem] = n
                self.emit(eng, lambda e, h=h, n=n: e.wait_ge(h, n))
            return t
        self.emit(eng, lambda e, fn=fn: fn(e))
        return self.tok(sem)

    def t(self, fn, deps=(), inc=True):
        return self.op("tensor", "TE", fn, deps, inc)

    def v(self, fn, deps=(), inc=True):
        return self.op("vector", "VE", fn, deps, inc)

    def s(self, fn, deps=(), inc=True):
        return self.op("scalar", "SE", fn, deps, inc)

    def _dma(self, eng, slot, out_ap, in_ap, deps, group):
        if slot not in self.sems:
            self.sems[slot] = self.nc.alloc_semaphore(name=slot)
            self.cnt[slot] = 0
        dd = list(deps)
        if not group:
            dd.append(self.tok(slot))
        self.waits(eng, dd)
        t = self.bump(slot, 16)
        h = self.sems[slot]
        self.emit(eng, lambda e, o=out_ap, i=in_ap, h=h:
                  e.dma_start(out=o, in_=i).then_inc(h, 16))
        return t

    def ld(self, out_ap, in_ap, deps=(), slot="Lgen", group=False):
        return self._dma("sync", slot, out_ap, in_ap, deps, group)

    def sp(self, out_ap, in_ap, deps=(), slot="Sgen", group=False):
        return self._dma("gpsimd", slot, out_ap, in_ap, deps, group)

    def g(self, fn, deps=(), inc=True):
        return self.op("gpsimd", "GP", fn, deps, inc)


def build_nc(dbg=False, upto=99):
    nc = bass.Bass("TRN2", target_bir_lowering=False)
    b = Builder(nc)
    scratch_kind = "ExternalOutput" if dbg else "Internal"

    d_x = nc.dram_tensor("x", [S, D], F32, kind="ExternalInput").ap()
    d_wdec = nc.dram_tensor("wdec", [D, D], F32, kind="ExternalInput").ap()
    d_wout = nc.dram_tensor("wout", [D, 2 * D], F32, kind="ExternalInput").ap()
    d_bout = nc.dram_tensor("bout", [128, 8], F32, kind="ExternalInput").ap()
    d_wgate = nc.dram_tensor("wgate", [NF, D], F32, kind="ExternalInput").ap()
    d_wup = nc.dram_tensor("wup", [NF, D], F32, kind="ExternalInput").ap()
    d_wdown = nc.dram_tensor("wdown", [D, NF], F32, kind="ExternalInput").ap()
    d_Wg = nc.dram_tensor("Wg", [D, D], F32, kind="ExternalInput").ap()
    d_Wv = nc.dram_tensor("Wv", [D, D], F32, kind="ExternalInput").ap()
    d_Wd = nc.dram_tensor("Wd", [D, D], F32, kind="ExternalInput").ap()
    d_bg = nc.dram_tensor("bg", [128, 8], F32, kind="ExternalInput").ap()
    d_bv = nc.dram_tensor("bv", [128, 8], F32, kind="ExternalInput").ap()
    d_bd = nc.dram_tensor("bd", [128, 8], F32, kind="ExternalInput").ap()
    d_rw = nc.dram_tensor("rw", [D, 8], F32, kind="ExternalInput").ap()
    d_w1 = nc.dram_tensor("w1", [D, F1], F32, kind="ExternalInput").ap()
    d_b1 = nc.dram_tensor("b1", [128, 32], F32, kind="ExternalInput").ap()
    d_w2 = nc.dram_tensor("w2", [F1, D], F32, kind="ExternalInput").ap()
    d_b2 = nc.dram_tensor("b2", [128, 8], F32, kind="ExternalInput").ap()
    d_out = nc.dram_tensor("out", [S, D], F32, kind="ExternalOutput").ap()

    def scr(name, shape, dt=F32):
        return nc.dram_tensor(name, shape, dt, kind=scratch_kind).ap()

    d_xntm = scr("s_xntm", [S, D])
    d_xfm = scr("s_xfm", [D, S])
    d_q = scr("s_q", [DC, 128, S])
    d_ctx = scr("s_ctx", [D, S])
    d_satt = scr("s_satt", [D, S])
    d_up = scr("s_up", [NF, S])
    d_gp = scr("s_gp", [NF, S])
    d_m = scr("s_m", [NF, S])
    d_mm = scr("s_mm", [NF, S])
    d_x1 = scr("s_x1", [D, S])
    d_s1 = scr("s_s1", [D, S])
    d_wupT = scr("s_wupT", [24, 8, 128, 256], BF16)
    d_wgateT = scr("s_wgateT", [24, 8, 128, 256], BF16)
    d_wdownT = scr("s_wdownT", [8, 24, 128, 256], BF16)
    d_woutT = scr("s_woutT", [8, 16, 128, 128])
    d_WgT = scr("s_WgT", [8, 8, 128, 256], BF16)
    d_WvT = scr("s_WvT", [8, 8, 128, 256], BF16)
    d_WdT = scr("s_WdT", [8, 8, 128, 256], BF16)
    d_w1h = scr("s_w1h", [D, F1], BF16)
    d_w1l = scr("s_w1l", [D, F1], BF16)
    d_w2h = scr("s_w2h", [F1, D], BF16)
    d_w2l = scr("s_w2l", [F1, D], BF16)
    d_cc_in = [nc.dram_tensor(f"cc_in{i}", [1, 8], F32) for i in range(4)]
    d_cc_out = [nc.dram_tensor(f"cc_out{i}", [1, 8], F32) for i in range(4)]

    sb = lambda n, shp, dt=F32: nc.alloc_sbuf_tensor(n, shp, dt)
    bigA = sb("bigA", [128, DC * S])
    bigB = sb("bigB", [128, 8192])
    tbufA = sb("tbufA", [128, 1024])
    tbufB = sb("tbufB", [128, 1024])
    qa = sb("qa", [128, 1024])
    ctxtm = sb("ctxtm", [128, 1024])
    stage0 = sb("stage0", [128, 512])
    stage1 = sb("stage1", [128, 512])
    wsA = sb("wsA", [128, 2048])
    wsB = sb("wsB", [128, 2048])
    maskT = sb("maskT", [128, 128])
    ident = sb("ident", [128, 128])
    ptA = sb("ptA", [128, 128])
    ptB = sb("ptB", [128, 128])
    rw_bf = sb("rw_bf", [128, 128], BF16)
    sm = sb("sm", [128, 448])
    rowb = sb("rowb", [1, 1536])

    SC = sm[:, 0:16]
    s0acc = sm[:, 16:17]; upacc = sm[:, 17:18]; gacc = sm[:, 18:19]
    macc = sm[:, 19:20]; mmacc = sm[:, 20:21]
    c_sum = sm[:, 21:22]; c_sq = sm[:, 22:23]; c_mean = sm[:, 23:24]
    c_msq = sm[:, 24:25]; c_var = sm[:, 25:26]; c_vstd = sm[:, 26:27]
    c_rstd = sm[:, 27:28]
    rowm = sm[:, 28:29]; nrowm = sm[:, 29:30]; lsum = sm[:, 30:31]
    rlsum = sm[:, 31:32]
    tmp128 = sm[:, 32:33]
    ones128 = sm[:, 34:35]
    eps128 = sm[:, 35:36]
    sb_bout = sm[:, 40:48]; sb_bg = sm[:, 48:56]; sb_bv = sm[:, 56:64]
    sb_bd = sm[:, 64:72]; sb_b2 = sm[:, 72:80]; sb_b1 = sm[:, 80:112]
    sb_rw = sm[:, 112:176]
    probsA = sm[:, 176:184]; probsB = sm[:, 184:192]
    r_m1 = sm[:, 192:193]; r_m2 = sm[:, 193:194]; r_eq = sm[:, 196:204]
    r_q = sm[:, 204:212]; r_den = sm[:, 212:213]; r_rden = sm[:, 213:214]
    r_a1 = sm[:, 214:215]; r_a2 = sm[:, 215:216]
    w_col = sm[:, 240:242]
    dq_sblk = sm[:, 256:352]
    dq_inv = sm[:, 352:448]

    rb_m = rowb[0:1, 128:384]
    rb_msq = rowb[0:1, 384:640]
    rb_m2 = rowb[0:1, 640:896]
    rb_var = rowb[0:1, 640:896]
    rb_vstd = rowb[0:1, 384:640]
    rb_rstd = rowb[0:1, 896:1152]
    rb_wrow = rowb[0:1, 1152:1408]
    ones_1 = rowb[0:1, 1408:1536]

    ps0 = nc.alloc_psum_tensor("ps0", [128, 512], F32)
    ps1 = nc.alloc_psum_tensor("ps1", [128, 512], F32)
    ps2 = nc.alloc_psum_tensor("ps2", [128, 512], F32)
    pc0 = nc.alloc_psum_tensor("pc0", [128, 512], F32)
    pc1 = nc.alloc_psum_tensor("pc1", [128, 512], F32)
    ptp = nc.alloc_psum_tensor("ptp", [128, 512], F32)
    ptp2 = nc.alloc_psum_tensor("ptp2", [128, 512], F32)
    pst = nc.alloc_psum_tensor("pst", [128, 512], F32)
    psp = pc0

    trk = {"n": 0}

    def do_transpose(src_ap, dst_p, dst_f, ident_ap, evac_cb, deps=()):
        k = trk["n"]
        trk["n"] += 1
        bank = ptp if k % 2 == 0 else ptp2
        dst = bank[0:dst_p, 0:dst_f]
        tt = b.t(lambda t, dst=dst, src_ap=src_ap, ident_ap=ident_ap:
                 t.transpose(dst, src_ap, ident_ap),
                 deps=list(deps) + [b.F(("trs", k % 2))])
        ev = evac_cb(dst, tt)
        b.setF(("trs", k % 2), ev)
        return ev

    bigA_bf = bigA[:].bitcast(BF16)
    bigB_bf = bigB[:].bitcast(BF16)
    wsA_bf = wsA[:].bitcast(BF16)
    wsB_bf = wsB[:].bitcast(BF16)

    def xn1_hi(kc, t0, w):
        return bigA_bf[:, 2 * kc * S + t0: 2 * kc * S + t0 + w]

    def xn1_lo(kc, t0, w):
        return bigA_bf[:, 2 * kc * S + S + t0: 2 * kc * S + S + t0 + w]

    for dst, dsrc in ((sb_bout, d_bout), (sb_bg, d_bg), (sb_bv, d_bv),
                      (sb_bd, d_bd), (sb_b1, d_b1), (sb_b2, d_b2)):
        b.ld(dst, dsrc, slot="Lsmall", group=True)
    tsl = b.ld(sb_rw.rearrange("p (c m) -> p c m", m=8),
               d_rw.rearrange("(c p) m -> p c m", p=128),
               slot="Lsmall", group=True)

    tld0 = b.tok("LD")
    b.v(lambda v: v.memset(ones128, 1.0))
    b.v(lambda v: v.memset(eps128, EPS))
    b.v(lambda v: v.memset(rowb[0:1, 32:33], EPS))
    b.v(lambda v: v.memset(ones_1, 1.0))
    b.v(lambda v: v.memset(rowb[0:1, 0:32], 0.0))
    b.v(lambda v: v.memset(stage0[:, 0:128], 0.0))
    b.v(lambda v: v.memset(stage1[:, 0:128], 1.0))
    b.v(lambda v: v.memset(s0acc, 0.0))
    b.v(lambda v: v.memset(upacc, 0.0))
    b.v(lambda v: v.memset(gacc, 0.0))
    b.v(lambda v: v.memset(macc, 0.0))
    tv0 = b.v(lambda v: v.memset(mmacc, 0.0))
    b.g(lambda g: g.affine_select(
        maskT[:], stage0[:, 0:128], [[-1, 128]], ALU.is_gt, NEG,
        base=0, channel_multiplier=1), deps=[tv0])
    tg_id = b.g(lambda g: g.affine_select(
        ident[:], stage1[:, 0:128], [[-1, 128]], ALU.is_equal, 0.0,
        base=0, channel_multiplier=1))
    tg_const = b.g(lambda g: g.engine_nop())

    if upto < 0.3:
        return finish(nc, b)

    b.barrier()

    def dequant_chunk(w_sb, n_in, out_sb, scr_sb, dep_ld, extra_out_dep):
        nblk = n_in // 32
        b.v(lambda v: v.tensor_reduce(
            dq_sblk[:, 0:nblk],
            w_sb.rearrange("p (a bk) -> p a bk", bk=32),
            axis=AX.X, op=ALU.max, apply_absolute_value=True),
            deps=[dep_ld])
        b.v(lambda v: v.tensor_scalar(
            dq_sblk[:, 0:nblk], dq_sblk[:, 0:nblk], 1.0 / QW, 1e-12,
            ALU.mult, ALU.max))
        b.v(lambda v: v.reciprocal(dq_inv[:, 0:nblk], dq_sblk[:, 0:nblk]))
        last = None
        for h in range(n_in // 512):
            sl = slice(h * 512, (h + 1) * 512)
            bsl = slice(h * 16, h * 16 + 16)
            b.v(lambda v, sl=sl, bsl=bsl: v.tensor_tensor(
                scr_sb,
                w_sb[:, sl].rearrange("p (a bk) -> p a bk", bk=32),
                dq_inv[:, bsl].unsqueeze(2).broadcast_to([128, 16, 32]),
                ALU.mult))
            b.v(lambda v: v.tensor_scalar(
                scr_sb, scr_sb, MAGIC, MAGIC, ALU.add, ALU.subtract))
            b.v(lambda v: v.tensor_scalar(
                scr_sb, scr_sb, -(QW + 1.0), QW, ALU.max, ALU.min))
            last = b.v(lambda v, sl=sl, bsl=bsl: v.tensor_tensor(
                out_sb[:, sl],
                scr_sb.rearrange("p (a bk) -> p a bk", bk=32),
                dq_sblk[:, bsl].unsqueeze(2).broadcast_to([128, 16, 32]),
                ALU.mult), deps=[extra_out_dep])
        return last

    def transpose_chunk_to_dram(src_sb, n_in, dram_slot_fn, dep, bf=False):
        for kc in range(n_in // 128):
            slot = kc % 2
            stg = stage1 if slot else stage0

            def ev(dst, tt, kc=kc, slot=slot, stg=stg):
                if not bf:
                    tv = b.v(lambda v: v.tensor_copy(stg[:, 0:128], dst),
                             deps=[tt, b.F(("stP", slot))])
                    tg = b.sp(dram_slot_fn(kc), stg[:, 0:128], deps=[tv],
                              slot=f"Sstg{slot}")
                    b.setF(("stP", slot), tg)
                    return tv
                sbf = stg[:].bitcast(BF16)
                tv = b.v(lambda v: v.tensor_copy(stg[:, 128:256], dst),
                         deps=[tt, b.F(("stP", slot))])
                b.v(lambda v: v.tensor_copy(sbf[:, 0:128], stg[:, 128:256]))
                tv = b.v(lambda v: v.tensor_tensor(
                    sbf[:, 128:256], stg[:, 128:256], sbf[:, 0:128],
                    ALU.subtract))
                tg = b.sp(dram_slot_fn(kc), sbf[:, 0:256], deps=[tv],
                          slot=f"Sstg{slot}")
                b.setF(("stP", slot), tg)
                return tv
            do_transpose(src_sb[:, kc * 128:(kc + 1) * 128], 128, 128,
                         ident[:], ev, deps=[dep, tg_id])
        return b.tok("TE")

    def gen_wsA_items():
        for w_d, dT in ((d_wup, d_wupT), (d_wgate, d_wgateT)):
            for oc in range(24):
                def item(w_d=w_d, dT=dT, oc=oc):
                    t_w = b.ld(wsA[:, 0:1024],
                               w_d[oc * 128:(oc + 1) * 128, :],
                               deps=[b.F("wsA_P")], slot="Lws0")
                    tdq = dequant_chunk(wsA[:, 0:1024], 1024,
                                        bigB[:, 7168:8192],
                                        bigB[:, 5632:6144], t_w,
                                        b.F("dqo_P"))
                    b.setF("wsA_P", tdq)
                    tte = transpose_chunk_to_dram(
                        bigB[:, 7168:8192], 1024,
                        lambda kc, dT=dT, oc=oc: dT[oc, kc], tdq, bf=True)
                    b.setF("dqo_P", tte)
                yield item
        for r in range(8):
            def item(r=r):
                t_w = b.ld(wsA[:, 0:2048], d_wout[r * 128:(r + 1) * 128, :],
                           deps=[b.F("wsA_P")], slot="Lws0")
                tte = transpose_chunk_to_dram(
                    wsA[:, 0:2048], 2048,
                    lambda kc, r=r: d_woutT[r, kc], t_w)
                b.setF("wsA_P", tte)
            yield item
        for w_d, dT in ((d_Wg, d_WgT), (d_Wv, d_WvT), (d_Wd, d_WdT)):
            for r in range(8):
                def item(w_d=w_d, dT=dT, r=r):
                    t_w = b.ld(wsA[:, 0:1024],
                               w_d[r * 128:(r + 1) * 128, :],
                               deps=[b.F("wsA_P")], slot="Lws0")
                    tte = transpose_chunk_to_dram(
                        wsA[:, 0:1024], 1024,
                        lambda dc, dT=dT, r=r: dT[r, dc], t_w, bf=True)
                    b.setF("wsA_P", tte)
                yield item

    def gen_bigB_items():
        for r in range(8):
            def item(r=r):
                t_w = b.ld(bigB[:, 0:3072], d_wdown[r * 128:(r + 1) * 128, :],
                           deps=[b.F("bigB_P")], slot="LbigB")
                tdq = dequant_chunk(bigB[:, 0:3072], 3072,
                                    bigB[:, 3072:6144],
                                    bigB[:, 7168:7680], t_w, b.F("bigB_P2"))
                b.setF("bigB_P", tdq)
                tte = transpose_chunk_to_dram(
                    bigB[:, 3072:6144], 3072,
                    lambda ic, r=r: d_wdownT[r, ic], tdq, bf=True)
                b.setF("bigB_P2", tte)
            yield item
        for name, d_src, d_h, d_l, rows, cols in (
                ("w1", d_w1, d_w1h, d_w1l, D, F1),
                ("w2", d_w2, d_w2h, d_w2l, F1, D)):
            for r in range(rows // 128):
                def item(d_src=d_src, d_h=d_h, d_l=d_l, r=r, cols=cols):
                    t_w = b.ld(bigB[:, 0:cols],
                               d_src[r * 128:(r + 1) * 128, :],
                               deps=[b.F("bigB_P"), b.F("bigB_P2")],
                               slot="LbigB")
                    b.v(lambda v: v.tensor_copy(
                        bigB_bf[:, 8192:8192 + cols], bigB[:, 0:cols]),
                        deps=[t_w])
                    tvl = b.v(lambda v: v.tensor_tensor(
                        bigB_bf[:, 8192 + cols:8192 + 2 * cols],
                        bigB[:, 0:cols],
                        bigB_bf[:, 8192:8192 + cols], ALU.subtract))
                    th = b.sp(d_h[r * 128:(r + 1) * 128, :],
                              bigB_bf[:, 8192:8192 + cols], deps=[tvl],
                              slot=f"Scv{r % 2}")
                    tl = b.sp(d_l[r * 128:(r + 1) * 128, :],
                              bigB_bf[:, 8192 + cols:8192 + 2 * cols],
                              deps=[tvl], slot=f"Scv{2 + r % 2}")
                    b.setF("bigB_P", [th, tl])
                    b.setF("bigB_P2", [th, tl])
                yield item

    gA = gen_wsA_items() if upto >= 2 else iter(())
    gB = gen_bigB_items() if upto >= 2 else iter(())

    TI = S // 128
    if upto < 0.7:
        TI = 2
    xtm = [tbufA, tbufB]
    xnt = [qa, ctxtm]
    sumP = sm[:, 216:218]; sqP = sm[:, 218:220]; meanP = sm[:, 220:222]
    msqP = sm[:, 222:224]; m2P = sm[:, 224:226]; varP = sm[:, 226:228]
    vstdP = sm[:, 228:230]; rstdP = sm[:, 230:232]
    for tp in range(TI // 2):
        tis = (2 * tp, 2 * tp + 1)
        t_xs = []
        for bi in (0, 1):
            t_xs.append(b.ld(xtm[bi][:],
                             d_x[tis[bi] * 128:(tis[bi] + 1) * 128, :],
                             deps=[b.F(("xtm", bi))], slot=f"Lxtm{bi}"))
        for bi in (0, 1):
            b.v(lambda v, x=xtm[bi], bi=bi: v.reduce_sum(
                sumP[:, bi:bi + 1], x[:], axis=AX.X), deps=[t_xs[bi]])
            tse_ = b.s(lambda s, x=xtm[bi], bi=bi: s.activation(
                bigB[:, 6144:7168], x[:], ACT.Square,
                accum_out=sqP[:, bi:bi + 1]),
                deps=[t_xs[bi], b.F("sq_ph0")])
            b.setF("sq_ph0", tse_)
        b.v(lambda v: v.tensor_scalar(meanP, sumP, 1.0 / D, None, ALU.mult))
        b.v(lambda v: v.tensor_scalar(msqP, sqP, 1.0 / D, None, ALU.mult),
            deps=[b.tok("SE")])
        b.v(lambda v: v.tensor_tensor(m2P, meanP, meanP, ALU.mult))
        tvv = b.v(lambda v: v.tensor_tensor(varP, msqP, m2P, ALU.subtract))
        tsv = b.s(lambda s: s.activation(vstdP, varP, ACT.Sqrt, bias=eps128),
                  deps=[tvv])
        b.v(lambda v: v.reciprocal(rstdP, vstdP), deps=[tsv])
        b.v(lambda v: v.engine_nop())
        b.v(lambda v: v.engine_nop())
        for bi in (0, 1):
            ti = tis[bi]
            tv_xn = b.v(lambda v, x=xtm[bi], o=xnt[bi], bi=bi:
                        v.tensor_scalar(
                            o[:], x[:], meanP[:, bi:bi + 1],
                            rstdP[:, bi:bi + 1], ALU.subtract, ALU.mult),
                        deps=[b.F(("xnt", bi))])
            b.v(lambda v, o=xnt[bi]: v.tensor_reduce(
                tmp128, o[:], axis=AX.X, op=ALU.max,
                apply_absolute_value=True))
            b.v(lambda v: v.tensor_max(s0acc, s0acc, tmp128))
            tg1 = b.sp(d_xntm[ti * 128:(ti + 1) * 128, :], xnt[bi][:],
                       deps=[tv_xn], slot=f"Sxnt{bi}")
            for dc in range(DC):
                def ev1(dst, tt, dc=dc, ti=ti):
                    return b.v(lambda v: v.tensor_copy(
                        bigA[:, dc * S + ti * 128: dc * S + ti * 128 + 128],
                        dst), deps=[tt])
                do_transpose(xnt[bi][:, dc * 128:(dc + 1) * 128], 128, 128,
                             ident[:], ev1, deps=[tv_xn, tg_id])
            for dc in range(DC):
                def ev2(dst, tt, dc=dc, ti=ti):
                    tv2 = b.v(lambda v: v.tensor_copy(
                        wsB[:, dc * 128:(dc + 1) * 128], dst),
                        deps=[tt, b.F(("wsB_ph0", dc))])
                    tg2 = b.sp(d_xfm[dc * 128:(dc + 1) * 128,
                                     ti * 128:(ti + 1) * 128],
                               wsB[:, dc * 128:(dc + 1) * 128], deps=[tv2],
                               slot=f"Sxfm{dc % 4}")
                    b.setF(("wsB_ph0", dc), tg2)
                    return tv2
                do_transpose(xtm[bi][:, dc * 128:(dc + 1) * 128], 128, 128,
                             ident[:], ev2, deps=[t_xs[bi], tg_id])
            b.setF(("xtm", bi), [b.tok("VE"), b.tok("SE"), b.tok("TE")])
            b.setF(("xnt", bi), [b.tok("VE"), b.tok("TE"), tg1])
        for _ in range(6):
            it = next(gA, None)
            if it is not None:
                it()
    for it in gA:
        it()
    tok_xn_done = b.tok("VE")
    tok_xn_te = b.tok("TE")
    tok_xn_sp = b.all_dma_tok()
    if upto < 0.9:
        return finish(nc, b)

    for ec in range(DC):
        bi = ec % 2
        ws = wsA if bi == 0 else wsB
        t_w = b.ld(ws[:, 0:1024].rearrange("p (c m) -> p c m", m=128),
                   d_wdec[:, ec * 128:(ec + 1) * 128].rearrange(
                       "(c p) m -> p c m", p=128),
                   deps=[b.F(("qw", bi)), tok_xn_sp, tok_xn_te, b.F("wsA_P")],
                   slot=f"Lws{bi}")
        for tt in range(8):
            pp = ps0 if tt % 2 == 0 else ps1
            mm_tok = None
            for kc in range(DC):
                mm_tok = b.t(lambda t, ws=ws, kc=kc, tt=tt, pp=pp: t.matmul(
                    pp[:], ws[:, kc * 128:(kc + 1) * 128],
                    bigA[:, kc * S + tt * 512: kc * S + tt * 512 + 512],
                    start=(kc == 0), stop=(kc == DC - 1)),
                    deps=[t_w, tok_xn_done, b.F(("psq", tt % 2))],
                    inc=(kc == DC - 1))
            st = stage0 if tt % 2 == 0 else stage1
            tv = b.v(lambda v, pp=pp, st=st: v.tensor_copy(st[:], pp[:]),
                     deps=[mm_tok, b.F(("stP", tt % 2))])
            b.setF(("psq", tt % 2), tv)
            tg = b.sp(d_q[ec, :, tt * 512:(tt + 1) * 512], st[:], deps=[tv],
                      slot=f"Sstg{tt % 2}")
            b.setF(("stP", tt % 2), tg)
        b.setF(("qw", bi), b.tok("TE"))
        for _ in range(6):
            it = next(gB, None)
            if it is not None:
                it()
    for it in gB:
        it()
    if upto < 2:
        return finish(nc, b)
    b.v(lambda v: v.tensor_copy(rw_bf[:, 0:64], sb_rw), deps=[tsl])
    b.v(lambda v: v.tensor_tensor(rw_bf[:, 64:128], sb_rw, rw_bf[:, 0:64],
                                  ALU.subtract))
    tok_wprep = b.all_dma_tok()

    cc_idx = [0]

    def allreduce_max(dep_tok):
        i = cc_idx[0]
        cc_idx[0] += 1
        tg1 = b.sp(d_cc_in[i].ap(), rowb[0:1, 0:8], deps=[dep_tok],
                   slot="Sccin")
        tcc = b.op("gpsimd", "CC", lambda g, i=i: g.collective_compute(
            "AllReduce", ALU.max, replica_groups=[list(range(8))],
            ins=[d_cc_in[i].ap().opt()], outs=[d_cc_out[i].ap().opt()]),
            deps=[tg1])
        return b.sp(rowb[0:1, 8:16], d_cc_out[i].ap(), deps=[tcc],
                    slot="Sccout")

    def finalize_partial_max(acc_ap, dst_col, dep):
        def ev(dst, tt, c=dst_col):
            return b.v(lambda v: v.reduce_max(
                rowb[0:1, c:c + 1], dst, axis=AX.X), deps=[tt])
        return do_transpose(acc_ap, 1, 128, ident[:], ev, deps=[dep, tg_id])

    def bcast_scalars(cols_src, n, dst_sc_col, dep):
        tt = b.t(lambda t, n=n: t.matmul(
            psp[:, 0:n], ones_1, rowb[0:1, cols_src:cols_src + n],
            start=True, stop=True), deps=[dep, b.F("pspR")])
        tv = b.v(lambda v, n=n, c=dst_sc_col: v.tensor_copy(
            SC[:, c:c + n], psp[:, 0:n]), deps=[tt])
        b.setF("pspR", tv)
        b.v(lambda v: v.engine_nop())
        b.v(lambda v: v.engine_nop())
        return b.tok("VE")

    if upto < 3:
        return finish(nc, b)
    tfin = finalize_partial_max(s0acc, 0, tok_xn_done)
    tcc1 = allreduce_max(tfin)
    tv = b.v(lambda v: v.tensor_scalar(
        rowb[0:1, 16:17], rowb[0:1, 8:9], 1.0 / QA, 1e-12, ALU.mult, ALU.max),
        deps=[tcc1])
    tv = b.v(lambda v: v.reciprocal(rowb[0:1, 17:18], rowb[0:1, 16:17]))
    tok_r1 = bcast_scalars(16, 2, 0, tv)

    if upto < 4:
        return finish(nc, b)
    b.barrier()
    raw = bigB
    pt_sb = [ptA, ptB]

    def a_scores(i):
        JT = i // 4 + 1
        k = i % 4
        t_q = b.ld(qa[:].rearrange("p (c m) -> p c m", m=128),
                   d_q[:, :, i * 128:(i + 1) * 128].rearrange(
                       "c p m -> p c m"), deps=[b.F("qa_A"), tok_xn_sp],
                   slot="Lqa")
        for jt in range(JT):
            pp = ps0 if jt % 2 == 0 else ps1
            mm_tok = None
            for kc in range(DC):
                mm_tok = b.t(lambda t, kc=kc, jt=jt, pp=pp: t.matmul(
                    pp[:], qa[:, kc * 128:(kc + 1) * 128],
                    bigA[:, kc * S + jt * 512: kc * S + jt * 512 + 512],
                    start=(kc == 0), stop=(kc == DC - 1)),
                    deps=[t_q, b.F(("psA", jt % 2))],
                    inc=(kc == DC - 1))
            if jt < JT - 1:
                tv = b.v(lambda v, jt=jt, pp=pp: v.tensor_copy(
                    raw[:, jt * 512:(jt + 1) * 512], pp[:]),
                    deps=[mm_tok, b.F("rawP")])
            else:
                base = jt * 512
                if k > 0:
                    b.v(lambda v, pp=pp, base=base, k=k: v.tensor_copy(
                        raw[:, base:base + k * 128], pp[:, 0:k * 128]),
                        deps=[mm_tok, b.F("rawP"), tg_const])
                tv = b.v(lambda v, pp=pp, base=base, k=k:
                         v.tensor_tensor(
                             raw[:, base + k * 128:base + (k + 1) * 128],
                             pp[:, k * 128:(k + 1) * 128], maskT[:], ALU.add),
                         deps=[mm_tok, b.F("rawP"), tg_const])
                if i == 0:
                    tv = b.v(lambda v, pp=pp: v.tensor_copy(
                        raw[0:1, 0:1], pp[0:1, 0:1]))
                if k < 3:
                    tv = b.v(lambda v, base=base, k=k: v.memset(
                        raw[:, base + (k + 1) * 128:base + 512], NEG))
            b.setF(("psA", jt % 2), tv)
        b.setF("qa_A", b.tok("TE"))

    def a_softmax(i):
        W = (i // 4 + 1) * 512
        b.v(lambda v, W=W: v.reduce_max(rowm, raw[:, 0:W], axis=AX.X))
        b.v(lambda v: v.tensor_scalar(nrowm, rowm, -1.0, None, ALU.mult))
        b.v(lambda v: v.engine_nop())
        tvn = b.v(lambda v: v.engine_nop())
        tse = b.s(lambda s, W=W: s.activation(
            bigB[:, 4096:4096 + W], raw[:, 0:W], ACT.Exp, bias=nrowm,
            accum_out=lsum), deps=[tvn, b.F("Pexp")])
        tv_rl = b.v(lambda v: v.reciprocal(rlsum, lsum), deps=[tse])
        b.v(lambda v: v.engine_nop())
        b.v(lambda v: v.engine_nop())
        b.setF("rawP", tse)
        return tse, tv_rl

    def a_ctx(i, tse, tv_rl):
        tm1 = None
        for jc in range(i + 1):
            sl = jc % 2
            xb = tbufA if sl == 0 else tbufB
            t_k = b.ld(xb[:], d_xntm[jc * 128:(jc + 1) * 128, :],
                       deps=[b.F(("xbA", sl)), tok_xn_sp],
                       slot=f"Ltb{sl}")

            def evP(dst, tt, sl=sl):
                return b.v(lambda v: v.tensor_copy(pt_sb[sl][:], dst),
                           deps=[tt, b.F(("ptsbA", sl))])
            tv = do_transpose(
                bigB[:, 4096 + jc * 128:4096 + (jc + 1) * 128], 128, 128,
                ident[:], evP, deps=[tse])
            tm0 = b.t(lambda t, sl=sl, jc=jc, i=i, xb=xb: t.matmul(
                pc0[:], pt_sb[sl][:], xb[:, 0:512],
                start=(jc == 0), stop=(jc == i)),
                deps=[tv, t_k] + ([b.F("pcFree")] if jc == 0 else []),
                inc=(jc == i))
            tm1 = b.t(lambda t, sl=sl, jc=jc, i=i, xb=xb: t.matmul(
                pc1[:], pt_sb[sl][:], xb[:, 512:1024],
                start=(jc == 0), stop=(jc == i)),
                deps=[], inc=True)
            b.setF(("ptsbA", sl), tm1)
            b.setF(("xbA", sl), tm1)
        tv0 = b.v(lambda v: v.tensor_scalar(
            ctxtm[:, 0:512], pc0[:], rlsum, None, ALU.mult),
            deps=[tm1, tv_rl, b.F("ctxtmA")])
        tv1 = b.v(lambda v: v.tensor_scalar(
            ctxtm[:, 512:1024], pc1[:], rlsum, None, ALU.mult))
        b.setF("pcFree", tv1)
        for dc in range(DC):
            stg = stage0 if dc % 2 == 0 else stage1

            def evC(dst, tt, dc=dc, stg=stg, i=i):
                tv = b.v(lambda v: v.tensor_copy(stg[:, 0:128], dst),
                         deps=[tt, b.F(("stA", dc % 2))])
                tg = b.sp(d_ctx[dc * 128:(dc + 1) * 128,
                                i * 128:(i + 1) * 128],
                          stg[:, 0:128], deps=[tv], slot=f"Sstg{dc % 2}")
                b.setF(("stA", dc % 2), tg)
                return tv
            do_transpose(ctxtm[:, dc * 128:(dc + 1) * 128], 128, 128,
                         ident[:], evC, deps=[tv1])
        b.setF("ctxtmA", b.tok("TE"))
        b.setF("Pexp", b.tok("TE"))

    pend = None
    for i in range(TI):
        a_scores(i)
        if pend is not None:
            a_ctx(*pend)
        tse, tv_rl = a_softmax(i)
        pend = (i, tse, tv_rl)
    a_ctx(*pend)
    tok_attn = b.all_dma_tok()

    if upto < 5:
        return finish(nc, b)
    b.barrier()
    for tt in range(8):
        ctx_toks = []
        for kc8 in range(8):
            ctx_toks.append(b.ld(
                bigB[:, kc8 * 512:(kc8 + 1) * 512],
                d_ctx[kc8 * 128:(kc8 + 1) * 128, tt * 512:(tt + 1) * 512],
                deps=[b.F("bigB_A2"), tok_attn], slot="LbigB", group=True))
        t_ctx = ctx_toks[-1]
        for oc in range(8):
            sl = oc % 2
            ws = wsA if sl == 0 else wsB
            tw0 = b.ld(ws[:, 0:1024].rearrange("p (a m) -> p a m", m=128),
                       d_woutT[oc, 0:8].rearrange("a p m -> p a m"),
                       deps=[b.F(("wsA2", sl)), tok_wprep],
                       slot=f"Lws{sl}")
            tw1 = b.ld(ws[:, 1024:2048].rearrange("p (a m) -> p a m", m=128),
                       d_woutT[oc, 8:16].rearrange("a p m -> p a m"),
                       slot=f"Lws{sl}")
            tw0 = tw1
            pp = ps0 if sl == 0 else ps1
            mm_tok = None
            for kc in range(16):
                if kc < 8:
                    rhs = bigA[:, kc * S + tt * 512: kc * S + tt * 512 + 512]
                    dep = [tw0]
                else:
                    rhs = bigB[:, (kc - 8) * 512:(kc - 7) * 512]
                    dep = [tw1, t_ctx]
                mm_tok = b.t(lambda t, kc=kc, pp=pp, ws=ws, rhs=rhs: t.matmul(
                    pp[:], ws[:, kc * 128:(kc + 1) * 128], rhs,
                    start=(kc == 0), stop=(kc == 15)),
                    deps=dep + [b.F(("psA2", sl))], inc=(kc == 15))
            stg = stage0 if sl == 0 else stage1
            tsv = b.s(lambda s, pp=pp, stg=stg, oc=oc: s.activation(
                stg[:], pp[:], ACT.Tanh, bias=sb_bout[:, oc:oc + 1]),
                deps=[mm_tok, tsl, b.F(("stA2", sl))])
            b.setF(("psA2", sl), tsv)
            tg = b.sp(d_satt[oc * 128:(oc + 1) * 128, tt * 512:(tt + 1) * 512],
                      stg[:], deps=[tsv], slot=f"Sstg{sl}")
            b.setF(("stA2", sl), tg)
            b.setF(("wsA2", sl), b.tok("TE"))
        b.setF("bigB_A2", b.tok("TE"))
    tok_a2 = b.all_dma_tok()

    if upto < 6:
        return finish(nc, b)
    b.barrier()
    qa_bf = qa[:].bitcast(BF16)
    ctx_bf = ctxtm[:].bitcast(BF16)
    tbA_bf = tbufA[:].bitcast(BF16)
    tv_xq = None
    for kc in range(DC):
        for h in range(8):
            sl = slice(kc * S + h * 512, kc * S + h * 512 + 512)
            hsl = slice((h % 4) * 512, (h % 4) * 512 + 512)
            b.v(lambda v, sl=sl: v.tensor_scalar(
                stage0[:], bigA[:, sl], SC[:, 1:2], MAGIC, ALU.mult, ALU.add),
                deps=[tok_r1, b.F("xqscr")])
            b.v(lambda v: v.tensor_scalar(
                stage0[:], stage0[:], MAGIC, -(QA + 1.0),
                ALU.subtract, ALU.max))
            b.v(lambda v: v.tensor_scalar(
                stage1[:], stage0[:], QA, SC[:, 0:1], ALU.min, ALU.mult))
            b.v(lambda v, hsl=hsl: v.tensor_copy(qa_bf[:, hsl], stage1[:]))
            lodst = ctx_bf if h < 4 else tbA_bf
            b.v(lambda v, hsl=hsl, lodst=lodst: v.tensor_tensor(
                lodst[:, hsl], stage1[:], qa_bf[:, hsl], ALU.subtract))
            if h == 3:
                b.v(lambda v, kc=kc: v.tensor_copy(
                    bigA_bf[:, 2 * kc * S: 2 * kc * S + 2048],
                    qa_bf[:, 0:2048]))
        b.v(lambda v, kc=kc: v.tensor_copy(
            bigA_bf[:, 2 * kc * S + 2048: 2 * kc * S + 4096], qa_bf[:, 0:2048]))
        b.v(lambda v, kc=kc: v.tensor_copy(
            bigA_bf[:, 2 * kc * S + S: 2 * kc * S + S + 2048],
            ctx_bf[:, 0:2048]))
        tv_xq = b.v(lambda v, kc=kc: v.tensor_copy(
            bigA_bf[:, 2 * kc * S + S + 2048: 2 * kc * S + S + 4096],
            tbA_bf[:, 0:2048]))
        b.setF("xqscr", None)
    tok_xq = tv_xq

    for w_i, (dT, d_pre, acc) in enumerate(
            ((d_wupT, d_up, upacc), (d_wgateT, d_gp, gacc))):
        for oc in range(24):
            sl = oc % 2
            ws = wsA if sl == 0 else wsB
            wbf = wsA_bf if sl == 0 else wsB_bf
            t_w = b.ld(wbf[:, 0:2048].rearrange("p (a m) -> p a m", m=256),
                       dT[oc].rearrange("a p m -> p a m"),
                       deps=[b.F(("wsB1", sl)), tok_wprep],
                       slot=f"Lws{sl}")
            for tt in range(8):
                pp = ps0 if tt % 2 == 0 else ps1
                mm_tok = None
                for kc in range(DC):
                    wh = wbf[:, kc * 256: kc * 256 + 128]
                    wl = wbf[:, kc * 256 + 128: kc * 256 + 256]
                    rh = bigA_bf[:, 2 * kc * S + tt * 512:
                                 2 * kc * S + tt * 512 + 512]
                    rl_ = bigA_bf[:, 2 * kc * S + S + tt * 512:
                                  2 * kc * S + S + tt * 512 + 512]
                    b.t(lambda t, pp=pp, wh=wh, rh=rh, kc=kc: t.matmul(
                        pp[:], wh, rh, start=(kc == 0), stop=False),
                        deps=[t_w, tok_xq, b.F(("psB1", tt % 2))], inc=False)
                    b.t(lambda t, pp=pp, wh=wh, rl_=rl_: t.matmul(
                        pp[:], wh, rl_, start=False, stop=False), inc=False)
                    mm_tok = b.t(lambda t, pp=pp, wl=wl, rh=rh, kc=kc:
                                 t.matmul(pp[:], wl, rh, start=False,
                                          stop=(kc == DC - 1)),
                                 inc=(kc == DC - 1))
                stg = stage0 if tt % 2 == 0 else stage1
                tv = b.v(lambda v, pp=pp, stg=stg: v.tensor_copy(
                    stg[:], pp[:]), deps=[mm_tok, b.F(("stB1", tt % 2))])
                b.setF(("psB1", tt % 2), tv)
                b.v(lambda v, stg=stg: v.tensor_reduce(
                    tmp128, stg[:], axis=AX.X, op=ALU.max,
                    apply_absolute_value=True))
                b.v(lambda v, acc=acc: v.tensor_max(acc, acc, tmp128))
                tg = b.sp(d_pre[oc * 128:(oc + 1) * 128,
                                tt * 512:(tt + 1) * 512], stg[:], deps=[tv],
                          slot=f"Sstg{tt % 2}")
                b.setF(("stB1", tt % 2), tg)
            b.setF(("wsB1", sl), b.tok("TE"))
    tok_p1v = b.tok("VE")
    b.barrier()

    t1 = finalize_partial_max(upacc, 0, tok_p1v)
    t2 = finalize_partial_max(gacc, 1, t1)
    tcc2 = allreduce_max(t2)
    b.v(lambda v: v.tensor_scalar(
        rowb[0:1, 16:17], rowb[0:1, 8:9], 1.0 / QA, 1e-12, ALU.mult, ALU.max),
        deps=[tcc2])
    b.v(lambda v: v.reciprocal(rowb[0:1, 17:18], rowb[0:1, 16:17]))
    b.v(lambda v: v.tensor_scalar(
        rowb[0:1, 18:19], rowb[0:1, 9:10], 1.0 / QA, 1e-12, ALU.mult,
        ALU.max))
    b.v(lambda v: v.reciprocal(rowb[0:1, 19:20], rowb[0:1, 18:19]))
    tv = b.v(lambda v: v.tensor_scalar(
        rowb[0:1, 20:21], rowb[0:1, 18:19], 1.0 / 65536.0, None, ALU.mult))
    tok_r2 = bcast_scalars(16, 5, 2, tv)

    mb_half = [ctxtm[:, 0:512], ctxtm[:, 512:1024]]
    gacc2 = sm[:, 232:233]
    b.v(lambda v: v.memset(gacc2, 0.0))
    for idx in range(192):
        oc, tt = idx // 8, idx % 8
        i2 = idx % 2
        E = b.g if idx % 3 == 2 else b.v
        acc_ = macc
        tmp_ = tmp128
        gp_buf = stage0 if i2 == 0 else stage1
        t_g = b.ld(gp_buf[:],
                   d_gp[oc * 128:(oc + 1) * 128, tt * 512:(tt + 1) * 512],
                   deps=[b.F(("gpB2", i2)), tok_a2], slot=f"Lstg{i2}")
        cbuf = tbufA[:, 0:512] if i2 == 0 else tbufA[:, 512:1024]
        sbuf_ = tbufB[:, 0:512] if i2 == 0 else tbufB[:, 512:1024]
        E(lambda v, gp_buf=gp_buf, cbuf=cbuf: v.tensor_scalar(
            cbuf, gp_buf[:], SC[:, 5:6], MAGIC, ALU.mult, ALU.add),
            deps=[t_g, tok_r2, b.F(("cB2", i2))])
        E(lambda v, cbuf=cbuf: v.tensor_scalar(
            cbuf, cbuf, MAGIC, -(QA + 1.0), ALU.subtract, ALU.max))
        tvc = E(lambda v, cbuf=cbuf: v.tensor_scalar(
            cbuf, cbuf, QA, None, ALU.min))
        tss = b.s(lambda s, cbuf=cbuf, sbuf_=sbuf_: s.activation(
            sbuf_, cbuf, ACT.Sigmoid, scale=SC[:, 4:5]),
            deps=[tvc, b.F(("sigB2", i2))])
        E(lambda v, sbuf_=sbuf_, gp_buf=gp_buf: v.tensor_scalar(
            gp_buf[:], sbuf_, 65536.0, MAGIC, ALU.mult, ALU.add),
            deps=[tss])
        b.setF(("sigB2", i2), b.tok("GP") if idx % 3 == 2 else b.tok("VE"))
        E(lambda v, gp_buf=gp_buf: v.tensor_scalar(
            gp_buf[:], gp_buf[:], MAGIC, 0.0, ALU.subtract, ALU.max))
        E(lambda v, gp_buf=gp_buf: v.tensor_scalar(
            gp_buf[:], gp_buf[:], 65535.0, None, ALU.min))
        mb = mb_half[i2]
        tvm = E(lambda v, cbuf=cbuf, gp_buf=gp_buf, mb=mb:
                v.tensor_tensor(mb, cbuf, gp_buf[:], ALU.mult),
                deps=[b.F(("mB2", i2))])
        b.setF(("cB2", i2), tvm)
        b.setF(("gpB2", i2), tvm)
        b.v(lambda v, mb=mb, tmp_=tmp_: v.tensor_reduce(
            tmp_, mb, axis=AX.X, op=ALU.max, apply_absolute_value=True),
            deps=[tvm])
        b.v(lambda v, acc_=acc_, tmp_=tmp_: v.tensor_max(acc_, acc_, tmp_))
        tg = b.sp(d_m[oc * 128:(oc + 1) * 128, tt * 512:(tt + 1) * 512],
                  mb, deps=[tvm], slot=f"Sctx{i2}")
        b.setF(("mB2", i2), tg)
    tvmm = b.tok("VE")
    tok_p2a = b.tok("VE")
    b.barrier()

    t1 = finalize_partial_max(macc, 0, tok_p2a)
    tcc3 = allreduce_max(t1)
    b.v(lambda v: v.tensor_tensor(
        rowb[0:1, 16:17], rowb[0:1, 8:9], rowb[0:1, 20:21], ALU.mult),
        deps=[tcc3])
    b.v(lambda v: v.tensor_scalar(
        rowb[0:1, 16:17], rowb[0:1, 16:17], 1.0 / QA, 1e-12,
        ALU.mult, ALU.max))
    b.v(lambda v: v.reciprocal(rowb[0:1, 21:22], rowb[0:1, 16:17]))
    tv = b.v(lambda v: v.tensor_tensor(
        rowb[0:1, 17:18], rowb[0:1, 20:21], rowb[0:1, 21:22], ALU.mult))
    tok_r3 = bcast_scalars(16, 2, 7, tv)


    for idx in range(192):
        oc, tt = idx // 8, idx % 8
        i2 = idx % 2
        E = b.g if idx % 3 == 2 else b.v
        acc_ = mmacc
        tmp_ = tmp128
        m_buf = stage0 if i2 == 0 else stage1
        u_buf = tbufA[:, 0:512] if i2 == 0 else tbufA[:, 512:1024]
        t_m = b.ld(m_buf[:],
                   d_m[oc * 128:(oc + 1) * 128, tt * 512:(tt + 1) * 512],
                   deps=[b.F(("m2B", i2))], slot=f"Lstg{i2}")
        t_u = b.ld(u_buf,
                   d_up[oc * 128:(oc + 1) * 128, tt * 512:(tt + 1) * 512],
                   deps=[b.F(("u2B", i2))], slot=f"LtbA{i2}")
        c3buf = tbufB[:, 0:512] if i2 == 0 else tbufB[:, 512:1024]
        E(lambda v, m_buf=m_buf, c3buf=c3buf: v.tensor_scalar(
            c3buf, m_buf[:], SC[:, 8:9], MAGIC, ALU.mult, ALU.add),
            deps=[t_m, tok_r3, b.F(("c3B", i2))])
        b.setF(("m2B", i2), b.tok("GP") if idx % 3 == 2 else b.tok("VE"))
        E(lambda v, c3buf=c3buf: v.tensor_scalar(
            c3buf, c3buf, MAGIC, -(QA + 1.0), ALU.subtract, ALU.max))
        E(lambda v, c3buf=c3buf: v.tensor_scalar(
            c3buf, c3buf, QA, None, ALU.min))
        E(lambda v, u_buf=u_buf: v.tensor_scalar(
            u_buf, u_buf, SC[:, 3:4], MAGIC, ALU.mult, ALU.add),
            deps=[t_u])
        E(lambda v, u_buf=u_buf: v.tensor_scalar(
            u_buf, u_buf, MAGIC, -(QA + 1.0), ALU.subtract, ALU.max))
        E(lambda v, u_buf=u_buf: v.tensor_scalar(
            u_buf, u_buf, QA, None, ALU.min))
        mb = mb_half[i2]
        tvm = E(lambda v, c3buf=c3buf, u_buf=u_buf, mb=mb:
                v.tensor_tensor(mb, c3buf, u_buf, ALU.mult),
                deps=[b.F(("mm2B", i2))])
        b.setF(("c3B", i2), tvm)
        b.setF(("u2B", i2), tvm)
        b.v(lambda v, mb=mb, tmp_=tmp_: v.tensor_reduce(
            tmp_, mb, axis=AX.X, op=ALU.max, apply_absolute_value=True),
            deps=[tvm])
        b.v(lambda v, acc_=acc_, tmp_=tmp_: v.tensor_max(acc_, acc_, tmp_))
        tg = b.sp(d_mm[oc * 128:(oc + 1) * 128, tt * 512:(tt + 1) * 512],
                  mb, deps=[tvm], slot=f"Sctx{i2}")
        b.setF(("mm2B", i2), tg)

    tok_p2b = b.tok("VE")
    b.barrier()

    t1 = finalize_partial_max(mmacc, 0, tok_p2b)
    tcc4 = allreduce_max(t1)
    b.v(lambda v: v.tensor_tensor(
        rowb[0:1, 22:23], SC[0:1, 7:8], SC[0:1, 2:3], ALU.mult),
        deps=[tcc4])
    b.v(lambda v: v.tensor_tensor(
        rowb[0:1, 16:17], rowb[0:1, 8:9], rowb[0:1, 22:23], ALU.mult))
    b.v(lambda v: v.tensor_scalar(
        rowb[0:1, 16:17], rowb[0:1, 16:17], 1.0 / QA, 1e-12,
        ALU.mult, ALU.max))
    b.v(lambda v: v.reciprocal(rowb[0:1, 23:24], rowb[0:1, 16:17]))
    tv = b.v(lambda v: v.tensor_tensor(
        rowb[0:1, 17:18], rowb[0:1, 22:23], rowb[0:1, 23:24], ALU.mult))
    tok_r4 = bcast_scalars(16, 2, 9, tv)

    b.barrier()
    TT256 = 16
    for tt in range(TT256):
        t0 = tt * 256
        oq_toks = []
        for ic in range(24):
            i2 = ic % 2
            mmb = stage0 if i2 == 0 else stage1
            t_mm = b.ld(mmb[:, 0:256],
                        d_mm[ic * 128:(ic + 1) * 128, t0:t0 + 256],
                        deps=[b.F(("mmC", i2))], slot=f"Lstg{i2}")
            b.v(lambda v, mmb=mmb: v.tensor_scalar(
                mmb[:, 256:512], mmb[:, 0:256], SC[:, 10:11], MAGIC,
                ALU.mult, ALU.add),
                deps=[t_mm, tok_r4])
            b.v(lambda v, mmb=mmb: v.tensor_scalar(
                mmb[:, 256:512], mmb[:, 256:512], MAGIC, -(QA + 1.0),
                ALU.subtract, ALU.max))
            b.v(lambda v, mmb=mmb: v.tensor_scalar(
                mmb[:, 0:256], mmb[:, 256:512], QA, None, ALU.min),
                deps=[b.F("bigB_B")])
            b.v(lambda v, mmb=mmb, ic=ic: v.tensor_copy(
                bigB_bf[:, 512 * ic: 512 * ic + 256], mmb[:, 0:256]))
            oq_toks.append(b.v(lambda v, mmb=mmb, ic=ic: v.tensor_tensor(
                bigB_bf[:, 512 * ic + 256: 512 * ic + 512], mmb[:, 0:256],
                bigB_bf[:, 512 * ic: 512 * ic + 256], ALU.subtract)))
            b.setF(("mmC", i2), b.tok("VE"))
        t_oq = oq_toks[-1]
        x1_toks = []
        for oc2 in range(8):
            sl = oc2 % 2
            twA = b.ld(wsA_bf[:, 0:3072].rearrange("p (a m) -> p a m",
                                                    m=256),
                       d_wdownT[oc2, 0:12].rearrange("a p m -> p a m"),
                       deps=[b.F(("wsC", 0)), tok_wprep], slot="Lws0")
            twB = b.ld(wsB_bf[:, 0:3072].rearrange("p (a m) -> p a m",
                                                   m=256),
                       d_wdownT[oc2, 12:24].rearrange("a p m -> p a m"),
                       deps=[b.F(("wsC", 1)), tok_wprep], slot="Lws1")
            pp = ps0 if sl == 0 else ps1
            mm_tok = None
            for ic in range(24):
                wbf_ = wsA_bf if ic < 12 else wsB_bf
                icl = ic if ic < 12 else ic - 12
                wh = wbf_[:, icl * 256: icl * 256 + 128]
                wl = wbf_[:, icl * 256 + 128: icl * 256 + 256]
                rh = bigB_bf[:, 512 * ic: 512 * ic + 256]
                rl_ = bigB_bf[:, 512 * ic + 256: 512 * ic + 512]
                b.t(lambda t, pp=pp, wh=wh, rh=rh, ic=ic: t.matmul(
                    pp[:, 0:256], wh, rh, start=(ic == 0), stop=False),
                    deps=[twA if ic < 12 else twB, oq_toks[ic],
                          b.F(("psC", sl))], inc=False)
                b.t(lambda t, pp=pp, wh=wh, rl_=rl_: t.matmul(
                    pp[:, 0:256], wh, rl_, start=False, stop=False),
                    inc=False)
                mm_tok = b.t(lambda t, pp=pp, wl=wl, rh=rh, ic=ic: t.matmul(
                    pp[:, 0:256], wl, rh, start=False, stop=(ic == 23)),
                    inc=(ic == 23))
            b.setF(("wsC", 0), mm_tok)
            b.setF(("wsC", 1), mm_tok)
            gb = tbufA[:, 0:256] if sl == 0 else tbufA[:, 256:512]
            tsv = b.s(lambda s, pp=pp, gb=gb: s.activation(
                gb, pp[:, 0:256], ACT.Silu, scale=SC[:, 9:10]),
                deps=[mm_tok, b.F(("gC", sl))])
            b.setF(("psC", sl), tsv)
            xb = tbufB[:, 0:256] if sl == 0 else tbufB[:, 256:512]
            sb_ = tbufB[:, 512:768] if sl == 0 else tbufB[:, 768:1024]
            t_x = b.ld(xb, d_xfm[oc2 * 128:(oc2 + 1) * 128, t0:t0 + 256],
                       deps=[b.F(("xC", sl))], slot=f"LtbB{sl}")
            t_s = b.ld(sb_, d_satt[oc2 * 128:(oc2 + 1) * 128, t0:t0 + 256],
                       deps=[b.F(("sC", sl)), tok_a2], slot=f"LtbB{2 + sl}")
            sgb = tbufA[:, 512 + sl * 256: 512 + (sl + 1) * 256]
            tv = b.v(lambda v, sb_=sb_, gb=gb, sgb=sgb: v.tensor_tensor(
                sgb, sb_, gb, ALU.mult), deps=[tsv, t_s])
            b.setF(("gC", sl), tv)
            b.setF(("sC", sl), tv)
            tvx1 = b.v(lambda v, xb=xb, sgb=sgb, oc2=oc2: v.tensor_tensor(
                bigB[:, 6144 + oc2 * 256: 6144 + (oc2 + 1) * 256],
                xb, sgb, ALU.add),
                deps=[t_x, b.F(("x1sp", oc2))])
            b.setF(("xC", sl), tvx1)
            x1_toks.append(tvx1)
            tgx = b.sp(d_x1[oc2 * 128:(oc2 + 1) * 128, t0:t0 + 256],
                       bigB[:, 6144 + oc2 * 256: 6144 + (oc2 + 1) * 256],
                       deps=[tvx1], slot=f"Sx1{oc2 % 4}")
            b.setF(("x1sp", oc2), tgx)
        tok_x1 = x1_toks[-1]
        mm_tok = None
        for kc in range(DC):
            x1c = bigB[:, 6144 + kc * 256: 6144 + (kc + 1) * 256]
            mm_tok = b.t(lambda t, x1c=x1c, kc=kc: t.matmul(
                pst[0:1, 0:256], ones128, x1c,
                start=(kc == 0), stop=(kc == DC - 1)),
                deps=[tok_x1, b.F("pstB")], inc=(kc == DC - 1))
        for kc in range(DC):
            x1c = bigB[:, 6144 + kc * 256: 6144 + (kc + 1) * 256]
            i2 = kc % 2
            sqb = stage0 if i2 == 0 else stage1
            tsq = b.s(lambda s, x1c=x1c, sqb=sqb: s.activation(
                sqb[:, 0:256], x1c, ACT.Square),
                deps=[tok_x1, b.F(("sqB", i2))])
            mm_tok = b.t(lambda t, sqb=sqb, kc=kc: t.matmul(
                pst[0:1, 256:512], ones128, sqb[:, 0:256],
                start=(kc == 0), stop=(kc == DC - 1)),
                deps=[tsq], inc=True)
            b.setF(("sqB", i2), mm_tok)
        tv = b.v(lambda v: v.tensor_scalar(
            rb_m, pst[0:1, 0:256], 1.0 / D, None, ALU.mult), deps=[mm_tok])
        tvq = b.v(lambda v: v.tensor_scalar(
            rb_msq, pst[0:1, 256:512], 1.0 / D, None, ALU.mult))
        b.setF("pstB", tvq)
        b.v(lambda v: v.tensor_tensor(rb_m2, rb_m, rb_m, ALU.mult))
        tvv = b.v(lambda v: v.tensor_tensor(rb_var, rb_msq, rb_m2,
                                            ALU.subtract))
        tsv = b.s(lambda s: s.activation(rb_vstd, rb_var, ACT.Sqrt,
                                         bias=rowb[0:1, 32:33]),
                  deps=[tvv])
        tvr = b.v(lambda v: v.reciprocal(rb_rstd, rb_vstd), deps=[tsv])
        tbm = b.t(lambda t: t.matmul(psp[:, 0:256], ones_1, rb_m,
                                     start=True, stop=True),
                  deps=[tv, b.F("pspB")])
        tbr = b.t(lambda t: t.matmul(psp[:, 256:512], ones_1, rb_rstd,
                                     start=True, stop=True), deps=[tvr])
        xn1_toks = []
        for kc in range(DC):
            x1c = bigB[:, 6144 + kc * 256: 6144 + (kc + 1) * 256]
            i2 = kc % 2
            df = stage0 if i2 == 0 else stage1
            b.v(lambda v, x1c=x1c, df=df: v.tensor_tensor(
                df[:, 256:512], x1c, psp[:, 0:256], ALU.subtract),
                deps=[tbm, tbr, b.F(("dfB", i2))])
            b.v(lambda v, df=df: v.tensor_tensor(
                df[:, 0:256], df[:, 256:512], psp[:, 256:512], ALU.mult))
            b.v(lambda v, df=df, kc=kc, t0=t0: v.tensor_copy(
                xn1_hi(kc, t0, 256), df[:, 0:256]))
            tvx = b.v(lambda v, df=df, kc=kc, t0=t0: v.tensor_tensor(
                xn1_lo(kc, t0, 256), df[:, 0:256], xn1_hi(kc, t0, 256),
                ALU.subtract))
            b.setF(("dfB", i2), tvx)
            xn1_toks.append(tvx)
        b.setF("pspB", xn1_toks[-1])
        b.setF("bigB_B", [xn1_toks[-1], b.tok("SE"), b.tok("TE")]
               + b.all_dma_tok())
        b.setF(("mmC", 0), [b.tok("SE"), b.tok("VE")])
        b.setF(("mmC", 1), [b.tok("SE"), b.tok("VE")])
    tok_xn1 = b.tok("VE")
    tok_x1sp = b.all_dma_tok()

    if upto < 7:
        return finish(nc, b)
    b.barrier()
    for ec in range(DC):
        tw = []
        for wi, dT in enumerate((d_WgT, d_WvT, d_WdT)):
            dst = (wsA_bf[:, 0:2048], wsA_bf[:, 2048:4096],
                   wsB_bf[:, 0:2048])[wi]
            tw.append(b.ld(dst.rearrange("p (a m) -> p a m", m=256),
                           dT[ec].rearrange("a p m -> p a m"),
                           deps=[b.F("wsC2"), tok_wprep],
                           slot=("Lws0", "Lws0", "Lws1")[wi]))
        for tt in range(8):
            ppl = (ps0, ps1, ps2)
            mts = []
            for wi, pp in enumerate(ppl):
                mm_tok = None
                for kc in range(DC):
                    wsl = (wsA_bf[:, 0:2048], wsA_bf[:, 2048:4096],
                           wsB_bf[:, 0:2048])[wi]
                    wh = wsl[:, kc * 256: kc * 256 + 128]
                    wl = wsl[:, kc * 256 + 128: kc * 256 + 256]
                    rh = xn1_hi(kc, tt * 512, 512)
                    rl_ = xn1_lo(kc, tt * 512, 512)
                    b.t(lambda t, pp=pp, wh=wh, rh=rh, kc=kc: t.matmul(
                        pp[:], wh, rh, start=(kc == 0), stop=False),
                        deps=[tw[1] if wi < 2 else tw[2], tok_xn1,
                              b.F(("psC2", wi))], inc=False)
                    b.t(lambda t, pp=pp, wh=wh, rl_=rl_: t.matmul(
                        pp[:], wh, rl_, start=False, stop=False), inc=False)
                    mm_tok = b.t(lambda t, pp=pp, wl=wl, rh=rh, kc=kc:
                                 t.matmul(pp[:], wl, rh, start=False,
                                          stop=(kc == DC - 1)),
                                 inc=(kc == DC - 1))
                mts.append(mm_tok)
            ts1 = b.s(lambda s, ec=ec: s.activation(
                tbufA[:, 0:512], ps0[:], ACT.Sigmoid,
                bias=sb_bg[:, ec:ec + 1]),
                deps=[mts[0], tsl, b.F("tbAC")])
            ts2 = b.s(lambda s, ec=ec: s.activation(
                tbufA[:, 512:1024], ps1[:], ACT.Tanh,
                bias=sb_bv[:, ec:ec + 1]), deps=[mts[1]])
            b.setF(("psC2", 0), ts1)
            b.setF(("psC2", 1), ts2)
            tvxs = b.v(lambda v, tt=tt: v.tensor_tensor(
                bigB[:, tt * 512:(tt + 1) * 512],
                tbufA[:, 0:512], tbufA[:, 512:1024], ALU.mult),
                deps=[ts2, b.F("bigB_C")])
            ts3 = b.s(lambda s, ec=ec: s.activation(
                tbufB[:, 0:512], ps2[:], ACT.Sigmoid,
                bias=sb_bd[:, ec:ec + 1]), deps=[mts[2], b.F("tbBC")])
            b.setF(("psC2", 2), ts3)
            tva = b.v(lambda v, tt=tt: v.tensor_scalar(
                bigB[:, 4096 + tt * 512: 4096 + (tt + 1) * 512],
                tbufB[:, 0:512], 0.998, 0.001, ALU.mult, ALU.add),
                deps=[ts3])
            b.setF("tbAC", tvxs)
            b.setF("tbBC", tva)
        tvscan = b.v(lambda v: v.tensor_tensor_scan(
            bigB[:, 0:4096], bigB[:, 4096:8192], bigB[:, 0:4096],
            0.0, ALU.mult, ALU.add))
        tg = b.sp(d_s1[ec * 128:(ec + 1) * 128, :], bigB[:, 0:4096],
                  deps=[tvscan], slot="Ss1")
        b.setF("bigB_C", tg)
        b.setF("wsC2", b.tok("TE"))
    tok_c = b.all_dma_tok()

    if upto < 8:
        return finish(nc, b)
    b.barrier()
    for tt in range(TT256):
        t0 = tt * 256
        mm_tok = None
        for kc in range(DC):
            wh = rw_bf[:, kc * 8: kc * 8 + 8]
            wl = rw_bf[:, 64 + kc * 8: 64 + kc * 8 + 8]
            rh = xn1_hi(kc, t0, 256)
            rl_ = xn1_lo(kc, t0, 256)
            b.t(lambda t, wh=wh, rh=rh, kc=kc: t.matmul(
                pst[0:8, 0:256], wh, rh, start=(kc == 0), stop=False),
                deps=[tsl, tok_xn1, b.F("pstD")], inc=False)
            b.t(lambda t, wh=wh, rl_=rl_: t.matmul(
                pst[0:8, 0:256], wh, rl_, start=False, stop=False),
                inc=False)
            mm_tok = b.t(lambda t, wl=wl, rh=rh, kc=kc: t.matmul(
                pst[0:8, 0:256], wl, rh, start=False,
                stop=(kc == DC - 1)), inc=(kc == DC - 1))
        tv8 = b.v(lambda v: v.tensor_copy(stage0[0:8, 0:256],
                                          pst[0:8, 0:256]),
                  deps=[mm_tok, b.F("st0D")])
        b.setF("pstD", tv8)
        w_toks = []
        for sc in range(2):
            pr = probsA if sc == 0 else probsB

            def evR(dst, tt, pr=pr):
                return b.v(lambda v: v.tensor_copy(pr, dst), deps=[tt])
            tvp = do_transpose(stage0[0:8, sc * 128:(sc + 1) * 128], 128, 8,
                               ident[0:8, 0:8], evR, deps=[tv8])
            b.v(lambda v, pr=pr: v.reduce_max(rowm, pr, axis=AX.X))
            b.v(lambda v: v.tensor_scalar(nrowm, rowm, -1.0, None, ALU.mult))
            b.v(lambda v: v.engine_nop())
            tvn = b.v(lambda v: v.engine_nop())
            tse = b.s(lambda s, pr=pr: s.activation(
                r_eq, pr, ACT.Exp, bias=nrowm, accum_out=lsum), deps=[tvn])
            b.v(lambda v: v.reciprocal(rlsum, lsum), deps=[tse])
            b.v(lambda v: v.engine_nop())
            b.v(lambda v: v.engine_nop())
            b.v(lambda v: v.tensor_scalar(r_q, r_eq, rlsum, None, ALU.mult))
            b.v(lambda v: v.reduce_max(r_m1, r_q, axis=AX.X))
            b.v(lambda v: v.engine_nop())
            b.v(lambda v: v.engine_nop())
            b.v(lambda v: v.tensor_scalar(r_eq, r_q, r_m1, None,
                                          ALU.is_equal))
            b.v(lambda v: v.scalar_tensor_tensor(
                r_eq, r_eq, -1e30, r_q, ALU.mult, ALU.add))
            b.v(lambda v: v.reduce_max(r_m2, r_eq, axis=AX.X))
            b.v(lambda v: v.tensor_tensor(r_den, r_m1, r_m2, ALU.add))
            b.v(lambda v: v.engine_nop())
            b.v(lambda v: v.reciprocal(r_rden, r_den))
            b.v(lambda v: v.engine_nop())
            b.v(lambda v: v.engine_nop())
            b.v(lambda v: v.tensor_tensor(r_a1, r_m1, r_rden, ALU.mult))
            b.v(lambda v: v.tensor_tensor(r_a2, r_m2, r_rden, ALU.mult))
            w_toks.append(b.v(lambda v, sc=sc: v.tensor_tensor(
                w_col[:, sc:sc + 1], r_a1, r_a2, ALU.add)))
        b.setF("st0D", w_toks[-1])
        def evW(dst, tt):
            return b.v(lambda v: v.tensor_copy(stage1[0:2, 0:128], dst),
                       deps=[tt, b.F("st1D")])
        tvw = do_transpose(w_col, 2, 128, ident[:], evW, deps=[w_toks[-1]])
        tgw = b.sp(rb_wrow, stage1[0:2, 0:128], deps=[tvw], slot="Swrow")
        b.setF("st1D", tgw)
        tbw = b.t(lambda t: t.matmul(psp[:, 0:256], ones_1, rb_wrow,
                                     start=True, stop=True),
                  deps=[tgw, b.F("pspD")])
        for ocf in range(32):
            sl = ocf % 2
            wbf = wsA_bf if sl == 0 else wsB_bf
            t_wh = b.ld(wbf[:, 0:1024].rearrange("p (c m) -> p c m", m=128),
                        d_w1h[:, ocf * 128:(ocf + 1) * 128].rearrange(
                            "(c p) m -> p c m", p=128),
                        deps=[b.F(("wsD", sl))], slot=f"Lws{sl}")
            t_w = b.ld(wbf[:, 1024:2048].rearrange("p (c m) -> p c m", m=128),
                       d_w1l[:, ocf * 128:(ocf + 1) * 128].rearrange(
                           "(c p) m -> p c m", p=128),
                       deps=[b.F(("wsD", sl))], slot=f"Lws{sl}l")
            pp = ps0 if sl == 0 else ps1
            mm_tok = None
            for kc in range(DC):
                wh = wbf[:, kc * 128: kc * 128 + 128]
                wl = wbf[:, 1024 + kc * 128: 1024 + kc * 128 + 128]
                rh = xn1_hi(kc, t0, 256)
                rl_ = xn1_lo(kc, t0, 256)
                b.t(lambda t, pp=pp, wh=wh, rh=rh, kc=kc: t.matmul(
                    pp[:, 0:256], wh, rh, start=(kc == 0), stop=False),
                    deps=[t_wh, t_w, tok_xn1, b.F(("psD", sl))], inc=False)
                b.t(lambda t, pp=pp, wh=wh, rl_=rl_: t.matmul(
                    pp[:, 0:256], wh, rl_, start=False, stop=False),
                    inc=False)
                mm_tok = b.t(lambda t, pp=pp, wl=wl, rh=rh, kc=kc: t.matmul(
                    pp[:, 0:256], wl, rh, start=False, stop=(kc == DC - 1)),
                    inc=(kc == DC - 1))
            b.setF(("wsD", sl), mm_tok)
            gf = tbufA[:, 0:256] if sl == 0 else tbufA[:, 256:512]
            tsv = b.s(lambda s, pp=pp, ocf=ocf, gf=gf: s.activation(
                gf, pp[:, 0:256],
                ACT.Gelu_apprx_tanh, bias=sb_b1[:, ocf:ocf + 1]),
                deps=[mm_tok, tok_c, b.F("bigB_D"), b.F(("gD", sl))])
            b.setF(("psD", sl), tsv)
            b.v(lambda v, gf=gf, ocf=ocf: v.tensor_copy(
                bigB_bf[:, 512 * ocf: 512 * ocf + 256], gf), deps=[tsv])
            tvl = b.v(lambda v, gf=gf, ocf=ocf: v.tensor_tensor(
                bigB_bf[:, 512 * ocf + 256: 512 * ocf + 512], gf,
                bigB_bf[:, 512 * ocf: 512 * ocf + 256], ALU.subtract))
            b.setF(("gD", sl), tvl)
        tok_h1 = [b.tok("SE"), b.tok("VE")]
        fin_toks = []
        for oc2 in range(8):
            mm_tok = None
            for half in range(2):
                wbf = wsA_bf if half == 0 else wsB_bf
                t_wh2 = b.ld(
                    wbf[:, 0:2048].rearrange("p (c m) -> p c m", m=128),
                    d_w2h[half * 2048:(half + 1) * 2048,
                          oc2 * 128:(oc2 + 1) * 128].rearrange(
                        "(c p) m -> p c m", p=128),
                    deps=[b.F(("wsD", half)), b.F(("wsD2", half))],
                    slot=f"Lws{half}")
                t_w = b.ld(wbf[:, 2048:4096].rearrange(
                               "p (c m) -> p c m", m=128),
                           d_w2l[half * 2048:(half + 1) * 2048,
                                 oc2 * 128:(oc2 + 1) * 128].rearrange(
                               "(c p) m -> p c m", p=128),
                           deps=[b.F(("wsD", half)), b.F(("wsD2", half))],
                           slot=f"Lws{half}l")
                for icq in range(16):
                    ic = half * 16 + icq
                    wh = wbf[:, icq * 128: icq * 128 + 128]
                    wl = wbf[:, 2048 + icq * 128: 2048 + icq * 128 + 128]
                    rh = bigB_bf[:, 512 * ic: 512 * ic + 256]
                    rl_ = bigB_bf[:, 512 * ic + 256: 512 * ic + 512]
                    b.t(lambda t, wh=wh, rh=rh, ic=ic: t.matmul(
                        ps2[:, 0:256], wh, rh, start=(ic == 0), stop=False),
                        deps=[t_wh2, t_w, tok_h1, b.F("ps2D")], inc=False)
                    b.t(lambda t, wh=wh, rl_=rl_: t.matmul(
                        ps2[:, 0:256], wh, rl_, start=False, stop=False),
                        inc=False)
                    mm_tok = b.t(lambda t, wl=wl, rh=rh, ic=ic: t.matmul(
                        ps2[:, 0:256], wl, rh, start=False, stop=(ic == 31)),
                        inc=(icq == 15))
                b.setF(("wsD2", half), mm_tok)
                b.setF(("wsD", half), mm_tok)
            tse = b.s(lambda s, oc2=oc2: s.activation(
                tbufA[:, 0:256], ps2[:, 0:256], ACT.Identity,
                bias=sb_b2[:, oc2:oc2 + 1]), deps=[mm_tok, b.F("tbAD")])
            b.setF("ps2D", tse)
            tv = b.v(lambda v: v.tensor_tensor(
                tbufA[:, 256:512], tbufA[:, 0:256], psp[:, 0:256], ALU.mult),
                deps=[tse, tbw])
            tsg = b.s(lambda s: s.activation(
                tbufA[:, 512:768], tbufA[:, 256:512], ACT.Silu), deps=[tv])
            t_s1 = b.ld(tbufB[:, 0:256],
                        d_s1[oc2 * 128:(oc2 + 1) * 128, t0:t0 + 256],
                        deps=[b.F("tbBD"), tok_c], slot="LtbB0")
            t_x1 = b.ld(tbufB[:, 256:512],
                        d_x1[oc2 * 128:(oc2 + 1) * 128, t0:t0 + 256],
                        deps=[tok_x1sp], slot="LtbB1")
            tv = b.v(lambda v: v.tensor_tensor(
                tbufA[:, 768:1024], tbufB[:, 0:256], tbufA[:, 512:768],
                ALU.mult), deps=[tsg, t_s1])
            tvf = b.v(lambda v: v.tensor_tensor(
                tbufB[:, 512:768], tbufB[:, 256:512], tbufA[:, 768:1024],
                ALU.add), deps=[t_x1, b.F("finD")])
            b.setF("tbAD", tvf)
            for tc in range(2):
                ot = qa if tc == 0 else ctxtm

                def evF(dst, tt, oc2=oc2, ot=ot, tc=tc):
                    return b.v(lambda v: v.tensor_copy(
                        ot[:, oc2 * 128:(oc2 + 1) * 128], dst),
                        deps=[tt, b.F(("otD", tc))])
                tvo = do_transpose(
                    tbufB[:, 512 + tc * 128: 512 + (tc + 1) * 128], 128, 128,
                    ident[:], evF, deps=[tvf])
                fin_toks.append(tvo)
            b.setF("finD", b.tok("TE"))
            b.setF("tbBD", b.tok("VE"))
        b.setF("bigB_D", b.tok("TE"))
        tg0 = b.sp(d_out[t0:t0 + 128, :], qa[:], deps=[fin_toks[-1]],
                   slot="Sout0")
        tg1 = b.sp(d_out[t0 + 128:t0 + 256, :], ctxtm[:],
                   deps=[fin_toks[-1]], slot="Sout1")
        b.setF(("otD", 0), tg0)
        b.setF(("otD", 1), tg1)
        b.setF("pspD", b.tok("VE"))

    return finish(nc, b)


def finish(nc, b):
    for s in list(b.cnt):
        if s in ("TE", "VE", "SE", "CC", "GP", "LD"):
            continue
        if b.cnt[s] > 0:
            b.emit("gpsimd",
                   lambda e, h=b.sems[s], v=b.cnt[s]: e.wait_ge(h, v))
    b.emit("gpsimd", lambda e, h=b.sems["GP"], v=b.cnt["GP"]: e.wait_ge(h, v))

    with nc.Block() as block:
        @block.sync
        def _(eng):
            for f in b.prog["sync"]:
                f(eng)

        @block.tensor
        def _(eng):
            for f in b.prog["tensor"]:
                f(eng)

        @block.vector
        def _(eng):
            for f in b.prog["vector"]:
                f(eng)

        @block.scalar
        def _(eng):
            for f in b.prog["scalar"]:
                f(eng)

        @block.gpsimd
        def _(eng):
            for f in b.prog["gpsimd"]:
                f(eng)

    return nc


_NC_CACHE = {}


def kernel(**inputs):
    if "main" not in _NC_CACHE:
        _NC_CACHE["main"] = build_nc(dbg=False)
    nc = _NC_CACHE["main"]
    f32 = np.float32

    def r2(a, cols):
        return np.ascontiguousarray(
            np.asarray(a, dtype=f32).reshape(cols, 128).T)

    base = {
        "wdec": np.ascontiguousarray(inputs["L0_attn_w_dec_attn"], dtype=f32),
        "wout": np.ascontiguousarray(inputs["L0_attn_w_out"], dtype=f32),
        "bout": r2(inputs["L0_attn_b_out"], 8),
        "wgate": np.ascontiguousarray(inputs["L0_ffn_w_gate"], dtype=f32),
        "wup": np.ascontiguousarray(inputs["L0_ffn_w_up"], dtype=f32),
        "wdown": np.ascontiguousarray(inputs["L0_ffn_w_down"], dtype=f32),
        "Wg": np.ascontiguousarray(inputs["L1_attn_Wg"], dtype=f32),
        "Wv": np.ascontiguousarray(inputs["L1_attn_Wv"], dtype=f32),
        "Wd": np.ascontiguousarray(inputs["L1_attn_Wd"], dtype=f32),
        "bg": r2(inputs["L1_attn_bg"], 8),
        "bv": r2(inputs["L1_attn_bv"], 8),
        "bd": r2(inputs["L1_attn_bd"], 8),
        "rw": np.ascontiguousarray(inputs["L1_ffn_router_w"], dtype=f32),
        "w1": np.ascontiguousarray(inputs["L1_ffn_w1"], dtype=f32),
        "b1": r2(inputs["L1_ffn_b1"], 32),
        "w2": np.ascontiguousarray(inputs["L1_ffn_w2"], dtype=f32),
        "b2": r2(inputs["L1_ffn_b2"], 8),
    }
    x_full = np.asarray(inputs["L0_attn_hidden_states"], dtype=f32)
    in_maps = []
    for bb in range(8):
        m = dict(base)
        m["x"] = np.ascontiguousarray(x_full[bb])
        in_maps.append(m)
    res = run_bass_kernel_spmd(nc, in_maps, core_ids=list(range(8)))
    return np.stack([np.asarray(res.results[bb]["out"]) for bb in range(8)],
                    axis=0)


# version 08
# speedup vs baseline: 1.1624x; 1.1624x over previous
import numpy as np
import concourse.bass as bass
import concourse.mybir as mybir
from concourse.bass_utils import run_bass_kernel_spmd

F32 = mybir.dt.float32
BF16 = mybir.dt.bfloat16
I32 = mybir.dt.int32
ALU = mybir.AluOpType
ACT = mybir.ActivationFunctionType
AX = mybir.AxisListType

D = 1024
S = 4096
DC = 8
NF = 3072
F1 = 4096
NEG = -1.0e30
EPS = 1e-5
QA = 32767.0
QW = 7.0
MAGIC = 12582912.0

ENGINES = ("sync", "tensor", "vector", "scalar", "gpsimd")


class Builder:
    def __init__(self, nc):
        self.nc = nc
        self.prog = {e: [] for e in ENGINES}
        self.sems = {}
        self.cnt = {}
        for k in ("LD", "GP", "TE", "VE", "SE", "CC"):
            self.sems[k] = nc.alloc_semaphore(name="sem" + k)
            self.cnt[k] = 0
        self.order_owner = {"TE": "tensor", "VE": "vector", "SE": "scalar"}
        self.lastw = {e: {} for e in ENGINES}
        self.free = {}
        self.bar = []

    def barrier(self):
        self.bar = [(s, self.cnt[s]) for s in self.cnt]
        self.free = {}

    def F(self, key):
        return self.free.get(key, self.bar)

    def setF(self, key, tok):
        self.free[key] = tok

    def emit(self, eng, fn):
        self.prog[eng].append(fn)

    def tok(self, s):
        return (s, self.cnt[s])

    def all_dma_tok(self):
        return [(s, self.cnt[s]) for s in self.cnt
                if s not in ("TE", "VE", "SE", "CC", "LD")]

    def bump(self, s, n=1):
        self.cnt[s] += n
        return (s, self.cnt[s])

    def _wait1(self, eng, tok):
        s, v = tok
        if v <= 0:
            return
        if self.order_owner.get(s) == eng:
            return
        if self.lastw[eng].get(s, 0) >= v:
            return
        self.lastw[eng][s] = v
        h = self.sems[s]
        self.emit(eng, lambda e, h=h, v=v: e.wait_ge(h, v))

    def waits(self, eng, toks):
        stack = [toks]
        while stack:
            t = stack.pop()
            if t is None:
                continue
            if isinstance(t, list):
                stack.extend(t)
            else:
                self._wait1(eng, t)

    def op(self, eng, sem, fn, deps=(), inc=True):
        self.waits(eng, list(deps))
        if inc or eng in ("vector", "scalar", "gpsimd"):
            t = self.bump(sem)
            h = self.sems[sem]
            self.emit(eng, lambda e, fn=fn, h=h: fn(e).then_inc(h, 1))
            if eng in ("vector", "scalar", "gpsimd"):
                n = t[1]
                self.lastw[eng][sem] = n
                self.emit(eng, lambda e, h=h, n=n: e.wait_ge(h, n))
            return t
        self.emit(eng, lambda e, fn=fn: fn(e))
        return self.tok(sem)

    def t(self, fn, deps=(), inc=True):
        return self.op("tensor", "TE", fn, deps, inc)

    def v(self, fn, deps=(), inc=True):
        return self.op("vector", "VE", fn, deps, inc)

    def s(self, fn, deps=(), inc=True):
        return self.op("scalar", "SE", fn, deps, inc)

    def _dma(self, eng, slot, out_ap, in_ap, deps, group):
        if slot not in self.sems:
            self.sems[slot] = self.nc.alloc_semaphore(name=slot)
            self.cnt[slot] = 0
        dd = list(deps)
        if not group:
            dd.append(self.tok(slot))
        self.waits(eng, dd)
        t = self.bump(slot, 16)
        h = self.sems[slot]
        self.emit(eng, lambda e, o=out_ap, i=in_ap, h=h:
                  e.dma_start(out=o, in_=i).then_inc(h, 16))
        return t

    def ld(self, out_ap, in_ap, deps=(), slot="Lgen", group=False):
        return self._dma("sync", slot, out_ap, in_ap, deps, group)

    def sp(self, out_ap, in_ap, deps=(), slot="Sgen", group=False):
        return self._dma("gpsimd", slot, out_ap, in_ap, deps, group)

    def g(self, fn, deps=(), inc=True):
        return self.op("gpsimd", "GP", fn, deps, inc)


def build_nc(dbg=False, upto=99):
    nc = bass.Bass("TRN2", target_bir_lowering=False)
    b = Builder(nc)
    scratch_kind = "ExternalOutput" if dbg else "Internal"

    d_x = nc.dram_tensor("x", [S, D], F32, kind="ExternalInput").ap()
    d_wdec = nc.dram_tensor("wdec", [D, D], F32, kind="ExternalInput").ap()
    d_wout = nc.dram_tensor("wout", [D, 2 * D], F32, kind="ExternalInput").ap()
    d_bout = nc.dram_tensor("bout", [128, 8], F32, kind="ExternalInput").ap()
    d_wgate = nc.dram_tensor("wgate", [NF, D], F32, kind="ExternalInput").ap()
    d_wup = nc.dram_tensor("wup", [NF, D], F32, kind="ExternalInput").ap()
    d_wdown = nc.dram_tensor("wdown", [D, NF], F32, kind="ExternalInput").ap()
    d_Wg = nc.dram_tensor("Wg", [D, D], F32, kind="ExternalInput").ap()
    d_Wv = nc.dram_tensor("Wv", [D, D], F32, kind="ExternalInput").ap()
    d_Wd = nc.dram_tensor("Wd", [D, D], F32, kind="ExternalInput").ap()
    d_bg = nc.dram_tensor("bg", [128, 8], F32, kind="ExternalInput").ap()
    d_bv = nc.dram_tensor("bv", [128, 8], F32, kind="ExternalInput").ap()
    d_bd = nc.dram_tensor("bd", [128, 8], F32, kind="ExternalInput").ap()
    d_rw = nc.dram_tensor("rw", [D, 8], F32, kind="ExternalInput").ap()
    d_w1 = nc.dram_tensor("w1", [D, F1], F32, kind="ExternalInput").ap()
    d_b1 = nc.dram_tensor("b1", [128, 32], F32, kind="ExternalInput").ap()
    d_w2 = nc.dram_tensor("w2", [F1, D], F32, kind="ExternalInput").ap()
    d_b2 = nc.dram_tensor("b2", [128, 8], F32, kind="ExternalInput").ap()
    d_out = nc.dram_tensor("out", [S, D], F32, kind="ExternalOutput").ap()

    def scr(name, shape, dt=F32):
        return nc.dram_tensor(name, shape, dt, kind=scratch_kind).ap()

    d_xntm = scr("s_xntm", [S, D])
    d_xfm = scr("s_xfm", [D, S])
    d_q = scr("s_q", [DC, 128, S])
    d_ctx = scr("s_ctx", [D, S])
    d_satt = scr("s_satt", [D, S])
    d_up = scr("s_up", [NF, S])
    d_gp = scr("s_gp", [NF, S])
    d_m = scr("s_m", [NF, S])
    d_mm = scr("s_mm", [NF, S])
    d_x1 = scr("s_x1", [D, S])
    d_s1 = scr("s_s1", [D, S])
    d_wupT = scr("s_wupT", [24, 8, 128, 256], BF16)
    d_wgateT = scr("s_wgateT", [24, 8, 128, 256], BF16)
    d_wdownT = scr("s_wdownT", [8, 24, 128, 256], BF16)
    d_woutT = scr("s_woutT", [8, 16, 128, 128])
    d_WgT = scr("s_WgT", [8, 8, 128, 256], BF16)
    d_WvT = scr("s_WvT", [8, 8, 128, 256], BF16)
    d_WdT = scr("s_WdT", [8, 8, 128, 256], BF16)
    d_w1h = scr("s_w1h", [D, F1], BF16)
    d_w1l = scr("s_w1l", [D, F1], BF16)
    d_w2h = scr("s_w2h", [F1, D], BF16)
    d_w2l = scr("s_w2l", [F1, D], BF16)
    d_cc_in = [nc.dram_tensor(f"cc_in{i}", [1, 8], F32) for i in range(4)]
    d_cc_out = [nc.dram_tensor(f"cc_out{i}", [1, 8], F32) for i in range(4)]

    sb = lambda n, shp, dt=F32: nc.alloc_sbuf_tensor(n, shp, dt)
    bigA = sb("bigA", [128, DC * S])
    bigB = sb("bigB", [128, 8192])
    tbufA = sb("tbufA", [128, 1024])
    tbufB = sb("tbufB", [128, 1024])
    qa = sb("qa", [128, 1024])
    ctxtm = sb("ctxtm", [128, 1024])
    stage0 = sb("stage0", [128, 512])
    stage1 = sb("stage1", [128, 512])
    wsA = sb("wsA", [128, 2048])
    wsB = sb("wsB", [128, 2048])
    maskT = sb("maskT", [128, 128])
    ident = sb("ident", [128, 128])
    ptA = sb("ptA", [128, 128])
    ptB = sb("ptB", [128, 128])
    rw_bf = sb("rw_bf", [128, 128], BF16)
    sm = sb("sm", [128, 448])
    rowb = sb("rowb", [1, 1536])

    SC = sm[:, 0:16]
    s0acc = sm[:, 16:17]; upacc = sm[:, 17:18]; gacc = sm[:, 18:19]
    macc = sm[:, 19:20]; mmacc = sm[:, 20:21]
    c_sum = sm[:, 21:22]; c_sq = sm[:, 22:23]; c_mean = sm[:, 23:24]
    c_msq = sm[:, 24:25]; c_var = sm[:, 25:26]; c_vstd = sm[:, 26:27]
    c_rstd = sm[:, 27:28]
    rowm = sm[:, 28:29]; nrowm = sm[:, 29:30]; lsum = sm[:, 30:31]
    rlsum = sm[:, 31:32]
    tmp128 = sm[:, 32:33]
    ones128 = sm[:, 34:35]
    eps128 = sm[:, 35:36]
    sb_bout = sm[:, 40:48]; sb_bg = sm[:, 48:56]; sb_bv = sm[:, 56:64]
    sb_bd = sm[:, 64:72]; sb_b2 = sm[:, 72:80]; sb_b1 = sm[:, 80:112]
    sb_rw = sm[:, 112:176]
    probsA = sm[:, 176:184]; probsB = sm[:, 184:192]
    r_m1 = sm[:, 192:193]; r_m2 = sm[:, 193:194]; r_eq = sm[:, 196:204]
    r_q = sm[:, 204:212]; r_den = sm[:, 212:213]; r_rden = sm[:, 213:214]
    r_a1 = sm[:, 214:215]; r_a2 = sm[:, 215:216]
    w_col = sm[:, 240:242]
    dq_sblk = sm[:, 256:352]
    dq_inv = sm[:, 352:448]

    rb_m = rowb[0:1, 128:384]
    rb_msq = rowb[0:1, 384:640]
    rb_m2 = rowb[0:1, 640:896]
    rb_var = rowb[0:1, 640:896]
    rb_vstd = rowb[0:1, 384:640]
    rb_rstd = rowb[0:1, 896:1152]
    rb_wrow = rowb[0:1, 1152:1408]
    ones_1 = rowb[0:1, 1408:1536]

    ps0 = nc.alloc_psum_tensor("ps0", [128, 512], F32)
    ps1 = nc.alloc_psum_tensor("ps1", [128, 512], F32)
    ps2 = nc.alloc_psum_tensor("ps2", [128, 512], F32)
    pc0 = nc.alloc_psum_tensor("pc0", [128, 512], F32)
    pc1 = nc.alloc_psum_tensor("pc1", [128, 512], F32)
    ptp = nc.alloc_psum_tensor("ptp", [128, 512], F32)
    ptp2 = nc.alloc_psum_tensor("ptp2", [128, 512], F32)
    pst = nc.alloc_psum_tensor("pst", [128, 512], F32)
    psp = pc0

    trk = {"n": 0}

    def do_transpose(src_ap, dst_p, dst_f, ident_ap, evac_cb, deps=()):
        k = trk["n"]
        trk["n"] += 1
        bank = ptp if k % 2 == 0 else ptp2
        dst = bank[0:dst_p, 0:dst_f]
        tt = b.t(lambda t, dst=dst, src_ap=src_ap, ident_ap=ident_ap:
                 t.transpose(dst, src_ap, ident_ap),
                 deps=list(deps) + [b.F(("trs", k % 2))])
        ev = evac_cb(dst, tt)
        b.setF(("trs", k % 2), ev)
        return ev

    bigA_bf = bigA[:].bitcast(BF16)
    bigB_bf = bigB[:].bitcast(BF16)
    wsA_bf = wsA[:].bitcast(BF16)
    wsB_bf = wsB[:].bitcast(BF16)

    def xn1_hi(kc, t0, w):
        return bigA_bf[:, 2 * kc * S + t0: 2 * kc * S + t0 + w]

    def xn1_lo(kc, t0, w):
        return bigA_bf[:, 2 * kc * S + S + t0: 2 * kc * S + S + t0 + w]

    for dst, dsrc in ((sb_bout, d_bout), (sb_bg, d_bg), (sb_bv, d_bv),
                      (sb_bd, d_bd), (sb_b1, d_b1), (sb_b2, d_b2)):
        b.ld(dst, dsrc, slot="Lsmall", group=True)
    tsl = b.ld(sb_rw.rearrange("p (c m) -> p c m", m=8),
               d_rw.rearrange("(c p) m -> p c m", p=128),
               slot="Lsmall", group=True)

    tld0 = b.tok("LD")
    b.v(lambda v: v.memset(ones128, 1.0))
    b.v(lambda v: v.memset(eps128, EPS))
    b.v(lambda v: v.memset(rowb[0:1, 32:33], EPS))
    b.v(lambda v: v.memset(ones_1, 1.0))
    b.v(lambda v: v.memset(rowb[0:1, 0:32], 0.0))
    b.v(lambda v: v.memset(stage0[:, 0:128], 0.0))
    b.v(lambda v: v.memset(stage1[:, 0:128], 1.0))
    b.v(lambda v: v.memset(s0acc, 0.0))
    b.v(lambda v: v.memset(upacc, 0.0))
    b.v(lambda v: v.memset(gacc, 0.0))
    b.v(lambda v: v.memset(macc, 0.0))
    tv0 = b.v(lambda v: v.memset(mmacc, 0.0))
    b.g(lambda g: g.affine_select(
        maskT[:], stage0[:, 0:128], [[-1, 128]], ALU.is_gt, NEG,
        base=0, channel_multiplier=1), deps=[tv0])
    tg_id = b.g(lambda g: g.affine_select(
        ident[:], stage1[:, 0:128], [[-1, 128]], ALU.is_equal, 0.0,
        base=0, channel_multiplier=1))
    tg_const = b.g(lambda g: g.engine_nop())

    if upto < 0.3:
        return finish(nc, b)

    b.barrier()

    def dequant_chunk(w_sb, n_in, out_sb, scr_sb, dep_ld, extra_out_dep):
        nblk = n_in // 32
        b.v(lambda v: v.tensor_reduce(
            dq_sblk[:, 0:nblk],
            w_sb.rearrange("p (a bk) -> p a bk", bk=32),
            axis=AX.X, op=ALU.max, apply_absolute_value=True),
            deps=[dep_ld])
        b.v(lambda v: v.tensor_scalar(
            dq_sblk[:, 0:nblk], dq_sblk[:, 0:nblk], 1.0 / QW, 1e-12,
            ALU.mult, ALU.max))
        b.v(lambda v: v.reciprocal(dq_inv[:, 0:nblk], dq_sblk[:, 0:nblk]))
        last = None
        for h in range(n_in // 512):
            sl = slice(h * 512, (h + 1) * 512)
            bsl = slice(h * 16, h * 16 + 16)
            b.v(lambda v, sl=sl, bsl=bsl: v.tensor_tensor(
                scr_sb,
                w_sb[:, sl].rearrange("p (a bk) -> p a bk", bk=32),
                dq_inv[:, bsl].unsqueeze(2).broadcast_to([128, 16, 32]),
                ALU.mult))
            b.v(lambda v: v.tensor_scalar(
                scr_sb, scr_sb, MAGIC, MAGIC, ALU.add, ALU.subtract))
            b.v(lambda v: v.tensor_scalar(
                scr_sb, scr_sb, -(QW + 1.0), QW, ALU.max, ALU.min))
            last = b.v(lambda v, sl=sl, bsl=bsl: v.tensor_tensor(
                out_sb[:, sl],
                scr_sb.rearrange("p (a bk) -> p a bk", bk=32),
                dq_sblk[:, bsl].unsqueeze(2).broadcast_to([128, 16, 32]),
                ALU.mult), deps=[extra_out_dep])
        return last

    def transpose_chunk_to_dram(src_sb, n_in, dram_slot_fn, dep, bf=False):
        for kc in range(n_in // 128):
            slot = kc % 2
            stg = stage1 if slot else stage0

            def ev(dst, tt, kc=kc, slot=slot, stg=stg):
                if not bf:
                    tv = b.v(lambda v: v.tensor_copy(stg[:, 0:128], dst),
                             deps=[tt, b.F(("stP", slot))])
                    tg = b.sp(dram_slot_fn(kc), stg[:, 0:128], deps=[tv],
                              slot=f"Sstg{slot}")
                    b.setF(("stP", slot), tg)
                    return tv
                sbf = stg[:].bitcast(BF16)
                tv = b.v(lambda v: v.tensor_copy(stg[:, 128:256], dst),
                         deps=[tt, b.F(("stP", slot))])
                b.v(lambda v: v.tensor_copy(sbf[:, 0:128], stg[:, 128:256]))
                tv = b.v(lambda v: v.tensor_tensor(
                    sbf[:, 128:256], stg[:, 128:256], sbf[:, 0:128],
                    ALU.subtract))
                tg = b.sp(dram_slot_fn(kc), sbf[:, 0:256], deps=[tv],
                          slot=f"Sstg{slot}")
                b.setF(("stP", slot), tg)
                return tv
            do_transpose(src_sb[:, kc * 128:(kc + 1) * 128], 128, 128,
                         ident[:], ev, deps=[dep, tg_id])
        return b.tok("TE")

    def gen_wsA_items():
        for w_d, dT in ((d_wup, d_wupT), (d_wgate, d_wgateT)):
            for oc in range(24):
                def item(w_d=w_d, dT=dT, oc=oc):
                    t_w = b.ld(wsA[:, 0:1024],
                               w_d[oc * 128:(oc + 1) * 128, :],
                               deps=[b.F("wsA_P")], slot="Lws0")
                    tdq = dequant_chunk(wsA[:, 0:1024], 1024,
                                        bigB[:, 7168:8192],
                                        bigB[:, 5632:6144], t_w,
                                        b.F("dqo_P"))
                    b.setF("wsA_P", tdq)
                    tte = transpose_chunk_to_dram(
                        bigB[:, 7168:8192], 1024,
                        lambda kc, dT=dT, oc=oc: dT[oc, kc], tdq, bf=True)
                    b.setF("dqo_P", tte)
                yield item
        for r in range(8):
            def item(r=r):
                t_w = b.ld(wsA[:, 0:2048], d_wout[r * 128:(r + 1) * 128, :],
                           deps=[b.F("wsA_P")], slot="Lws0")
                tte = transpose_chunk_to_dram(
                    wsA[:, 0:2048], 2048,
                    lambda kc, r=r: d_woutT[r, kc], t_w)
                b.setF("wsA_P", tte)
            yield item
        for w_d, dT in ((d_Wg, d_WgT), (d_Wv, d_WvT), (d_Wd, d_WdT)):
            for r in range(8):
                def item(w_d=w_d, dT=dT, r=r):
                    t_w = b.ld(wsA[:, 0:1024],
                               w_d[r * 128:(r + 1) * 128, :],
                               deps=[b.F("wsA_P")], slot="Lws0")
                    tte = transpose_chunk_to_dram(
                        wsA[:, 0:1024], 1024,
                        lambda dc, dT=dT, r=r: dT[r, dc], t_w, bf=True)
                    b.setF("wsA_P", tte)
                yield item

    def gen_bigB_items():
        for r in range(8):
            def item(r=r):
                t_w = b.ld(bigB[:, 0:3072], d_wdown[r * 128:(r + 1) * 128, :],
                           deps=[b.F("bigB_P")], slot="LbigB")
                tdq = dequant_chunk(bigB[:, 0:3072], 3072,
                                    bigB[:, 3072:6144],
                                    bigB[:, 7168:7680], t_w, b.F("bigB_P2"))
                b.setF("bigB_P", tdq)
                tte = transpose_chunk_to_dram(
                    bigB[:, 3072:6144], 3072,
                    lambda ic, r=r: d_wdownT[r, ic], tdq, bf=True)
                b.setF("bigB_P2", tte)
            yield item
        for name, d_src, d_h, d_l, rows, cols in (
                ("w1", d_w1, d_w1h, d_w1l, D, F1),
                ("w2", d_w2, d_w2h, d_w2l, F1, D)):
            for r in range(rows // 128):
                def item(d_src=d_src, d_h=d_h, d_l=d_l, r=r, cols=cols):
                    t_w = b.ld(bigB[:, 0:cols],
                               d_src[r * 128:(r + 1) * 128, :],
                               deps=[b.F("bigB_P"), b.F("bigB_P2")],
                               slot="LbigB")
                    b.v(lambda v: v.tensor_copy(
                        bigB_bf[:, 8192:8192 + cols], bigB[:, 0:cols]),
                        deps=[t_w])
                    tvl = b.v(lambda v: v.tensor_tensor(
                        bigB_bf[:, 8192 + cols:8192 + 2 * cols],
                        bigB[:, 0:cols],
                        bigB_bf[:, 8192:8192 + cols], ALU.subtract))
                    th = b.sp(d_h[r * 128:(r + 1) * 128, :],
                              bigB_bf[:, 8192:8192 + cols], deps=[tvl],
                              slot=f"Scv{r % 2}")
                    tl = b.sp(d_l[r * 128:(r + 1) * 128, :],
                              bigB_bf[:, 8192 + cols:8192 + 2 * cols],
                              deps=[tvl], slot=f"Scv{2 + r % 2}")
                    b.setF("bigB_P", [th, tl])
                    b.setF("bigB_P2", [th, tl])
                yield item

    gA = gen_wsA_items() if upto >= 2 else iter(())
    gB = gen_bigB_items() if upto >= 2 else iter(())

    TI = S // 128
    if upto < 0.7:
        TI = 2
    xtm = [tbufA, tbufB]
    xnt = [qa, ctxtm]
    for ti in range(TI):
        bi = ti % 2
        t_x = b.ld(xtm[bi][:], d_x[ti * 128:(ti + 1) * 128, :],
                   deps=[b.F(("xtm", bi))], slot=f"Lxtm{bi}")
        tvs = b.v(lambda v, x=xtm[bi]: v.reduce_sum(c_sum, x[:], axis=AX.X),
                  deps=[t_x])
        ts = b.s(lambda s, x=xtm[bi]: s.activation(
            bigB[:, 6144:7168], x[:], ACT.Square, accum_out=c_sq),
            deps=[t_x, b.F("sq_ph0")])
        b.setF("sq_ph0", ts)
        b.v(lambda v: v.tensor_scalar(c_mean, c_sum, 1.0 / D, None, ALU.mult))
        b.v(lambda v: v.tensor_scalar(c_msq, c_sq, 1.0 / D, None, ALU.mult),
            deps=[ts])
        b.v(lambda v: v.tensor_tensor(tmp128, c_mean, c_mean, ALU.mult))
        tvv = b.v(lambda v: v.tensor_tensor(c_var, c_msq, tmp128,
                                            ALU.subtract))
        tsv = b.s(lambda s: s.activation(c_vstd, c_var, ACT.Sqrt, bias=eps128),
                  deps=[tvv])
        b.v(lambda v: v.reciprocal(c_rstd, c_vstd), deps=[tsv])
        b.v(lambda v: v.engine_nop())
        b.v(lambda v: v.engine_nop())
        tv_xn = b.v(lambda v, x=xtm[bi], o=xnt[bi]: v.tensor_scalar(
            o[:], x[:], c_mean, c_rstd, ALU.subtract, ALU.mult),
            deps=[b.F(("xnt", bi))])
        b.v(lambda v, o=xnt[bi]: v.tensor_reduce(
            tmp128, o[:], axis=AX.X, op=ALU.max, apply_absolute_value=True))
        b.v(lambda v: v.tensor_max(s0acc, s0acc, tmp128))
        tg1 = b.sp(d_xntm[ti * 128:(ti + 1) * 128, :], xnt[bi][:],
                   deps=[tv_xn], slot=f"Sxnt{bi}")
        for dc in range(DC):
            def ev1(dst, tt, dc=dc, ti=ti):
                return b.v(lambda v: v.tensor_copy(
                    bigA[:, dc * S + ti * 128: dc * S + ti * 128 + 128], dst),
                    deps=[tt])
            do_transpose(xnt[bi][:, dc * 128:(dc + 1) * 128], 128, 128,
                         ident[:], ev1, deps=[tv_xn, tg_id])
        for dc in range(DC):
            def ev2(dst, tt, dc=dc, ti=ti):
                tv2 = b.v(lambda v: v.tensor_copy(
                    wsB[:, dc * 128:(dc + 1) * 128], dst),
                    deps=[tt, b.F(("wsB_ph0", dc))])
                tg2 = b.sp(d_xfm[dc * 128:(dc + 1) * 128,
                                 ti * 128:(ti + 1) * 128],
                           wsB[:, dc * 128:(dc + 1) * 128], deps=[tv2],
                           slot=f"Sxfm{dc % 4}")
                b.setF(("wsB_ph0", dc), tg2)
                return tv2
            do_transpose(xtm[bi][:, dc * 128:(dc + 1) * 128], 128, 128,
                         ident[:], ev2, deps=[t_x, tg_id])
        b.setF(("xtm", bi), [b.tok("VE"), b.tok("SE"), b.tok("TE")])
        b.setF(("xnt", bi), [b.tok("VE"), b.tok("TE"), tg1])
        for _ in range(3):
            it = next(gA, None)
            if it is not None:
                it()
    for it in gA:
        it()
    tok_xn_done = b.tok("VE")
    tok_xn_te = b.tok("TE")
    tok_xn_sp = b.all_dma_tok()
    if upto < 0.9:
        return finish(nc, b)

    for ec in range(DC):
        bi = ec % 2
        ws = wsA if bi == 0 else wsB
        t_w = b.ld(ws[:, 0:1024].rearrange("p (c m) -> p c m", m=128),
                   d_wdec[:, ec * 128:(ec + 1) * 128].rearrange(
                       "(c p) m -> p c m", p=128),
                   deps=[b.F(("qw", bi)), tok_xn_sp, tok_xn_te, b.F("wsA_P")],
                   slot=f"Lws{bi}")
        for tt in range(8):
            pp = ps0 if tt % 2 == 0 else ps1
            mm_tok = None
            for kc in range(DC):
                mm_tok = b.t(lambda t, ws=ws, kc=kc, tt=tt, pp=pp: t.matmul(
                    pp[:], ws[:, kc * 128:(kc + 1) * 128],
                    bigA[:, kc * S + tt * 512: kc * S + tt * 512 + 512],
                    start=(kc == 0), stop=(kc == DC - 1)),
                    deps=[t_w, tok_xn_done, b.F(("psq", tt % 2))],
                    inc=(kc == DC - 1))
            st = stage0 if tt % 2 == 0 else stage1
            tv = b.v(lambda v, pp=pp, st=st: v.tensor_copy(st[:], pp[:]),
                     deps=[mm_tok, b.F(("stP", tt % 2))])
            b.setF(("psq", tt % 2), tv)
            tg = b.sp(d_q[ec, :, tt * 512:(tt + 1) * 512], st[:], deps=[tv],
                      slot=f"Sstg{tt % 2}")
            b.setF(("stP", tt % 2), tg)
        b.setF(("qw", bi), b.tok("TE"))
        for _ in range(6):
            it = next(gB, None)
            if it is not None:
                it()
    for it in gB:
        it()
    if upto < 2:
        return finish(nc, b)
    b.v(lambda v: v.tensor_copy(rw_bf[:, 0:64], sb_rw), deps=[tsl])
    b.v(lambda v: v.tensor_tensor(rw_bf[:, 64:128], sb_rw, rw_bf[:, 0:64],
                                  ALU.subtract))
    tok_wprep = b.all_dma_tok()

    cc_idx = [0]

    def allreduce_max(dep_tok):
        i = cc_idx[0]
        cc_idx[0] += 1
        tg1 = b.sp(d_cc_in[i].ap(), rowb[0:1, 0:8], deps=[dep_tok],
                   slot="Sccin")
        tcc = b.op("gpsimd", "CC", lambda g, i=i: g.collective_compute(
            "AllReduce", ALU.max, replica_groups=[list(range(8))],
            ins=[d_cc_in[i].ap().opt()], outs=[d_cc_out[i].ap().opt()]),
            deps=[tg1])
        return b.sp(rowb[0:1, 8:16], d_cc_out[i].ap(), deps=[tcc],
                    slot="Sccout")

    def finalize_partial_max(acc_ap, dst_col, dep):
        def ev(dst, tt, c=dst_col):
            return b.v(lambda v: v.reduce_max(
                rowb[0:1, c:c + 1], dst, axis=AX.X), deps=[tt])
        return do_transpose(acc_ap, 1, 128, ident[:], ev, deps=[dep, tg_id])

    def bcast_scalars(cols_src, n, dst_sc_col, dep):
        tt = b.t(lambda t, n=n: t.matmul(
            psp[:, 0:n], ones_1, rowb[0:1, cols_src:cols_src + n],
            start=True, stop=True), deps=[dep, b.F("pspR")])
        tv = b.v(lambda v, n=n, c=dst_sc_col: v.tensor_copy(
            SC[:, c:c + n], psp[:, 0:n]), deps=[tt])
        b.setF("pspR", tv)
        b.v(lambda v: v.engine_nop())
        b.v(lambda v: v.engine_nop())
        return b.tok("VE")

    if upto < 3:
        return finish(nc, b)
    tfin = finalize_partial_max(s0acc, 0, tok_xn_done)
    tcc1 = allreduce_max(tfin)
    tv = b.v(lambda v: v.tensor_scalar(
        rowb[0:1, 16:17], rowb[0:1, 8:9], 1.0 / QA, 1e-12, ALU.mult, ALU.max),
        deps=[tcc1])
    tv = b.v(lambda v: v.reciprocal(rowb[0:1, 17:18], rowb[0:1, 16:17]))
    tok_r1 = bcast_scalars(16, 2, 0, tv)

    if upto < 4:
        return finish(nc, b)
    b.barrier()
    raw = bigB
    pt_sb = [ptA, ptB]

    def a_scores(i):
        JT = i // 4 + 1
        k = i % 4
        t_q = b.ld(qa[:].rearrange("p (c m) -> p c m", m=128),
                   d_q[:, :, i * 128:(i + 1) * 128].rearrange(
                       "c p m -> p c m"), deps=[b.F("qa_A"), tok_xn_sp],
                   slot="Lqa")
        for jt in range(JT):
            pp = ps0 if jt % 2 == 0 else ps1
            mm_tok = None
            for kc in range(DC):
                mm_tok = b.t(lambda t, kc=kc, jt=jt, pp=pp: t.matmul(
                    pp[:], qa[:, kc * 128:(kc + 1) * 128],
                    bigA[:, kc * S + jt * 512: kc * S + jt * 512 + 512],
                    start=(kc == 0), stop=(kc == DC - 1)),
                    deps=[t_q, b.F(("psA", jt % 2))],
                    inc=(kc == DC - 1))
            if jt < JT - 1:
                tv = b.v(lambda v, jt=jt, pp=pp: v.tensor_copy(
                    raw[:, jt * 512:(jt + 1) * 512], pp[:]),
                    deps=[mm_tok, b.F("rawP")])
            else:
                base = jt * 512
                if k > 0:
                    b.v(lambda v, pp=pp, base=base, k=k: v.tensor_copy(
                        raw[:, base:base + k * 128], pp[:, 0:k * 128]),
                        deps=[mm_tok, b.F("rawP"), tg_const])
                tv = b.v(lambda v, pp=pp, base=base, k=k:
                         v.tensor_tensor(
                             raw[:, base + k * 128:base + (k + 1) * 128],
                             pp[:, k * 128:(k + 1) * 128], maskT[:], ALU.add),
                         deps=[mm_tok, b.F("rawP"), tg_const])
                if i == 0:
                    tv = b.v(lambda v, pp=pp: v.tensor_copy(
                        raw[0:1, 0:1], pp[0:1, 0:1]))
                if k < 3:
                    tv = b.v(lambda v, base=base, k=k: v.memset(
                        raw[:, base + (k + 1) * 128:base + 512], NEG))
            b.setF(("psA", jt % 2), tv)
        b.setF("qa_A", b.tok("TE"))

    def a_softmax(i):
        W = (i // 4 + 1) * 512
        b.v(lambda v, W=W: v.reduce_max(rowm, raw[:, 0:W], axis=AX.X))
        b.v(lambda v: v.tensor_scalar(nrowm, rowm, -1.0, None, ALU.mult))
        b.v(lambda v: v.engine_nop())
        tvn = b.v(lambda v: v.engine_nop())
        tse = b.s(lambda s, W=W: s.activation(
            bigB[:, 4096:4096 + W], raw[:, 0:W], ACT.Exp, bias=nrowm,
            accum_out=lsum), deps=[tvn, b.F("Pexp")])
        tv_rl = b.v(lambda v: v.reciprocal(rlsum, lsum), deps=[tse])
        b.v(lambda v: v.engine_nop())
        b.v(lambda v: v.engine_nop())
        b.setF("rawP", tse)
        return tse, tv_rl

    def a_ctx(i, tse, tv_rl):
        tm1 = None
        kbufs = (tbufA[:, 0:1024], tbufB[:, 0:1024],
                 wsA[:, 0:1024], wsB[:, 0:1024])
        for jc in range(i + 1):
            sl = jc % 2
            k4 = jc % 4
            xb = kbufs[k4]
            t_k = b.ld(xb, d_xntm[jc * 128:(jc + 1) * 128, :],
                       deps=[b.F(("xbA", k4)), tok_xn_sp],
                       slot=f"Ltb{k4}")

            def evP(dst, tt, sl=sl):
                return b.v(lambda v: v.tensor_copy(pt_sb[sl][:], dst),
                           deps=[tt, b.F(("ptsbA", sl))])
            tv = do_transpose(
                bigB[:, 4096 + jc * 128:4096 + (jc + 1) * 128], 128, 128,
                ident[:], evP, deps=[tse])
            tm0 = b.t(lambda t, sl=sl, jc=jc, i=i, xb=xb: t.matmul(
                pc0[:], pt_sb[sl][:], xb[:, 0:512],
                start=(jc == 0), stop=(jc == i)),
                deps=[tv, t_k] + ([b.F("pcFree")] if jc == 0 else []),
                inc=(jc == i))
            tm1 = b.t(lambda t, sl=sl, jc=jc, i=i, xb=xb: t.matmul(
                pc1[:], pt_sb[sl][:], xb[:, 512:1024],
                start=(jc == 0), stop=(jc == i)),
                deps=[], inc=True)
            b.setF(("ptsbA", sl), tm1)
            b.setF(("xbA", k4), tm1)
        tv0 = b.v(lambda v: v.tensor_scalar(
            ctxtm[:, 0:512], pc0[:], rlsum, None, ALU.mult),
            deps=[tm1, tv_rl, b.F("ctxtmA")])
        tv1 = b.v(lambda v: v.tensor_scalar(
            ctxtm[:, 512:1024], pc1[:], rlsum, None, ALU.mult))
        b.setF("pcFree", tv1)
        for dc in range(DC):
            stg = stage0 if dc % 2 == 0 else stage1

            def evC(dst, tt, dc=dc, stg=stg, i=i):
                tv = b.v(lambda v: v.tensor_copy(stg[:, 0:128], dst),
                         deps=[tt, b.F(("stA", dc % 2))])
                tg = b.sp(d_ctx[dc * 128:(dc + 1) * 128,
                                i * 128:(i + 1) * 128],
                          stg[:, 0:128], deps=[tv], slot=f"Sstg{dc % 2}")
                b.setF(("stA", dc % 2), tg)
                return tv
            do_transpose(ctxtm[:, dc * 128:(dc + 1) * 128], 128, 128,
                         ident[:], evC, deps=[tv1])
        b.setF("ctxtmA", b.tok("TE"))
        b.setF("Pexp", b.tok("TE"))

    pend = None
    for i in range(TI):
        a_scores(i)
        if pend is not None:
            a_ctx(*pend)
        tse, tv_rl = a_softmax(i)
        pend = (i, tse, tv_rl)
    a_ctx(*pend)
    tok_attn = b.all_dma_tok()

    if upto < 5:
        return finish(nc, b)
    b.barrier()
    for tt in range(8):
        ctx_toks = []
        for kc8 in range(8):
            ctx_toks.append(b.ld(
                bigB[:, kc8 * 512:(kc8 + 1) * 512],
                d_ctx[kc8 * 128:(kc8 + 1) * 128, tt * 512:(tt + 1) * 512],
                deps=[b.F("bigB_A2"), tok_attn], slot="LbigB", group=True))
        t_ctx = ctx_toks[-1]
        for oc in range(8):
            sl = oc % 2
            ws = wsA if sl == 0 else wsB
            tw0 = b.ld(ws[:, 0:1024].rearrange("p (a m) -> p a m", m=128),
                       d_woutT[oc, 0:8].rearrange("a p m -> p a m"),
                       deps=[b.F(("wsA2", sl)), tok_wprep],
                       slot=f"Lws{sl}")
            tw1 = b.ld(ws[:, 1024:2048].rearrange("p (a m) -> p a m", m=128),
                       d_woutT[oc, 8:16].rearrange("a p m -> p a m"),
                       slot=f"Lws{sl}")
            tw0 = tw1
            pp = ps0 if sl == 0 else ps1
            mm_tok = None
            for kc in range(16):
                if kc < 8:
                    rhs = bigA[:, kc * S + tt * 512: kc * S + tt * 512 + 512]
                    dep = [tw0]
                else:
                    rhs = bigB[:, (kc - 8) * 512:(kc - 7) * 512]
                    dep = [tw1, t_ctx]
                mm_tok = b.t(lambda t, kc=kc, pp=pp, ws=ws, rhs=rhs: t.matmul(
                    pp[:], ws[:, kc * 128:(kc + 1) * 128], rhs,
                    start=(kc == 0), stop=(kc == 15)),
                    deps=dep + [b.F(("psA2", sl))], inc=(kc == 15))
            stg = stage0 if sl == 0 else stage1
            tsv = b.s(lambda s, pp=pp, stg=stg, oc=oc: s.activation(
                stg[:], pp[:], ACT.Tanh, bias=sb_bout[:, oc:oc + 1]),
                deps=[mm_tok, tsl, b.F(("stA2", sl))])
            b.setF(("psA2", sl), tsv)
            tg = b.sp(d_satt[oc * 128:(oc + 1) * 128, tt * 512:(tt + 1) * 512],
                      stg[:], deps=[tsv], slot=f"Sstg{sl}")
            b.setF(("stA2", sl), tg)
            b.setF(("wsA2", sl), b.tok("TE"))
        b.setF("bigB_A2", b.tok("TE"))
    tok_a2 = b.all_dma_tok()

    if upto < 6:
        return finish(nc, b)
    b.barrier()
    qa_bf = qa[:].bitcast(BF16)
    ctx_bf = ctxtm[:].bitcast(BF16)
    tbA_bf = tbufA[:].bitcast(BF16)
    tv_xq = None
    for kc in range(DC):
        for h in range(8):
            sl = slice(kc * S + h * 512, kc * S + h * 512 + 512)
            hsl = slice((h % 4) * 512, (h % 4) * 512 + 512)
            b.v(lambda v, sl=sl: v.tensor_scalar(
                stage0[:], bigA[:, sl], SC[:, 1:2], MAGIC, ALU.mult, ALU.add),
                deps=[tok_r1, b.F("xqscr")])
            b.v(lambda v: v.tensor_scalar(
                stage0[:], stage0[:], MAGIC, -(QA + 1.0),
                ALU.subtract, ALU.max))
            b.v(lambda v: v.tensor_scalar(
                stage1[:], stage0[:], QA, SC[:, 0:1], ALU.min, ALU.mult))
            b.v(lambda v, hsl=hsl: v.tensor_copy(qa_bf[:, hsl], stage1[:]))
            lodst = ctx_bf if h < 4 else tbA_bf
            b.v(lambda v, hsl=hsl, lodst=lodst: v.tensor_tensor(
                lodst[:, hsl], stage1[:], qa_bf[:, hsl], ALU.subtract))
            if h == 3:
                b.v(lambda v, kc=kc: v.tensor_copy(
                    bigA_bf[:, 2 * kc * S: 2 * kc * S + 2048],
                    qa_bf[:, 0:2048]))
        b.v(lambda v, kc=kc: v.tensor_copy(
            bigA_bf[:, 2 * kc * S + 2048: 2 * kc * S + 4096], qa_bf[:, 0:2048]))
        b.v(lambda v, kc=kc: v.tensor_copy(
            bigA_bf[:, 2 * kc * S + S: 2 * kc * S + S + 2048],
            ctx_bf[:, 0:2048]))
        tv_xq = b.v(lambda v, kc=kc: v.tensor_copy(
            bigA_bf[:, 2 * kc * S + S + 2048: 2 * kc * S + S + 4096],
            tbA_bf[:, 0:2048]))
        b.setF("xqscr", None)
    tok_xq = tv_xq

    for w_i, (dT, d_pre, acc) in enumerate(
            ((d_wupT, d_up, upacc), (d_wgateT, d_gp, gacc))):
        for oc in range(24):
            sl = oc % 2
            ws = wsA if sl == 0 else wsB
            wbf = wsA_bf if sl == 0 else wsB_bf
            t_w = b.ld(wbf[:, 0:2048].rearrange("p (a m) -> p a m", m=256),
                       dT[oc].rearrange("a p m -> p a m"),
                       deps=[b.F(("wsB1", sl)), tok_wprep],
                       slot=f"Lws{sl}")
            for tt in range(8):
                pp = ps0 if tt % 2 == 0 else ps1
                mm_tok = None
                for kc in range(DC):
                    wh = wbf[:, kc * 256: kc * 256 + 128]
                    wl = wbf[:, kc * 256 + 128: kc * 256 + 256]
                    rh = bigA_bf[:, 2 * kc * S + tt * 512:
                                 2 * kc * S + tt * 512 + 512]
                    rl_ = bigA_bf[:, 2 * kc * S + S + tt * 512:
                                  2 * kc * S + S + tt * 512 + 512]
                    b.t(lambda t, pp=pp, wh=wh, rh=rh, kc=kc: t.matmul(
                        pp[:], wh, rh, start=(kc == 0), stop=False),
                        deps=[t_w, tok_xq, b.F(("psB1", tt % 2))], inc=False)
                    b.t(lambda t, pp=pp, wh=wh, rl_=rl_: t.matmul(
                        pp[:], wh, rl_, start=False, stop=False), inc=False)
                    mm_tok = b.t(lambda t, pp=pp, wl=wl, rh=rh, kc=kc:
                                 t.matmul(pp[:], wl, rh, start=False,
                                          stop=(kc == DC - 1)),
                                 inc=(kc == DC - 1))
                stg = stage0 if tt % 2 == 0 else stage1
                tv = b.v(lambda v, pp=pp, stg=stg: v.tensor_copy(
                    stg[:], pp[:]), deps=[mm_tok, b.F(("stB1", tt % 2))])
                b.setF(("psB1", tt % 2), tv)
                b.v(lambda v, stg=stg: v.tensor_reduce(
                    tmp128, stg[:], axis=AX.X, op=ALU.max,
                    apply_absolute_value=True))
                b.v(lambda v, acc=acc: v.tensor_max(acc, acc, tmp128))
                tg = b.sp(d_pre[oc * 128:(oc + 1) * 128,
                                tt * 512:(tt + 1) * 512], stg[:], deps=[tv],
                          slot=f"Sstg{tt % 2}")
                b.setF(("stB1", tt % 2), tg)
            b.setF(("wsB1", sl), b.tok("TE"))
    tok_p1v = b.tok("VE")
    b.barrier()

    t1 = finalize_partial_max(upacc, 0, tok_p1v)
    t2 = finalize_partial_max(gacc, 1, t1)
    tcc2 = allreduce_max(t2)
    b.v(lambda v: v.tensor_scalar(
        rowb[0:1, 16:17], rowb[0:1, 8:9], 1.0 / QA, 1e-12, ALU.mult, ALU.max),
        deps=[tcc2])
    b.v(lambda v: v.reciprocal(rowb[0:1, 17:18], rowb[0:1, 16:17]))
    b.v(lambda v: v.tensor_scalar(
        rowb[0:1, 18:19], rowb[0:1, 9:10], 1.0 / QA, 1e-12, ALU.mult,
        ALU.max))
    b.v(lambda v: v.reciprocal(rowb[0:1, 19:20], rowb[0:1, 18:19]))
    tv = b.v(lambda v: v.tensor_scalar(
        rowb[0:1, 20:21], rowb[0:1, 18:19], 1.0 / 65536.0, None, ALU.mult))
    tok_r2 = bcast_scalars(16, 5, 2, tv)

    mb_half = [ctxtm[:, 0:512], ctxtm[:, 512:1024]]
    for oc in range(24):
        for tt in range(8):
            i2 = (oc * 8 + tt) % 2
            gp_buf = stage0 if i2 == 0 else stage1
            t_g = b.ld(gp_buf[:],
                       d_gp[oc * 128:(oc + 1) * 128, tt * 512:(tt + 1) * 512],
                       deps=[b.F(("gpB2", i2)), tok_a2], slot=f"Lstg{i2}")
            cbuf = tbufA[:, 0:512] if i2 == 0 else tbufA[:, 512:1024]
            sbuf_ = tbufB[:, 0:512] if i2 == 0 else tbufB[:, 512:1024]
            b.v(lambda v, gp_buf=gp_buf, cbuf=cbuf: v.tensor_scalar(
                cbuf, gp_buf[:], SC[:, 5:6], MAGIC, ALU.mult, ALU.add),
                deps=[t_g, tok_r2, b.F(("cB2", i2))])
            b.v(lambda v, cbuf=cbuf: v.tensor_scalar(
                cbuf, cbuf, MAGIC, -(QA + 1.0), ALU.subtract, ALU.max))
            tvc = b.v(lambda v, cbuf=cbuf: v.tensor_scalar(
                cbuf, cbuf, QA, None, ALU.min))
            tss = b.s(lambda s, cbuf=cbuf, sbuf_=sbuf_: s.activation(
                sbuf_, cbuf, ACT.Sigmoid, scale=SC[:, 4:5]),
                deps=[tvc, b.F(("sigB2", i2))])
            b.v(lambda v, sbuf_=sbuf_, gp_buf=gp_buf: v.tensor_scalar(
                gp_buf[:], sbuf_, 65536.0, MAGIC, ALU.mult, ALU.add),
                deps=[tss])
            b.setF(("sigB2", i2), b.tok("VE"))
            b.v(lambda v, gp_buf=gp_buf: v.tensor_scalar(
                gp_buf[:], gp_buf[:], MAGIC, 0.0, ALU.subtract, ALU.max))
            b.v(lambda v, gp_buf=gp_buf: v.tensor_scalar(
                gp_buf[:], gp_buf[:], 65535.0, None, ALU.min))
            mb = mb_half[i2]
            tvm = b.v(lambda v, cbuf=cbuf, gp_buf=gp_buf, mb=mb:
                      v.tensor_tensor(mb, cbuf, gp_buf[:], ALU.mult),
                      deps=[b.F(("mB2", i2))])
            b.setF(("cB2", i2), tvm)
            b.setF(("gpB2", i2), tvm)
            b.v(lambda v, mb=mb: v.tensor_reduce(
                tmp128, mb, axis=AX.X, op=ALU.max, apply_absolute_value=True))
            b.v(lambda v: v.tensor_max(macc, macc, tmp128))
            tg = b.sp(d_m[oc * 128:(oc + 1) * 128, tt * 512:(tt + 1) * 512],
                      mb, deps=[tvm], slot=f"Sctx{i2}")
            b.setF(("mB2", i2), tg)
    tok_p2a = b.tok("VE")
    b.barrier()

    t1 = finalize_partial_max(macc, 0, tok_p2a)
    tcc3 = allreduce_max(t1)
    b.v(lambda v: v.tensor_tensor(
        rowb[0:1, 16:17], rowb[0:1, 8:9], rowb[0:1, 20:21], ALU.mult),
        deps=[tcc3])
    b.v(lambda v: v.tensor_scalar(
        rowb[0:1, 16:17], rowb[0:1, 16:17], 1.0 / QA, 1e-12,
        ALU.mult, ALU.max))
    b.v(lambda v: v.reciprocal(rowb[0:1, 21:22], rowb[0:1, 16:17]))
    tv = b.v(lambda v: v.tensor_tensor(
        rowb[0:1, 17:18], rowb[0:1, 20:21], rowb[0:1, 21:22], ALU.mult))
    tok_r3 = bcast_scalars(16, 2, 7, tv)

    for oc in range(24):
        for tt in range(8):
            i2 = (oc * 8 + tt) % 2
            m_buf = stage0 if i2 == 0 else stage1
            u_buf = tbufA[:, 0:512] if i2 == 0 else tbufA[:, 512:1024]
            t_m = b.ld(m_buf[:],
                       d_m[oc * 128:(oc + 1) * 128, tt * 512:(tt + 1) * 512],
                       deps=[b.F(("m2B", i2))], slot=f"Lstg{i2}")
            t_u = b.ld(u_buf,
                       d_up[oc * 128:(oc + 1) * 128, tt * 512:(tt + 1) * 512],
                       deps=[b.F(("u2B", i2))], slot=f"LtbA{i2}")
            c3buf = tbufB[:, 0:512] if i2 == 0 else tbufB[:, 512:1024]
            b.v(lambda v, m_buf=m_buf, c3buf=c3buf: v.tensor_scalar(
                c3buf, m_buf[:], SC[:, 8:9], MAGIC, ALU.mult, ALU.add),
                deps=[t_m, tok_r3, b.F(("c3B", i2))])
            b.setF(("m2B", i2), b.tok("VE"))
            b.v(lambda v, c3buf=c3buf: v.tensor_scalar(
                c3buf, c3buf, MAGIC, -(QA + 1.0), ALU.subtract, ALU.max))
            b.v(lambda v, c3buf=c3buf: v.tensor_scalar(
                c3buf, c3buf, QA, None, ALU.min))
            b.v(lambda v, u_buf=u_buf: v.tensor_scalar(
                u_buf, u_buf, SC[:, 3:4], MAGIC, ALU.mult, ALU.add),
                deps=[t_u])
            b.v(lambda v, u_buf=u_buf: v.tensor_scalar(
                u_buf, u_buf, MAGIC, -(QA + 1.0), ALU.subtract, ALU.max))
            b.v(lambda v, u_buf=u_buf: v.tensor_scalar(
                u_buf, u_buf, QA, None, ALU.min))
            mb = mb_half[i2]
            tvm = b.v(lambda v, c3buf=c3buf, u_buf=u_buf, mb=mb:
                      v.tensor_tensor(mb, c3buf, u_buf, ALU.mult),
                      deps=[b.F(("mm2B", i2))])
            b.setF(("c3B", i2), tvm)
            b.setF(("u2B", i2), tvm)
            b.v(lambda v, mb=mb: v.tensor_reduce(
                tmp128, mb, axis=AX.X, op=ALU.max, apply_absolute_value=True))
            b.v(lambda v: v.tensor_max(mmacc, mmacc, tmp128))
            tg = b.sp(d_mm[oc * 128:(oc + 1) * 128, tt * 512:(tt + 1) * 512],
                      mb, deps=[tvm], slot=f"Sctx{i2}")
            b.setF(("mm2B", i2), tg)
    tok_p2b = b.tok("VE")
    b.barrier()

    t1 = finalize_partial_max(mmacc, 0, tok_p2b)
    tcc4 = allreduce_max(t1)
    b.v(lambda v: v.tensor_tensor(
        rowb[0:1, 22:23], SC[0:1, 7:8], SC[0:1, 2:3], ALU.mult),
        deps=[tcc4])
    b.v(lambda v: v.tensor_tensor(
        rowb[0:1, 16:17], rowb[0:1, 8:9], rowb[0:1, 22:23], ALU.mult))
    b.v(lambda v: v.tensor_scalar(
        rowb[0:1, 16:17], rowb[0:1, 16:17], 1.0 / QA, 1e-12,
        ALU.mult, ALU.max))
    b.v(lambda v: v.reciprocal(rowb[0:1, 23:24], rowb[0:1, 16:17]))
    tv = b.v(lambda v: v.tensor_tensor(
        rowb[0:1, 17:18], rowb[0:1, 22:23], rowb[0:1, 23:24], ALU.mult))
    tok_r4 = bcast_scalars(16, 2, 9, tv)

    b.barrier()
    TT256 = 16
    for tt in range(TT256):
        t0 = tt * 256
        oq_toks = []
        for ic in range(24):
            i2 = ic % 2
            mmb = stage0 if i2 == 0 else stage1
            t_mm = b.ld(mmb[:, 0:256],
                        d_mm[ic * 128:(ic + 1) * 128, t0:t0 + 256],
                        deps=[b.F(("mmC", i2))], slot=f"Lstg{i2}")
            b.v(lambda v, mmb=mmb: v.tensor_scalar(
                mmb[:, 256:512], mmb[:, 0:256], SC[:, 10:11], MAGIC,
                ALU.mult, ALU.add),
                deps=[t_mm, tok_r4])
            b.v(lambda v, mmb=mmb: v.tensor_scalar(
                mmb[:, 256:512], mmb[:, 256:512], MAGIC, -(QA + 1.0),
                ALU.subtract, ALU.max))
            b.v(lambda v, mmb=mmb: v.tensor_scalar(
                mmb[:, 0:256], mmb[:, 256:512], QA, None, ALU.min),
                deps=[b.F("bigB_B")])
            b.v(lambda v, mmb=mmb, ic=ic: v.tensor_copy(
                bigB_bf[:, 512 * ic: 512 * ic + 256], mmb[:, 0:256]))
            oq_toks.append(b.v(lambda v, mmb=mmb, ic=ic: v.tensor_tensor(
                bigB_bf[:, 512 * ic + 256: 512 * ic + 512], mmb[:, 0:256],
                bigB_bf[:, 512 * ic: 512 * ic + 256], ALU.subtract)))
            b.setF(("mmC", i2), b.tok("VE"))
        t_oq = oq_toks[-1]
        x1_toks = []
        for oc2 in range(8):
            sl = oc2 % 2
            twA = b.ld(wsA_bf[:, 0:3072].rearrange("p (a m) -> p a m",
                                                    m=256),
                       d_wdownT[oc2, 0:12].rearrange("a p m -> p a m"),
                       deps=[b.F(("wsC", 0)), tok_wprep], slot="Lws0")
            twB = b.ld(wsB_bf[:, 0:3072].rearrange("p (a m) -> p a m",
                                                   m=256),
                       d_wdownT[oc2, 12:24].rearrange("a p m -> p a m"),
                       deps=[b.F(("wsC", 1)), tok_wprep], slot="Lws1")
            pp = ps0 if sl == 0 else ps1
            mm_tok = None
            for ic in range(24):
                wbf_ = wsA_bf if ic < 12 else wsB_bf
                icl = ic if ic < 12 else ic - 12
                wh = wbf_[:, icl * 256: icl * 256 + 128]
                wl = wbf_[:, icl * 256 + 128: icl * 256 + 256]
                rh = bigB_bf[:, 512 * ic: 512 * ic + 256]
                rl_ = bigB_bf[:, 512 * ic + 256: 512 * ic + 512]
                b.t(lambda t, pp=pp, wh=wh, rh=rh, ic=ic: t.matmul(
                    pp[:, 0:256], wh, rh, start=(ic == 0), stop=False),
                    deps=[twA if ic < 12 else twB, oq_toks[ic],
                          b.F(("psC", sl))], inc=False)
                b.t(lambda t, pp=pp, wh=wh, rl_=rl_: t.matmul(
                    pp[:, 0:256], wh, rl_, start=False, stop=False),
                    inc=False)
                mm_tok = b.t(lambda t, pp=pp, wl=wl, rh=rh, ic=ic: t.matmul(
                    pp[:, 0:256], wl, rh, start=False, stop=(ic == 23)),
                    inc=(ic == 23))
            b.setF(("wsC", 0), mm_tok)
            b.setF(("wsC", 1), mm_tok)
            gb = tbufA[:, 0:256] if sl == 0 else tbufA[:, 256:512]
            tsv = b.s(lambda s, pp=pp, gb=gb: s.activation(
                gb, pp[:, 0:256], ACT.Silu, scale=SC[:, 9:10]),
                deps=[mm_tok, b.F(("gC", sl))])
            b.setF(("psC", sl), tsv)
            xb = tbufB[:, 0:256] if sl == 0 else tbufB[:, 256:512]
            sb_ = tbufB[:, 512:768] if sl == 0 else tbufB[:, 768:1024]
            t_x = b.ld(xb, d_xfm[oc2 * 128:(oc2 + 1) * 128, t0:t0 + 256],
                       deps=[b.F(("xC", sl))], slot=f"LtbB{sl}")
            t_s = b.ld(sb_, d_satt[oc2 * 128:(oc2 + 1) * 128, t0:t0 + 256],
                       deps=[b.F(("sC", sl)), tok_a2], slot=f"LtbB{2 + sl}")
            sgb = tbufA[:, 512 + sl * 256: 512 + (sl + 1) * 256]
            tv = b.v(lambda v, sb_=sb_, gb=gb, sgb=sgb: v.tensor_tensor(
                sgb, sb_, gb, ALU.mult), deps=[tsv, t_s])
            b.setF(("gC", sl), tv)
            b.setF(("sC", sl), tv)
            tvx1 = b.v(lambda v, xb=xb, sgb=sgb, oc2=oc2: v.tensor_tensor(
                bigB[:, 6144 + oc2 * 256: 6144 + (oc2 + 1) * 256],
                xb, sgb, ALU.add),
                deps=[t_x, b.F(("x1sp", oc2))])
            b.setF(("xC", sl), tvx1)
            x1_toks.append(tvx1)
            tgx = b.sp(d_x1[oc2 * 128:(oc2 + 1) * 128, t0:t0 + 256],
                       bigB[:, 6144 + oc2 * 256: 6144 + (oc2 + 1) * 256],
                       deps=[tvx1], slot=f"Sx1{oc2 % 4}")
            b.setF(("x1sp", oc2), tgx)
        tok_x1 = x1_toks[-1]
        mm_tok = None
        for kc in range(DC):
            x1c = bigB[:, 6144 + kc * 256: 6144 + (kc + 1) * 256]
            mm_tok = b.t(lambda t, x1c=x1c, kc=kc: t.matmul(
                pst[0:1, 0:256], ones128, x1c,
                start=(kc == 0), stop=(kc == DC - 1)),
                deps=[tok_x1, b.F("pstB")], inc=(kc == DC - 1))
        for kc in range(DC):
            x1c = bigB[:, 6144 + kc * 256: 6144 + (kc + 1) * 256]
            i2 = kc % 2
            sqb = stage0 if i2 == 0 else stage1
            tsq = b.s(lambda s, x1c=x1c, sqb=sqb: s.activation(
                sqb[:, 0:256], x1c, ACT.Square),
                deps=[tok_x1, b.F(("sqB", i2))])
            mm_tok = b.t(lambda t, sqb=sqb, kc=kc: t.matmul(
                pst[0:1, 256:512], ones128, sqb[:, 0:256],
                start=(kc == 0), stop=(kc == DC - 1)),
                deps=[tsq], inc=True)
            b.setF(("sqB", i2), mm_tok)
        tv = b.v(lambda v: v.tensor_scalar(
            rb_m, pst[0:1, 0:256], 1.0 / D, None, ALU.mult), deps=[mm_tok])
        tvq = b.v(lambda v: v.tensor_scalar(
            rb_msq, pst[0:1, 256:512], 1.0 / D, None, ALU.mult))
        b.setF("pstB", tvq)
        b.v(lambda v: v.tensor_tensor(rb_m2, rb_m, rb_m, ALU.mult))
        tvv = b.v(lambda v: v.tensor_tensor(rb_var, rb_msq, rb_m2,
                                            ALU.subtract))
        tsv = b.s(lambda s: s.activation(rb_vstd, rb_var, ACT.Sqrt,
                                         bias=rowb[0:1, 32:33]),
                  deps=[tvv])
        tvr = b.v(lambda v: v.reciprocal(rb_rstd, rb_vstd), deps=[tsv])
        tbm = b.t(lambda t: t.matmul(psp[:, 0:256], ones_1, rb_m,
                                     start=True, stop=True),
                  deps=[tv, b.F("pspB")])
        tbr = b.t(lambda t: t.matmul(psp[:, 256:512], ones_1, rb_rstd,
                                     start=True, stop=True), deps=[tvr])
        xn1_toks = []
        for kc in range(DC):
            x1c = bigB[:, 6144 + kc * 256: 6144 + (kc + 1) * 256]
            i2 = kc % 2
            df = stage0 if i2 == 0 else stage1
            b.v(lambda v, x1c=x1c, df=df: v.tensor_tensor(
                df[:, 256:512], x1c, psp[:, 0:256], ALU.subtract),
                deps=[tbm, tbr, b.F(("dfB", i2))])
            b.v(lambda v, df=df: v.tensor_tensor(
                df[:, 0:256], df[:, 256:512], psp[:, 256:512], ALU.mult))
            b.v(lambda v, df=df, kc=kc, t0=t0: v.tensor_copy(
                xn1_hi(kc, t0, 256), df[:, 0:256]))
            tvx = b.v(lambda v, df=df, kc=kc, t0=t0: v.tensor_tensor(
                xn1_lo(kc, t0, 256), df[:, 0:256], xn1_hi(kc, t0, 256),
                ALU.subtract))
            b.setF(("dfB", i2), tvx)
            xn1_toks.append(tvx)
        b.setF("pspB", xn1_toks[-1])
        b.setF("bigB_B", [xn1_toks[-1], b.tok("SE"), b.tok("TE")]
               + b.all_dma_tok())
        b.setF(("mmC", 0), [b.tok("SE"), b.tok("VE")])
        b.setF(("mmC", 1), [b.tok("SE"), b.tok("VE")])
    tok_xn1 = b.tok("VE")
    tok_x1sp = b.all_dma_tok()

    if upto < 7:
        return finish(nc, b)
    b.barrier()
    for ec in range(DC):
        tw = []
        for wi, dT in enumerate((d_WgT, d_WvT, d_WdT)):
            dst = (wsA_bf[:, 0:2048], wsA_bf[:, 2048:4096],
                   wsB_bf[:, 0:2048])[wi]
            tw.append(b.ld(dst.rearrange("p (a m) -> p a m", m=256),
                           dT[ec].rearrange("a p m -> p a m"),
                           deps=[b.F("wsC2"), tok_wprep],
                           slot=("Lws0", "Lws0", "Lws1")[wi]))
        for tt in range(8):
            ppl = (ps0, ps1, ps2)
            mts = []
            for wi, pp in enumerate(ppl):
                mm_tok = None
                for kc in range(DC):
                    wsl = (wsA_bf[:, 0:2048], wsA_bf[:, 2048:4096],
                           wsB_bf[:, 0:2048])[wi]
                    wh = wsl[:, kc * 256: kc * 256 + 128]
                    wl = wsl[:, kc * 256 + 128: kc * 256 + 256]
                    rh = xn1_hi(kc, tt * 512, 512)
                    rl_ = xn1_lo(kc, tt * 512, 512)
                    b.t(lambda t, pp=pp, wh=wh, rh=rh, kc=kc: t.matmul(
                        pp[:], wh, rh, start=(kc == 0), stop=False),
                        deps=[tw[1] if wi < 2 else tw[2], tok_xn1,
                              b.F(("psC2", wi))], inc=False)
                    b.t(lambda t, pp=pp, wh=wh, rl_=rl_: t.matmul(
                        pp[:], wh, rl_, start=False, stop=False), inc=False)
                    mm_tok = b.t(lambda t, pp=pp, wl=wl, rh=rh, kc=kc:
                                 t.matmul(pp[:], wl, rh, start=False,
                                          stop=(kc == DC - 1)),
                                 inc=(kc == DC - 1))
                mts.append(mm_tok)
            ts1 = b.s(lambda s, ec=ec: s.activation(
                tbufA[:, 0:512], ps0[:], ACT.Sigmoid,
                bias=sb_bg[:, ec:ec + 1]),
                deps=[mts[0], tsl, b.F("tbAC")])
            ts2 = b.s(lambda s, ec=ec: s.activation(
                tbufA[:, 512:1024], ps1[:], ACT.Tanh,
                bias=sb_bv[:, ec:ec + 1]), deps=[mts[1]])
            b.setF(("psC2", 0), ts1)
            b.setF(("psC2", 1), ts2)
            tvxs = b.v(lambda v, tt=tt: v.tensor_tensor(
                bigB[:, tt * 512:(tt + 1) * 512],
                tbufA[:, 0:512], tbufA[:, 512:1024], ALU.mult),
                deps=[ts2, b.F("bigB_C")])
            ts3 = b.s(lambda s, ec=ec: s.activation(
                tbufB[:, 0:512], ps2[:], ACT.Sigmoid,
                bias=sb_bd[:, ec:ec + 1]), deps=[mts[2], b.F("tbBC")])
            b.setF(("psC2", 2), ts3)
            tva = b.v(lambda v, tt=tt: v.tensor_scalar(
                bigB[:, 4096 + tt * 512: 4096 + (tt + 1) * 512],
                tbufB[:, 0:512], 0.998, 0.001, ALU.mult, ALU.add),
                deps=[ts3])
            b.setF("tbAC", tvxs)
            b.setF("tbBC", tva)
        tvscan = b.v(lambda v: v.tensor_tensor_scan(
            bigB[:, 0:4096], bigB[:, 4096:8192], bigB[:, 0:4096],
            0.0, ALU.mult, ALU.add))
        tg = b.sp(d_s1[ec * 128:(ec + 1) * 128, :], bigB[:, 0:4096],
                  deps=[tvscan], slot="Ss1")
        b.setF("bigB_C", tg)
        b.setF("wsC2", b.tok("TE"))
    tok_c = b.all_dma_tok()

    if upto < 8:
        return finish(nc, b)
    b.barrier()
    for tt in range(TT256):
        t0 = tt * 256
        mm_tok = None
        for kc in range(DC):
            wh = rw_bf[:, kc * 8: kc * 8 + 8]
            wl = rw_bf[:, 64 + kc * 8: 64 + kc * 8 + 8]
            rh = xn1_hi(kc, t0, 256)
            rl_ = xn1_lo(kc, t0, 256)
            b.t(lambda t, wh=wh, rh=rh, kc=kc: t.matmul(
                pst[0:8, 0:256], wh, rh, start=(kc == 0), stop=False),
                deps=[tsl, tok_xn1, b.F("pstD")], inc=False)
            b.t(lambda t, wh=wh, rl_=rl_: t.matmul(
                pst[0:8, 0:256], wh, rl_, start=False, stop=False),
                inc=False)
            mm_tok = b.t(lambda t, wl=wl, rh=rh, kc=kc: t.matmul(
                pst[0:8, 0:256], wl, rh, start=False,
                stop=(kc == DC - 1)), inc=(kc == DC - 1))
        tv8 = b.v(lambda v: v.tensor_copy(stage0[0:8, 0:256],
                                          pst[0:8, 0:256]),
                  deps=[mm_tok, b.F("st0D")])
        b.setF("pstD", tv8)
        w_toks = []
        for sc in range(2):
            pr = probsA if sc == 0 else probsB

            def evR(dst, tt, pr=pr):
                return b.v(lambda v: v.tensor_copy(pr, dst), deps=[tt])
            tvp = do_transpose(stage0[0:8, sc * 128:(sc + 1) * 128], 128, 8,
                               ident[0:8, 0:8], evR, deps=[tv8])
            b.v(lambda v, pr=pr: v.reduce_max(rowm, pr, axis=AX.X))
            b.v(lambda v: v.tensor_scalar(nrowm, rowm, -1.0, None, ALU.mult))
            b.v(lambda v: v.engine_nop())
            tvn = b.v(lambda v: v.engine_nop())
            tse = b.s(lambda s, pr=pr: s.activation(
                r_eq, pr, ACT.Exp, bias=nrowm, accum_out=lsum), deps=[tvn])
            b.v(lambda v: v.reciprocal(rlsum, lsum), deps=[tse])
            b.v(lambda v: v.engine_nop())
            b.v(lambda v: v.engine_nop())
            b.v(lambda v: v.tensor_scalar(r_q, r_eq, rlsum, None, ALU.mult))
            b.v(lambda v: v.reduce_max(r_m1, r_q, axis=AX.X))
            b.v(lambda v: v.engine_nop())
            b.v(lambda v: v.engine_nop())
            b.v(lambda v: v.tensor_scalar(r_eq, r_q, r_m1, None,
                                          ALU.is_equal))
            b.v(lambda v: v.scalar_tensor_tensor(
                r_eq, r_eq, -1e30, r_q, ALU.mult, ALU.add))
            b.v(lambda v: v.reduce_max(r_m2, r_eq, axis=AX.X))
            b.v(lambda v: v.tensor_tensor(r_den, r_m1, r_m2, ALU.add))
            b.v(lambda v: v.engine_nop())
            b.v(lambda v: v.reciprocal(r_rden, r_den))
            b.v(lambda v: v.engine_nop())
            b.v(lambda v: v.engine_nop())
            b.v(lambda v: v.tensor_tensor(r_a1, r_m1, r_rden, ALU.mult))
            b.v(lambda v: v.tensor_tensor(r_a2, r_m2, r_rden, ALU.mult))
            w_toks.append(b.v(lambda v, sc=sc: v.tensor_tensor(
                w_col[:, sc:sc + 1], r_a1, r_a2, ALU.add)))
        b.setF("st0D", w_toks[-1])
        def evW(dst, tt):
            return b.v(lambda v: v.tensor_copy(stage1[0:2, 0:128], dst),
                       deps=[tt, b.F("st1D")])
        tvw = do_transpose(w_col, 2, 128, ident[:], evW, deps=[w_toks[-1]])
        tgw = b.sp(rb_wrow, stage1[0:2, 0:128], deps=[tvw], slot="Swrow")
        b.setF("st1D", tgw)
        tbw = b.t(lambda t: t.matmul(psp[:, 0:256], ones_1, rb_wrow,
                                     start=True, stop=True),
                  deps=[tgw, b.F("pspD")])
        for ocf in range(32):
            sl = ocf % 2
            wbf = wsA_bf if sl == 0 else wsB_bf
            t_wh = b.ld(wbf[:, 0:1024].rearrange("p (c m) -> p c m", m=128),
                        d_w1h[:, ocf * 128:(ocf + 1) * 128].rearrange(
                            "(c p) m -> p c m", p=128),
                        deps=[b.F(("wsD", sl))], slot=f"Lws{sl}")
            t_w = b.ld(wbf[:, 1024:2048].rearrange("p (c m) -> p c m", m=128),
                       d_w1l[:, ocf * 128:(ocf + 1) * 128].rearrange(
                           "(c p) m -> p c m", p=128),
                       deps=[b.F(("wsD", sl))], slot=f"Lws{sl}l")
            pp = ps0 if sl == 0 else ps1
            mm_tok = None
            for kc in range(DC):
                wh = wbf[:, kc * 128: kc * 128 + 128]
                wl = wbf[:, 1024 + kc * 128: 1024 + kc * 128 + 128]
                rh = xn1_hi(kc, t0, 256)
                rl_ = xn1_lo(kc, t0, 256)
                b.t(lambda t, pp=pp, wh=wh, rh=rh, kc=kc: t.matmul(
                    pp[:, 0:256], wh, rh, start=(kc == 0), stop=False),
                    deps=[t_wh, t_w, tok_xn1, b.F(("psD", sl))], inc=False)
                b.t(lambda t, pp=pp, wh=wh, rl_=rl_: t.matmul(
                    pp[:, 0:256], wh, rl_, start=False, stop=False),
                    inc=False)
                mm_tok = b.t(lambda t, pp=pp, wl=wl, rh=rh, kc=kc: t.matmul(
                    pp[:, 0:256], wl, rh, start=False, stop=(kc == DC - 1)),
                    inc=(kc == DC - 1))
            b.setF(("wsD", sl), mm_tok)
            gf = tbufA[:, 0:256] if sl == 0 else tbufA[:, 256:512]
            tsv = b.s(lambda s, pp=pp, ocf=ocf, gf=gf: s.activation(
                gf, pp[:, 0:256],
                ACT.Gelu_apprx_tanh, bias=sb_b1[:, ocf:ocf + 1]),
                deps=[mm_tok, tok_c, b.F("bigB_D"), b.F(("gD", sl))])
            b.setF(("psD", sl), tsv)
            b.v(lambda v, gf=gf, ocf=ocf: v.tensor_copy(
                bigB_bf[:, 512 * ocf: 512 * ocf + 256], gf), deps=[tsv])
            tvl = b.v(lambda v, gf=gf, ocf=ocf: v.tensor_tensor(
                bigB_bf[:, 512 * ocf + 256: 512 * ocf + 512], gf,
                bigB_bf[:, 512 * ocf: 512 * ocf + 256], ALU.subtract))
            b.setF(("gD", sl), tvl)
        tok_h1 = [b.tok("SE"), b.tok("VE")]
        fin_toks = []
        for oc2 in range(8):
            mm_tok = None
            for half in range(2):
                wbf = wsA_bf if half == 0 else wsB_bf
                t_wh2 = b.ld(
                    wbf[:, 0:2048].rearrange("p (c m) -> p c m", m=128),
                    d_w2h[half * 2048:(half + 1) * 2048,
                          oc2 * 128:(oc2 + 1) * 128].rearrange(
                        "(c p) m -> p c m", p=128),
                    deps=[b.F(("wsD", half)), b.F(("wsD2", half))],
                    slot=f"Lws{half}")
                t_w = b.ld(wbf[:, 2048:4096].rearrange(
                               "p (c m) -> p c m", m=128),
                           d_w2l[half * 2048:(half + 1) * 2048,
                                 oc2 * 128:(oc2 + 1) * 128].rearrange(
                               "(c p) m -> p c m", p=128),
                           deps=[b.F(("wsD", half)), b.F(("wsD2", half))],
                           slot=f"Lws{half}l")
                for icq in range(16):
                    ic = half * 16 + icq
                    wh = wbf[:, icq * 128: icq * 128 + 128]
                    wl = wbf[:, 2048 + icq * 128: 2048 + icq * 128 + 128]
                    rh = bigB_bf[:, 512 * ic: 512 * ic + 256]
                    rl_ = bigB_bf[:, 512 * ic + 256: 512 * ic + 512]
                    b.t(lambda t, wh=wh, rh=rh, ic=ic: t.matmul(
                        ps2[:, 0:256], wh, rh, start=(ic == 0), stop=False),
                        deps=[t_wh2, t_w, tok_h1, b.F("ps2D")], inc=False)
                    b.t(lambda t, wh=wh, rl_=rl_: t.matmul(
                        ps2[:, 0:256], wh, rl_, start=False, stop=False),
                        inc=False)
                    mm_tok = b.t(lambda t, wl=wl, rh=rh, ic=ic: t.matmul(
                        ps2[:, 0:256], wl, rh, start=False, stop=(ic == 31)),
                        inc=(icq == 15))
                b.setF(("wsD2", half), mm_tok)
                b.setF(("wsD", half), mm_tok)
            tse = b.s(lambda s, oc2=oc2: s.activation(
                tbufA[:, 0:256], ps2[:, 0:256], ACT.Identity,
                bias=sb_b2[:, oc2:oc2 + 1]), deps=[mm_tok, b.F("tbAD")])
            b.setF("ps2D", tse)
            tv = b.v(lambda v: v.tensor_tensor(
                tbufA[:, 256:512], tbufA[:, 0:256], psp[:, 0:256], ALU.mult),
                deps=[tse, tbw])
            tsg = b.s(lambda s: s.activation(
                tbufA[:, 512:768], tbufA[:, 256:512], ACT.Silu), deps=[tv])
            t_s1 = b.ld(tbufB[:, 0:256],
                        d_s1[oc2 * 128:(oc2 + 1) * 128, t0:t0 + 256],
                        deps=[b.F("tbBD"), tok_c], slot="LtbB0")
            t_x1 = b.ld(tbufB[:, 256:512],
                        d_x1[oc2 * 128:(oc2 + 1) * 128, t0:t0 + 256],
                        deps=[tok_x1sp], slot="LtbB1")
            tv = b.v(lambda v: v.tensor_tensor(
                tbufA[:, 768:1024], tbufB[:, 0:256], tbufA[:, 512:768],
                ALU.mult), deps=[tsg, t_s1])
            tvf = b.v(lambda v: v.tensor_tensor(
                tbufB[:, 512:768], tbufB[:, 256:512], tbufA[:, 768:1024],
                ALU.add), deps=[t_x1, b.F("finD")])
            b.setF("tbAD", tvf)
            for tc in range(2):
                ot = qa if tc == 0 else ctxtm

                def evF(dst, tt, oc2=oc2, ot=ot, tc=tc):
                    return b.v(lambda v: v.tensor_copy(
                        ot[:, oc2 * 128:(oc2 + 1) * 128], dst),
                        deps=[tt, b.F(("otD", tc))])
                tvo = do_transpose(
                    tbufB[:, 512 + tc * 128: 512 + (tc + 1) * 128], 128, 128,
                    ident[:], evF, deps=[tvf])
                fin_toks.append(tvo)
            b.setF("finD", b.tok("TE"))
            b.setF("tbBD", b.tok("VE"))
        b.setF("bigB_D", b.tok("TE"))
        tg0 = b.sp(d_out[t0:t0 + 128, :], qa[:], deps=[fin_toks[-1]],
                   slot="Sout0")
        tg1 = b.sp(d_out[t0 + 128:t0 + 256, :], ctxtm[:],
                   deps=[fin_toks[-1]], slot="Sout1")
        b.setF(("otD", 0), tg0)
        b.setF(("otD", 1), tg1)
        b.setF("pspD", b.tok("VE"))

    return finish(nc, b)


def finish(nc, b):
    for s in list(b.cnt):
        if s in ("TE", "VE", "SE", "CC", "GP", "LD"):
            continue
        if b.cnt[s] > 0:
            b.emit("gpsimd",
                   lambda e, h=b.sems[s], v=b.cnt[s]: e.wait_ge(h, v))
    b.emit("gpsimd", lambda e, h=b.sems["GP"], v=b.cnt["GP"]: e.wait_ge(h, v))

    with nc.Block() as block:
        @block.sync
        def _(eng):
            for f in b.prog["sync"]:
                f(eng)

        @block.tensor
        def _(eng):
            for f in b.prog["tensor"]:
                f(eng)

        @block.vector
        def _(eng):
            for f in b.prog["vector"]:
                f(eng)

        @block.scalar
        def _(eng):
            for f in b.prog["scalar"]:
                f(eng)

        @block.gpsimd
        def _(eng):
            for f in b.prog["gpsimd"]:
                f(eng)

    return nc


_NC_CACHE = {}


def kernel(**inputs):
    if "main" not in _NC_CACHE:
        _NC_CACHE["main"] = build_nc(dbg=False)
    nc = _NC_CACHE["main"]
    f32 = np.float32

    def r2(a, cols):
        return np.ascontiguousarray(
            np.asarray(a, dtype=f32).reshape(cols, 128).T)

    base = {
        "wdec": np.ascontiguousarray(inputs["L0_attn_w_dec_attn"], dtype=f32),
        "wout": np.ascontiguousarray(inputs["L0_attn_w_out"], dtype=f32),
        "bout": r2(inputs["L0_attn_b_out"], 8),
        "wgate": np.ascontiguousarray(inputs["L0_ffn_w_gate"], dtype=f32),
        "wup": np.ascontiguousarray(inputs["L0_ffn_w_up"], dtype=f32),
        "wdown": np.ascontiguousarray(inputs["L0_ffn_w_down"], dtype=f32),
        "Wg": np.ascontiguousarray(inputs["L1_attn_Wg"], dtype=f32),
        "Wv": np.ascontiguousarray(inputs["L1_attn_Wv"], dtype=f32),
        "Wd": np.ascontiguousarray(inputs["L1_attn_Wd"], dtype=f32),
        "bg": r2(inputs["L1_attn_bg"], 8),
        "bv": r2(inputs["L1_attn_bv"], 8),
        "bd": r2(inputs["L1_attn_bd"], 8),
        "rw": np.ascontiguousarray(inputs["L1_ffn_router_w"], dtype=f32),
        "w1": np.ascontiguousarray(inputs["L1_ffn_w1"], dtype=f32),
        "b1": r2(inputs["L1_ffn_b1"], 32),
        "w2": np.ascontiguousarray(inputs["L1_ffn_w2"], dtype=f32),
        "b2": r2(inputs["L1_ffn_b2"], 8),
    }
    x_full = np.asarray(inputs["L0_attn_hidden_states"], dtype=f32)
    in_maps = []
    for bb in range(8):
        m = dict(base)
        m["x"] = np.ascontiguousarray(x_full[bb])
        in_maps.append(m)
    res = run_bass_kernel_spmd(nc, in_maps, core_ids=list(range(8)))
    return np.stack([np.asarray(res.results[bb]["out"]) for bb in range(8)],
                    axis=0)


# version 09
# speedup vs baseline: 1.1805x; 1.0155x over previous
import numpy as np
import concourse.bass as bass
import concourse.mybir as mybir
from concourse.bass_utils import run_bass_kernel_spmd

F32 = mybir.dt.float32
BF16 = mybir.dt.bfloat16
I32 = mybir.dt.int32
ALU = mybir.AluOpType
ACT = mybir.ActivationFunctionType
AX = mybir.AxisListType

D = 1024
S = 4096
DC = 8
NF = 3072
F1 = 4096
NEG = -1.0e30
EPS = 1e-5
QA = 32767.0
QW = 7.0
MAGIC = 12582912.0

ENGINES = ("sync", "tensor", "vector", "scalar", "gpsimd")


class Builder:
    def __init__(self, nc):
        self.nc = nc
        self.prog = {e: [] for e in ENGINES}
        self.sems = {}
        self.cnt = {}
        for k in ("LD", "GP", "TE", "VE", "SE", "CC"):
            self.sems[k] = nc.alloc_semaphore(name="sem" + k)
            self.cnt[k] = 0
        self.order_owner = {"TE": "tensor", "VE": "vector", "SE": "scalar"}
        self.lastw = {e: {} for e in ENGINES}
        self.free = {}
        self.bar = []

    def barrier(self):
        self.bar = [(s, self.cnt[s]) for s in self.cnt]
        self.free = {}

    def F(self, key):
        return self.free.get(key, self.bar)

    def setF(self, key, tok):
        self.free[key] = tok

    def emit(self, eng, fn):
        self.prog[eng].append(fn)

    def tok(self, s):
        return (s, self.cnt[s])

    def all_dma_tok(self):
        return [(s, self.cnt[s]) for s in self.cnt
                if s not in ("TE", "VE", "SE", "CC", "LD")]

    def bump(self, s, n=1):
        self.cnt[s] += n
        return (s, self.cnt[s])

    def _wait1(self, eng, tok):
        s, v = tok
        if v <= 0:
            return
        if self.order_owner.get(s) == eng:
            return
        if self.lastw[eng].get(s, 0) >= v:
            return
        self.lastw[eng][s] = v
        h = self.sems[s]
        self.emit(eng, lambda e, h=h, v=v: e.wait_ge(h, v))

    def waits(self, eng, toks):
        stack = [toks]
        while stack:
            t = stack.pop()
            if t is None:
                continue
            if isinstance(t, list):
                stack.extend(t)
            else:
                self._wait1(eng, t)

    def op(self, eng, sem, fn, deps=(), inc=True):
        self.waits(eng, list(deps))
        if inc or eng in ("vector", "scalar", "gpsimd"):
            t = self.bump(sem)
            h = self.sems[sem]
            self.emit(eng, lambda e, fn=fn, h=h: fn(e).then_inc(h, 1))
            if eng in ("vector", "scalar", "gpsimd"):
                n = t[1]
                self.lastw[eng][sem] = n
                self.emit(eng, lambda e, h=h, n=n: e.wait_ge(h, n))
            return t
        self.emit(eng, lambda e, fn=fn: fn(e))
        return self.tok(sem)

    def t(self, fn, deps=(), inc=True):
        return self.op("tensor", "TE", fn, deps, inc)

    def v(self, fn, deps=(), inc=True):
        return self.op("vector", "VE", fn, deps, inc)

    def s(self, fn, deps=(), inc=True):
        return self.op("scalar", "SE", fn, deps, inc)

    def _dma(self, eng, slot, out_ap, in_ap, deps, group):
        if slot not in self.sems:
            self.sems[slot] = self.nc.alloc_semaphore(name=slot)
            self.cnt[slot] = 0
        dd = list(deps)
        if not group:
            dd.append(self.tok(slot))
        self.waits(eng, dd)
        t = self.bump(slot, 16)
        h = self.sems[slot]
        self.emit(eng, lambda e, o=out_ap, i=in_ap, h=h:
                  e.dma_start(out=o, in_=i).then_inc(h, 16))
        return t

    def ld(self, out_ap, in_ap, deps=(), slot="Lgen", group=False):
        return self._dma("sync", slot, out_ap, in_ap, deps, group)

    def sp(self, out_ap, in_ap, deps=(), slot="Sgen", group=False):
        return self._dma("gpsimd", slot, out_ap, in_ap, deps, group)

    def g(self, fn, deps=(), inc=True):
        return self.op("gpsimd", "GP", fn, deps, inc)


def build_nc(dbg=False, upto=99):
    nc = bass.Bass("TRN2", target_bir_lowering=False)
    b = Builder(nc)
    scratch_kind = "ExternalOutput" if dbg else "Internal"

    d_x = nc.dram_tensor("x", [S, D], F32, kind="ExternalInput").ap()
    d_wdec = nc.dram_tensor("wdec", [D, D], F32, kind="ExternalInput").ap()
    d_wout = nc.dram_tensor("wout", [D, 2 * D], F32, kind="ExternalInput").ap()
    d_bout = nc.dram_tensor("bout", [128, 8], F32, kind="ExternalInput").ap()
    d_wgate = nc.dram_tensor("wgate", [NF, D], F32, kind="ExternalInput").ap()
    d_wup = nc.dram_tensor("wup", [NF, D], F32, kind="ExternalInput").ap()
    d_wdown = nc.dram_tensor("wdown", [D, NF], F32, kind="ExternalInput").ap()
    d_Wg = nc.dram_tensor("Wg", [D, D], F32, kind="ExternalInput").ap()
    d_Wv = nc.dram_tensor("Wv", [D, D], F32, kind="ExternalInput").ap()
    d_Wd = nc.dram_tensor("Wd", [D, D], F32, kind="ExternalInput").ap()
    d_bg = nc.dram_tensor("bg", [128, 8], F32, kind="ExternalInput").ap()
    d_bv = nc.dram_tensor("bv", [128, 8], F32, kind="ExternalInput").ap()
    d_bd = nc.dram_tensor("bd", [128, 8], F32, kind="ExternalInput").ap()
    d_rw = nc.dram_tensor("rw", [D, 8], F32, kind="ExternalInput").ap()
    d_w1 = nc.dram_tensor("w1", [D, F1], F32, kind="ExternalInput").ap()
    d_b1 = nc.dram_tensor("b1", [128, 32], F32, kind="ExternalInput").ap()
    d_w2 = nc.dram_tensor("w2", [F1, D], F32, kind="ExternalInput").ap()
    d_b2 = nc.dram_tensor("b2", [128, 8], F32, kind="ExternalInput").ap()
    d_out = nc.dram_tensor("out", [S, D], F32, kind="ExternalOutput").ap()

    def scr(name, shape, dt=F32):
        return nc.dram_tensor(name, shape, dt, kind=scratch_kind).ap()

    d_xntm = scr("s_xntm", [S, D])
    d_xfm = scr("s_xfm", [D, S])
    d_q = scr("s_q", [DC, 128, S])
    d_ctx = scr("s_ctx", [D, S])
    d_satt = scr("s_satt", [D, S])
    d_up = scr("s_up", [NF, S])
    d_gp = scr("s_gp", [NF, S])
    d_m = scr("s_m", [NF, S])
    d_mm = scr("s_mm", [NF, S])
    d_x1 = scr("s_x1", [D, S])
    d_s1 = scr("s_s1", [D, S])
    d_wupT = scr("s_wupT", [24, 8, 128, 256], BF16)
    d_wgateT = scr("s_wgateT", [24, 8, 128, 256], BF16)
    d_wdownT = scr("s_wdownT", [8, 24, 128, 256], BF16)
    d_woutT = scr("s_woutT", [8, 16, 128, 128])
    d_WgT = scr("s_WgT", [8, 8, 128, 256], BF16)
    d_WvT = scr("s_WvT", [8, 8, 128, 256], BF16)
    d_WdT = scr("s_WdT", [8, 8, 128, 256], BF16)
    d_w1h = scr("s_w1h", [D, F1], BF16)
    d_w1l = scr("s_w1l", [D, F1], BF16)
    d_w2h = scr("s_w2h", [F1, D], BF16)
    d_w2l = scr("s_w2l", [F1, D], BF16)
    d_cc_in = [nc.dram_tensor(f"cc_in{i}", [1, 8], F32) for i in range(4)]
    d_cc_out = [nc.dram_tensor(f"cc_out{i}", [1, 8], F32) for i in range(4)]

    sb = lambda n, shp, dt=F32: nc.alloc_sbuf_tensor(n, shp, dt)
    bigA = sb("bigA", [128, DC * S])
    bigB = sb("bigB", [128, 8192])
    tbufA = sb("tbufA", [128, 1024])
    tbufB = sb("tbufB", [128, 1024])
    qa = sb("qa", [128, 1024])
    ctxtm = sb("ctxtm", [128, 1024])
    stage0 = sb("stage0", [128, 512])
    stage1 = sb("stage1", [128, 512])
    wsA = sb("wsA", [128, 2048])
    wsB = sb("wsB", [128, 2048])
    maskT = sb("maskT", [128, 128])
    ident = sb("ident", [128, 128])
    ptA = sb("ptA", [128, 128])
    ptB = sb("ptB", [128, 128])
    rw_bf = sb("rw_bf", [128, 128], BF16)
    sm = sb("sm", [128, 448])
    rowb = sb("rowb", [1, 1536])

    SC = sm[:, 0:16]
    s0acc = sm[:, 16:17]; upacc = sm[:, 17:18]; gacc = sm[:, 18:19]
    macc = sm[:, 19:20]; mmacc = sm[:, 20:21]
    c_sum = sm[:, 21:22]; c_sq = sm[:, 22:23]; c_mean = sm[:, 23:24]
    c_msq = sm[:, 24:25]; c_var = sm[:, 25:26]; c_vstd = sm[:, 26:27]
    c_rstd = sm[:, 27:28]
    rowm = sm[:, 28:29]; nrowm = sm[:, 29:30]; lsum = sm[:, 30:31]
    rlsum = sm[:, 31:32]
    tmp128 = sm[:, 32:33]
    ones128 = sm[:, 34:35]
    eps128 = sm[:, 35:36]
    sb_bout = sm[:, 40:48]; sb_bg = sm[:, 48:56]; sb_bv = sm[:, 56:64]
    sb_bd = sm[:, 64:72]; sb_b2 = sm[:, 72:80]; sb_b1 = sm[:, 80:112]
    sb_rw = sm[:, 112:176]
    probsA = sm[:, 176:184]; probsB = sm[:, 184:192]
    r_m1 = sm[:, 192:193]; r_m2 = sm[:, 193:194]; r_eq = sm[:, 196:204]
    r_q = sm[:, 204:212]; r_den = sm[:, 212:213]; r_rden = sm[:, 213:214]
    r_a1 = sm[:, 214:215]; r_a2 = sm[:, 215:216]
    w_col = sm[:, 240:242]
    dq_sblk = sm[:, 256:352]
    dq_inv = sm[:, 352:448]

    rb_m = rowb[0:1, 128:384]
    rb_msq = rowb[0:1, 384:640]
    rb_m2 = rowb[0:1, 640:896]
    rb_var = rowb[0:1, 640:896]
    rb_vstd = rowb[0:1, 384:640]
    rb_rstd = rowb[0:1, 896:1152]
    rb_wrow = rowb[0:1, 1152:1408]
    ones_1 = rowb[0:1, 1408:1536]

    ps0 = nc.alloc_psum_tensor("ps0", [128, 512], F32)
    ps1 = nc.alloc_psum_tensor("ps1", [128, 512], F32)
    ps2 = nc.alloc_psum_tensor("ps2", [128, 512], F32)
    pc0 = nc.alloc_psum_tensor("pc0", [128, 512], F32)
    pc1 = nc.alloc_psum_tensor("pc1", [128, 512], F32)
    ptp = nc.alloc_psum_tensor("ptp", [128, 512], F32)
    ptp2 = nc.alloc_psum_tensor("ptp2", [128, 512], F32)
    pst = nc.alloc_psum_tensor("pst", [128, 512], F32)
    psp = pc0

    trk = {"n": 0}

    def do_transpose(src_ap, dst_p, dst_f, ident_ap, evac_cb, deps=()):
        k = trk["n"]
        trk["n"] += 1
        bank = ptp if k % 2 == 0 else ptp2
        dst = bank[0:dst_p, 0:dst_f]
        tt = b.t(lambda t, dst=dst, src_ap=src_ap, ident_ap=ident_ap:
                 t.transpose(dst, src_ap, ident_ap),
                 deps=list(deps) + [b.F(("trs", k % 2))])
        ev = evac_cb(dst, tt)
        b.setF(("trs", k % 2), ev)
        return ev

    bigA_bf = bigA[:].bitcast(BF16)
    bigB_bf = bigB[:].bitcast(BF16)
    wsA_bf = wsA[:].bitcast(BF16)
    wsB_bf = wsB[:].bitcast(BF16)

    def xn1_hi(kc, t0, w):
        return bigA_bf[:, 2 * kc * S + t0: 2 * kc * S + t0 + w]

    def xn1_lo(kc, t0, w):
        return bigA_bf[:, 2 * kc * S + S + t0: 2 * kc * S + S + t0 + w]

    for dst, dsrc in ((sb_bout, d_bout), (sb_bg, d_bg), (sb_bv, d_bv),
                      (sb_bd, d_bd), (sb_b1, d_b1), (sb_b2, d_b2)):
        b.ld(dst, dsrc, slot="Lsmall", group=True)
    tsl = b.ld(sb_rw.rearrange("p (c m) -> p c m", m=8),
               d_rw.rearrange("(c p) m -> p c m", p=128),
               slot="Lsmall", group=True)

    tld0 = b.tok("LD")
    b.v(lambda v: v.memset(ones128, 1.0))
    b.v(lambda v: v.memset(eps128, EPS))
    b.v(lambda v: v.memset(rowb[0:1, 32:33], EPS))
    b.v(lambda v: v.memset(ones_1, 1.0))
    b.v(lambda v: v.memset(rowb[0:1, 0:32], 0.0))
    b.v(lambda v: v.memset(stage0[:, 0:128], 0.0))
    b.v(lambda v: v.memset(stage1[:, 0:128], 1.0))
    b.v(lambda v: v.memset(s0acc, 0.0))
    b.v(lambda v: v.memset(upacc, 0.0))
    b.v(lambda v: v.memset(gacc, 0.0))
    b.v(lambda v: v.memset(macc, 0.0))
    tv0 = b.v(lambda v: v.memset(mmacc, 0.0))
    b.g(lambda g: g.affine_select(
        maskT[:], stage0[:, 0:128], [[-1, 128]], ALU.is_gt, NEG,
        base=0, channel_multiplier=1), deps=[tv0])
    tg_id = b.g(lambda g: g.affine_select(
        ident[:], stage1[:, 0:128], [[-1, 128]], ALU.is_equal, 0.0,
        base=0, channel_multiplier=1))
    tg_const = b.g(lambda g: g.engine_nop())

    if upto < 0.3:
        return finish(nc, b)

    b.barrier()

    def dequant_chunk(w_sb, n_in, out_sb, scr_sb, dep_ld, extra_out_dep):
        nblk = n_in // 32
        b.v(lambda v: v.tensor_reduce(
            dq_sblk[:, 0:nblk],
            w_sb.rearrange("p (a bk) -> p a bk", bk=32),
            axis=AX.X, op=ALU.max, apply_absolute_value=True),
            deps=[dep_ld])
        b.v(lambda v: v.tensor_scalar(
            dq_sblk[:, 0:nblk], dq_sblk[:, 0:nblk], 1.0 / QW, 1e-12,
            ALU.mult, ALU.max))
        b.v(lambda v: v.reciprocal(dq_inv[:, 0:nblk], dq_sblk[:, 0:nblk]))
        last = None
        for h in range(n_in // 512):
            sl = slice(h * 512, (h + 1) * 512)
            bsl = slice(h * 16, h * 16 + 16)
            b.v(lambda v, sl=sl, bsl=bsl: v.tensor_tensor(
                scr_sb,
                w_sb[:, sl].rearrange("p (a bk) -> p a bk", bk=32),
                dq_inv[:, bsl].unsqueeze(2).broadcast_to([128, 16, 32]),
                ALU.mult))
            b.v(lambda v: v.tensor_scalar(
                scr_sb, scr_sb, MAGIC, MAGIC, ALU.add, ALU.subtract))
            b.v(lambda v: v.tensor_scalar(
                scr_sb, scr_sb, -(QW + 1.0), QW, ALU.max, ALU.min))
            last = b.v(lambda v, sl=sl, bsl=bsl: v.tensor_tensor(
                out_sb[:, sl],
                scr_sb.rearrange("p (a bk) -> p a bk", bk=32),
                dq_sblk[:, bsl].unsqueeze(2).broadcast_to([128, 16, 32]),
                ALU.mult), deps=[extra_out_dep])
        return last

    def transpose_chunk_to_dram(src_sb, n_in, dram_slot_fn, dep, bf=False):
        for kc in range(n_in // 128):
            slot = kc % 2
            stg = stage1 if slot else stage0

            def ev(dst, tt, kc=kc, slot=slot, stg=stg):
                if not bf:
                    tv = b.v(lambda v: v.tensor_copy(stg[:, 0:128], dst),
                             deps=[tt, b.F(("stP", slot))])
                    tg = b.sp(dram_slot_fn(kc), stg[:, 0:128], deps=[tv],
                              slot=f"Sstg{slot}")
                    b.setF(("stP", slot), tg)
                    return tv
                sbf = stg[:].bitcast(BF16)
                tv = b.v(lambda v: v.tensor_copy(stg[:, 128:256], dst),
                         deps=[tt, b.F(("stP", slot))])
                b.v(lambda v: v.tensor_copy(sbf[:, 0:128], stg[:, 128:256]))
                tv = b.v(lambda v: v.tensor_tensor(
                    sbf[:, 128:256], stg[:, 128:256], sbf[:, 0:128],
                    ALU.subtract))
                tg = b.sp(dram_slot_fn(kc), sbf[:, 0:256], deps=[tv],
                          slot=f"Sstg{slot}")
                b.setF(("stP", slot), tg)
                return tv
            do_transpose(src_sb[:, kc * 128:(kc + 1) * 128], 128, 128,
                         ident[:], ev, deps=[dep, tg_id])
        return b.tok("TE")

    def gen_wsA_items():
        for w_d, dT in ((d_wup, d_wupT), (d_wgate, d_wgateT)):
            for oc in range(24):
                def item(w_d=w_d, dT=dT, oc=oc):
                    t_w = b.ld(wsA[:, 0:1024],
                               w_d[oc * 128:(oc + 1) * 128, :],
                               deps=[b.F("wsA_P")], slot="Lws0")
                    tdq = dequant_chunk(wsA[:, 0:1024], 1024,
                                        bigB[:, 7168:8192],
                                        bigB[:, 5632:6144], t_w,
                                        b.F("dqo_P"))
                    b.setF("wsA_P", tdq)
                    tte = transpose_chunk_to_dram(
                        bigB[:, 7168:8192], 1024,
                        lambda kc, dT=dT, oc=oc: dT[oc, kc], tdq, bf=True)
                    b.setF("dqo_P", tte)
                yield item
        for r in range(8):
            def item(r=r):
                t_w = b.ld(wsA[:, 0:2048], d_wout[r * 128:(r + 1) * 128, :],
                           deps=[b.F("wsA_P")], slot="Lws0")
                tte = transpose_chunk_to_dram(
                    wsA[:, 0:2048], 2048,
                    lambda kc, r=r: d_woutT[r, kc], t_w)
                b.setF("wsA_P", tte)
            yield item
        for w_d, dT in ((d_Wg, d_WgT), (d_Wv, d_WvT), (d_Wd, d_WdT)):
            for r in range(8):
                def item(w_d=w_d, dT=dT, r=r):
                    t_w = b.ld(wsA[:, 0:1024],
                               w_d[r * 128:(r + 1) * 128, :],
                               deps=[b.F("wsA_P")], slot="Lws0")
                    tte = transpose_chunk_to_dram(
                        wsA[:, 0:1024], 1024,
                        lambda dc, dT=dT, r=r: dT[r, dc], t_w, bf=True)
                    b.setF("wsA_P", tte)
                yield item

    def gen_bigB_items():
        for r in range(8):
            def item(r=r):
                t_w = b.ld(bigB[:, 0:3072], d_wdown[r * 128:(r + 1) * 128, :],
                           deps=[b.F("bigB_P")], slot="LbigB")
                tdq = dequant_chunk(bigB[:, 0:3072], 3072,
                                    bigB[:, 3072:6144],
                                    bigB[:, 7168:7680], t_w, b.F("bigB_P2"))
                b.setF("bigB_P", tdq)
                tte = transpose_chunk_to_dram(
                    bigB[:, 3072:6144], 3072,
                    lambda ic, r=r: d_wdownT[r, ic], tdq, bf=True)
                b.setF("bigB_P2", tte)
            yield item
        for name, d_src, d_h, d_l, rows, cols in (
                ("w1", d_w1, d_w1h, d_w1l, D, F1),
                ("w2", d_w2, d_w2h, d_w2l, F1, D)):
            for r in range(rows // 128):
                def item(d_src=d_src, d_h=d_h, d_l=d_l, r=r, cols=cols):
                    t_w = b.ld(bigB[:, 0:cols],
                               d_src[r * 128:(r + 1) * 128, :],
                               deps=[b.F("bigB_P"), b.F("bigB_P2")],
                               slot="LbigB")
                    b.v(lambda v: v.tensor_copy(
                        bigB_bf[:, 8192:8192 + cols], bigB[:, 0:cols]),
                        deps=[t_w])
                    tvl = b.v(lambda v: v.tensor_tensor(
                        bigB_bf[:, 8192 + cols:8192 + 2 * cols],
                        bigB[:, 0:cols],
                        bigB_bf[:, 8192:8192 + cols], ALU.subtract))
                    th = b.sp(d_h[r * 128:(r + 1) * 128, :],
                              bigB_bf[:, 8192:8192 + cols], deps=[tvl],
                              slot=f"Scv{r % 2}")
                    tl = b.sp(d_l[r * 128:(r + 1) * 128, :],
                              bigB_bf[:, 8192 + cols:8192 + 2 * cols],
                              deps=[tvl], slot=f"Scv{2 + r % 2}")
                    b.setF("bigB_P", [th, tl])
                    b.setF("bigB_P2", [th, tl])
                yield item

    gA = gen_wsA_items() if upto >= 2 else iter(())
    gB = gen_bigB_items() if upto >= 2 else iter(())

    TI = S // 128
    if upto < 0.7:
        TI = 2
    xtm = [tbufA, tbufB]
    xnt = [qa, ctxtm]
    for ti in range(TI):
        bi = ti % 2
        t_x = b.ld(xtm[bi][:], d_x[ti * 128:(ti + 1) * 128, :],
                   deps=[b.F(("xtm", bi))], slot=f"Lxtm{bi}")
        tvs = b.v(lambda v, x=xtm[bi]: v.reduce_sum(c_sum, x[:], axis=AX.X),
                  deps=[t_x])
        ts = b.s(lambda s, x=xtm[bi]: s.activation(
            bigB[:, 6144:7168], x[:], ACT.Square, accum_out=c_sq),
            deps=[t_x, b.F("sq_ph0")])
        b.setF("sq_ph0", ts)
        b.v(lambda v: v.tensor_scalar(c_mean, c_sum, 1.0 / D, None, ALU.mult))
        b.v(lambda v: v.tensor_scalar(c_msq, c_sq, 1.0 / D, None, ALU.mult),
            deps=[ts])
        b.v(lambda v: v.tensor_tensor(tmp128, c_mean, c_mean, ALU.mult))
        tvv = b.v(lambda v: v.tensor_tensor(c_var, c_msq, tmp128,
                                            ALU.subtract))
        tsv = b.s(lambda s: s.activation(c_vstd, c_var, ACT.Sqrt, bias=eps128),
                  deps=[tvv])
        b.v(lambda v: v.reciprocal(c_rstd, c_vstd), deps=[tsv])
        b.v(lambda v: v.engine_nop())
        b.v(lambda v: v.engine_nop())
        tv_xn = b.v(lambda v, x=xtm[bi], o=xnt[bi]: v.tensor_scalar(
            o[:], x[:], c_mean, c_rstd, ALU.subtract, ALU.mult),
            deps=[b.F(("xnt", bi))])
        b.v(lambda v, o=xnt[bi]: v.tensor_reduce(
            tmp128, o[:], axis=AX.X, op=ALU.max, apply_absolute_value=True))
        b.v(lambda v: v.tensor_max(s0acc, s0acc, tmp128))
        tg1 = b.sp(d_xntm[ti * 128:(ti + 1) * 128, :], xnt[bi][:],
                   deps=[tv_xn], slot=f"Sxnt{bi}")
        for dc in range(DC):
            def ev1(dst, tt, dc=dc, ti=ti):
                return b.v(lambda v: v.tensor_copy(
                    bigA[:, dc * S + ti * 128: dc * S + ti * 128 + 128], dst),
                    deps=[tt])
            do_transpose(xnt[bi][:, dc * 128:(dc + 1) * 128], 128, 128,
                         ident[:], ev1, deps=[tv_xn, tg_id])
        for dc in range(DC):
            def ev2(dst, tt, dc=dc, ti=ti):
                tv2 = b.v(lambda v: v.tensor_copy(
                    wsB[:, dc * 128:(dc + 1) * 128], dst),
                    deps=[tt, b.F(("wsB_ph0", dc))])
                tg2 = b.sp(d_xfm[dc * 128:(dc + 1) * 128,
                                 ti * 128:(ti + 1) * 128],
                           wsB[:, dc * 128:(dc + 1) * 128], deps=[tv2],
                           slot=f"Sxfm{dc % 4}")
                b.setF(("wsB_ph0", dc), tg2)
                return tv2
            do_transpose(xtm[bi][:, dc * 128:(dc + 1) * 128], 128, 128,
                         ident[:], ev2, deps=[t_x, tg_id])
        b.setF(("xtm", bi), [b.tok("VE"), b.tok("SE"), b.tok("TE")])
        b.setF(("xnt", bi), [b.tok("VE"), b.tok("TE"), tg1])
        for _ in range(3):
            it = next(gA, None)
            if it is not None:
                it()
    for it in gA:
        it()
    tok_xn_done = b.tok("VE")
    tok_xn_te = b.tok("TE")
    tok_xn_sp = b.all_dma_tok()
    if upto < 0.9:
        return finish(nc, b)

    for ec in range(DC):
        bi = ec % 2
        ws = wsA if bi == 0 else wsB
        t_w = b.ld(ws[:, 0:1024].rearrange("p (c m) -> p c m", m=128),
                   d_wdec[:, ec * 128:(ec + 1) * 128].rearrange(
                       "(c p) m -> p c m", p=128),
                   deps=[b.F(("qw", bi)), tok_xn_sp, tok_xn_te, b.F("wsA_P")],
                   slot=f"Lws{bi}")
        for tt in range(8):
            pp = ps0 if tt % 2 == 0 else ps1
            mm_tok = None
            for kc in range(DC):
                mm_tok = b.t(lambda t, ws=ws, kc=kc, tt=tt, pp=pp: t.matmul(
                    pp[:], ws[:, kc * 128:(kc + 1) * 128],
                    bigA[:, kc * S + tt * 512: kc * S + tt * 512 + 512],
                    start=(kc == 0), stop=(kc == DC - 1)),
                    deps=[t_w, tok_xn_done, b.F(("psq", tt % 2))],
                    inc=(kc == DC - 1))
            st = stage0 if tt % 2 == 0 else stage1
            tv = b.v(lambda v, pp=pp, st=st: v.tensor_copy(st[:], pp[:]),
                     deps=[mm_tok, b.F(("stP", tt % 2))])
            b.setF(("psq", tt % 2), tv)
            tg = b.sp(d_q[ec, :, tt * 512:(tt + 1) * 512], st[:], deps=[tv],
                      slot=f"Sstg{tt % 2}")
            b.setF(("stP", tt % 2), tg)
        b.setF(("qw", bi), b.tok("TE"))
        for _ in range(6):
            it = next(gB, None)
            if it is not None:
                it()
    for it in gB:
        it()
    if upto < 2:
        return finish(nc, b)
    b.v(lambda v: v.tensor_copy(rw_bf[:, 0:64], sb_rw), deps=[tsl])
    b.v(lambda v: v.tensor_tensor(rw_bf[:, 64:128], sb_rw, rw_bf[:, 0:64],
                                  ALU.subtract))
    tok_wprep = b.all_dma_tok()

    cc_idx = [0]

    def allreduce_max(dep_tok):
        i = cc_idx[0]
        cc_idx[0] += 1
        tg1 = b.sp(d_cc_in[i].ap(), rowb[0:1, 0:8], deps=[dep_tok],
                   slot="Sccin")
        tcc = b.op("gpsimd", "CC", lambda g, i=i: g.collective_compute(
            "AllReduce", ALU.max, replica_groups=[list(range(8))],
            ins=[d_cc_in[i].ap().opt()], outs=[d_cc_out[i].ap().opt()]),
            deps=[tg1])
        return b.sp(rowb[0:1, 8:16], d_cc_out[i].ap(), deps=[tcc],
                    slot="Sccout")

    def finalize_partial_max(acc_ap, dst_col, dep):
        def ev(dst, tt, c=dst_col):
            return b.v(lambda v: v.reduce_max(
                rowb[0:1, c:c + 1], dst, axis=AX.X), deps=[tt])
        return do_transpose(acc_ap, 1, 128, ident[:], ev, deps=[dep, tg_id])

    def bcast_scalars(cols_src, n, dst_sc_col, dep):
        tt = b.t(lambda t, n=n: t.matmul(
            psp[:, 0:n], ones_1, rowb[0:1, cols_src:cols_src + n],
            start=True, stop=True), deps=[dep, b.F("pspR")])
        tv = b.v(lambda v, n=n, c=dst_sc_col: v.tensor_copy(
            SC[:, c:c + n], psp[:, 0:n]), deps=[tt])
        b.setF("pspR", tv)
        b.v(lambda v: v.engine_nop())
        b.v(lambda v: v.engine_nop())
        return b.tok("VE")

    if upto < 3:
        return finish(nc, b)
    tfin = finalize_partial_max(s0acc, 0, tok_xn_done)
    tcc1 = allreduce_max(tfin)
    tv = b.v(lambda v: v.tensor_scalar(
        rowb[0:1, 16:17], rowb[0:1, 8:9], 1.0 / QA, 1e-12, ALU.mult, ALU.max),
        deps=[tcc1])
    tv = b.v(lambda v: v.reciprocal(rowb[0:1, 17:18], rowb[0:1, 16:17]))
    tok_r1 = bcast_scalars(16, 2, 0, tv)

    if upto < 4:
        return finish(nc, b)
    b.barrier()
    raw = bigB
    pt_sb = [ptA, ptB]

    def a_scores(i):
        JT = i // 4 + 1
        k = i % 4
        t_q = b.ld(qa[:].rearrange("p (c m) -> p c m", m=128),
                   d_q[:, :, i * 128:(i + 1) * 128].rearrange(
                       "c p m -> p c m"), deps=[b.F("qa_A"), tok_xn_sp],
                   slot="Lqa")
        for jt in range(JT):
            pp = ps0 if jt % 2 == 0 else ps1
            mm_tok = None
            for kc in range(DC):
                mm_tok = b.t(lambda t, kc=kc, jt=jt, pp=pp: t.matmul(
                    pp[:], qa[:, kc * 128:(kc + 1) * 128],
                    bigA[:, kc * S + jt * 512: kc * S + jt * 512 + 512],
                    start=(kc == 0), stop=(kc == DC - 1)),
                    deps=[t_q, b.F(("psA", jt % 2))],
                    inc=(kc == DC - 1))
            if jt < JT - 1:
                tv = b.v(lambda v, jt=jt, pp=pp: v.tensor_copy(
                    raw[:, jt * 512:(jt + 1) * 512], pp[:]),
                    deps=[mm_tok, b.F("rawP")])
            else:
                base = jt * 512
                if k > 0:
                    b.v(lambda v, pp=pp, base=base, k=k: v.tensor_copy(
                        raw[:, base:base + k * 128], pp[:, 0:k * 128]),
                        deps=[mm_tok, b.F("rawP"), tg_const])
                tv = b.v(lambda v, pp=pp, base=base, k=k:
                         v.tensor_tensor(
                             raw[:, base + k * 128:base + (k + 1) * 128],
                             pp[:, k * 128:(k + 1) * 128], maskT[:], ALU.add),
                         deps=[mm_tok, b.F("rawP"), tg_const])
                if i == 0:
                    tv = b.v(lambda v, pp=pp: v.tensor_copy(
                        raw[0:1, 0:1], pp[0:1, 0:1]))
                if k < 3:
                    tv = b.v(lambda v, base=base, k=k: v.memset(
                        raw[:, base + (k + 1) * 128:base + 512], NEG))
            b.setF(("psA", jt % 2), tv)
        b.setF("qa_A", b.tok("TE"))

    def a_softmax(i):
        W = (i // 4 + 1) * 512
        b.v(lambda v, W=W: v.reduce_max(rowm, raw[:, 0:W], axis=AX.X))
        b.v(lambda v: v.tensor_scalar(nrowm, rowm, -1.0, None, ALU.mult))
        b.v(lambda v: v.engine_nop())
        tvn = b.v(lambda v: v.engine_nop())
        tse = b.s(lambda s, W=W: s.activation(
            bigB[:, 4096:4096 + W], raw[:, 0:W], ACT.Exp, bias=nrowm,
            accum_out=lsum), deps=[tvn, b.F("Pexp")])
        tv_rl = b.v(lambda v: v.reciprocal(rlsum, lsum), deps=[tse])
        b.v(lambda v: v.engine_nop())
        b.v(lambda v: v.engine_nop())
        b.setF("rawP", tse)
        return tse, tv_rl

    def a_ctx(i, tse, tv_rl):
        tm1 = None
        kbufs = (tbufA[:, 0:1024], tbufB[:, 0:1024],
                 wsA[:, 0:1024], wsB[:, 0:1024])
        for jc in range(i + 1):
            sl = jc % 2
            k4 = jc % 4
            xb = kbufs[k4]
            t_k = b.ld(xb, d_xntm[jc * 128:(jc + 1) * 128, :],
                       deps=[b.F(("xbA", k4)), tok_xn_sp],
                       slot=f"Ltb{k4}")

            def evP(dst, tt, sl=sl):
                return b.v(lambda v: v.tensor_copy(pt_sb[sl][:], dst),
                           deps=[tt, b.F(("ptsbA", sl))])
            tv = do_transpose(
                bigB[:, 4096 + jc * 128:4096 + (jc + 1) * 128], 128, 128,
                ident[:], evP, deps=[tse])
            tm0 = b.t(lambda t, sl=sl, jc=jc, i=i, xb=xb: t.matmul(
                pc0[:], pt_sb[sl][:], xb[:, 0:512],
                start=(jc == 0), stop=(jc == i)),
                deps=[tv, t_k] + ([b.F("pcFree")] if jc == 0 else []),
                inc=(jc == i))
            tm1 = b.t(lambda t, sl=sl, jc=jc, i=i, xb=xb: t.matmul(
                pc1[:], pt_sb[sl][:], xb[:, 512:1024],
                start=(jc == 0), stop=(jc == i)),
                deps=[], inc=True)
            b.setF(("ptsbA", sl), tm1)
            b.setF(("xbA", k4), tm1)
        tv0 = b.v(lambda v: v.tensor_scalar(
            ctxtm[:, 0:512], pc0[:], rlsum, None, ALU.mult),
            deps=[tm1, tv_rl, b.F("ctxtmA")])
        tv1 = b.v(lambda v: v.tensor_scalar(
            ctxtm[:, 512:1024], pc1[:], rlsum, None, ALU.mult))
        b.setF("pcFree", tv1)
        for dc in range(DC):
            stg = stage0 if dc % 2 == 0 else stage1

            def evC(dst, tt, dc=dc, stg=stg, i=i):
                tv = b.v(lambda v: v.tensor_copy(stg[:, 0:128], dst),
                         deps=[tt, b.F(("stA", dc % 2))])
                tg = b.sp(d_ctx[dc * 128:(dc + 1) * 128,
                                i * 128:(i + 1) * 128],
                          stg[:, 0:128], deps=[tv], slot=f"Sstg{dc % 2}")
                b.setF(("stA", dc % 2), tg)
                return tv
            do_transpose(ctxtm[:, dc * 128:(dc + 1) * 128], 128, 128,
                         ident[:], evC, deps=[tv1])
        b.setF("ctxtmA", b.tok("TE"))
        b.setF("Pexp", b.tok("TE"))

    pend = None
    for i in range(TI):
        a_scores(i)
        if pend is not None:
            a_ctx(*pend)
        tse, tv_rl = a_softmax(i)
        pend = (i, tse, tv_rl)
    a_ctx(*pend)
    tok_attn = b.all_dma_tok()

    if upto < 5:
        return finish(nc, b)
    b.barrier()
    for tt in range(8):
        ctx_toks = []
        for kc8 in range(8):
            ctx_toks.append(b.ld(
                bigB[:, kc8 * 512:(kc8 + 1) * 512],
                d_ctx[kc8 * 128:(kc8 + 1) * 128, tt * 512:(tt + 1) * 512],
                deps=[b.F("bigB_A2"), tok_attn], slot="LbigB", group=True))
        t_ctx = ctx_toks[-1]
        for oc in range(8):
            sl = oc % 2
            ws = wsA if sl == 0 else wsB
            tw0 = b.ld(ws[:, 0:1024].rearrange("p (a m) -> p a m", m=128),
                       d_woutT[oc, 0:8].rearrange("a p m -> p a m"),
                       deps=[b.F(("wsA2", sl)), tok_wprep],
                       slot=f"Lws{sl}")
            tw1 = b.ld(ws[:, 1024:2048].rearrange("p (a m) -> p a m", m=128),
                       d_woutT[oc, 8:16].rearrange("a p m -> p a m"),
                       slot=f"Lws{sl}")
            tw0 = tw1
            pp = ps0 if sl == 0 else ps1
            mm_tok = None
            for kc in range(16):
                if kc < 8:
                    rhs = bigA[:, kc * S + tt * 512: kc * S + tt * 512 + 512]
                    dep = [tw0]
                else:
                    rhs = bigB[:, (kc - 8) * 512:(kc - 7) * 512]
                    dep = [tw1, t_ctx]
                mm_tok = b.t(lambda t, kc=kc, pp=pp, ws=ws, rhs=rhs: t.matmul(
                    pp[:], ws[:, kc * 128:(kc + 1) * 128], rhs,
                    start=(kc == 0), stop=(kc == 15)),
                    deps=dep + [b.F(("psA2", sl))], inc=(kc == 15))
            stg = stage0 if sl == 0 else stage1
            tsv = b.s(lambda s, pp=pp, stg=stg, oc=oc: s.activation(
                stg[:], pp[:], ACT.Tanh, bias=sb_bout[:, oc:oc + 1]),
                deps=[mm_tok, tsl, b.F(("stA2", sl))])
            b.setF(("psA2", sl), tsv)
            tg = b.sp(d_satt[oc * 128:(oc + 1) * 128, tt * 512:(tt + 1) * 512],
                      stg[:], deps=[tsv], slot=f"Sstg{sl}")
            b.setF(("stA2", sl), tg)
            b.setF(("wsA2", sl), b.tok("TE"))
        b.setF("bigB_A2", b.tok("TE"))
    tok_a2 = b.all_dma_tok()

    if upto < 6:
        return finish(nc, b)
    b.barrier()
    qa_bf = qa[:].bitcast(BF16)
    ctx_bf = ctxtm[:].bitcast(BF16)
    tbA_bf = tbufA[:].bitcast(BF16)
    tv_xq = None
    for kc in range(DC):
        for h in range(8):
            sl = slice(kc * S + h * 512, kc * S + h * 512 + 512)
            hsl = slice((h % 4) * 512, (h % 4) * 512 + 512)
            b.v(lambda v, sl=sl: v.tensor_scalar(
                stage0[:], bigA[:, sl], SC[:, 1:2], MAGIC, ALU.mult, ALU.add),
                deps=[tok_r1, b.F("xqscr")])
            b.v(lambda v: v.tensor_scalar(
                stage0[:], stage0[:], MAGIC, -(QA + 1.0),
                ALU.subtract, ALU.max))
            b.v(lambda v: v.tensor_scalar(
                stage1[:], stage0[:], QA, SC[:, 0:1], ALU.min, ALU.mult))
            b.v(lambda v, hsl=hsl: v.tensor_copy(qa_bf[:, hsl], stage1[:]))
            lodst = ctx_bf if h < 4 else tbA_bf
            b.v(lambda v, hsl=hsl, lodst=lodst: v.tensor_tensor(
                lodst[:, hsl], stage1[:], qa_bf[:, hsl], ALU.subtract))
            if h == 3:
                b.v(lambda v, kc=kc: v.tensor_copy(
                    bigA_bf[:, 2 * kc * S: 2 * kc * S + 2048],
                    qa_bf[:, 0:2048]))
        b.v(lambda v, kc=kc: v.tensor_copy(
            bigA_bf[:, 2 * kc * S + 2048: 2 * kc * S + 4096], qa_bf[:, 0:2048]))
        b.v(lambda v, kc=kc: v.tensor_copy(
            bigA_bf[:, 2 * kc * S + S: 2 * kc * S + S + 2048],
            ctx_bf[:, 0:2048]))
        tv_xq = b.v(lambda v, kc=kc: v.tensor_copy(
            bigA_bf[:, 2 * kc * S + S + 2048: 2 * kc * S + S + 4096],
            tbA_bf[:, 0:2048]))
        b.setF("xqscr", None)
    tok_xq = tv_xq

    for w_i, (dT, d_pre, acc) in enumerate(
            ((d_wupT, d_up, upacc), (d_wgateT, d_gp, gacc))):
        for oc in range(24):
            sl = oc % 2
            ws = wsA if sl == 0 else wsB
            wbf = wsA_bf if sl == 0 else wsB_bf
            t_w = b.ld(wbf[:, 0:2048].rearrange("p (a m) -> p a m", m=256),
                       dT[oc].rearrange("a p m -> p a m"),
                       deps=[b.F(("wsB1", sl)), tok_wprep],
                       slot=f"Lws{sl}")
            for tt in range(8):
                pp = ps0 if tt % 2 == 0 else ps1
                mm_tok = None
                for kc in range(DC):
                    wh = wbf[:, kc * 256: kc * 256 + 128]
                    wl = wbf[:, kc * 256 + 128: kc * 256 + 256]
                    rh = bigA_bf[:, 2 * kc * S + tt * 512:
                                 2 * kc * S + tt * 512 + 512]
                    rl_ = bigA_bf[:, 2 * kc * S + S + tt * 512:
                                  2 * kc * S + S + tt * 512 + 512]
                    b.t(lambda t, pp=pp, wh=wh, rh=rh, kc=kc: t.matmul(
                        pp[:], wh, rh, start=(kc == 0), stop=False),
                        deps=[t_w, tok_xq, b.F(("psB1", tt % 2))], inc=False)
                    b.t(lambda t, pp=pp, wh=wh, rl_=rl_: t.matmul(
                        pp[:], wh, rl_, start=False, stop=False), inc=False)
                    mm_tok = b.t(lambda t, pp=pp, wl=wl, rh=rh, kc=kc:
                                 t.matmul(pp[:], wl, rh, start=False,
                                          stop=(kc == DC - 1)),
                                 inc=(kc == DC - 1))
                stg = stage0 if tt % 2 == 0 else stage1
                tv = b.v(lambda v, pp=pp, stg=stg: v.tensor_copy(
                    stg[:], pp[:]), deps=[mm_tok, b.F(("stB1", tt % 2))])
                b.setF(("psB1", tt % 2), tv)
                b.v(lambda v, stg=stg: v.tensor_reduce(
                    tmp128, stg[:], axis=AX.X, op=ALU.max,
                    apply_absolute_value=True))
                b.v(lambda v, acc=acc: v.tensor_max(acc, acc, tmp128))
                tg = b.sp(d_pre[oc * 128:(oc + 1) * 128,
                                tt * 512:(tt + 1) * 512], stg[:], deps=[tv],
                          slot=f"Sstg{tt % 2}")
                b.setF(("stB1", tt % 2), tg)
            b.setF(("wsB1", sl), b.tok("TE"))
    tok_p1v = b.tok("VE")
    b.barrier()

    t1 = finalize_partial_max(upacc, 0, tok_p1v)
    t2 = finalize_partial_max(gacc, 1, t1)
    tcc2 = allreduce_max(t2)
    b.v(lambda v: v.tensor_scalar(
        rowb[0:1, 16:17], rowb[0:1, 8:9], 1.0 / QA, 1e-12, ALU.mult, ALU.max),
        deps=[tcc2])
    b.v(lambda v: v.reciprocal(rowb[0:1, 17:18], rowb[0:1, 16:17]))
    b.v(lambda v: v.tensor_scalar(
        rowb[0:1, 18:19], rowb[0:1, 9:10], 1.0 / QA, 1e-12, ALU.mult,
        ALU.max))
    b.v(lambda v: v.reciprocal(rowb[0:1, 19:20], rowb[0:1, 18:19]))
    tv = b.v(lambda v: v.tensor_scalar(
        rowb[0:1, 20:21], rowb[0:1, 18:19], 1.0 / 65536.0, None, ALU.mult))
    tok_r2 = bcast_scalars(16, 5, 2, tv)

    mb_half = [ctxtm[:, 0:512], ctxtm[:, 512:1024]]
    for oc in range(24):
        for tt in range(8):
            i2 = (oc * 8 + tt) % 2
            gp_buf = stage0 if i2 == 0 else stage1
            t_g = b.ld(gp_buf[:],
                       d_gp[oc * 128:(oc + 1) * 128, tt * 512:(tt + 1) * 512],
                       deps=[b.F(("gpB2", i2)), tok_a2], slot=f"Lstg{i2}")
            cbuf = tbufA[:, 0:512] if i2 == 0 else tbufA[:, 512:1024]
            sbuf_ = tbufB[:, 0:512] if i2 == 0 else tbufB[:, 512:1024]
            b.v(lambda v, gp_buf=gp_buf, cbuf=cbuf: v.tensor_scalar(
                cbuf, gp_buf[:], SC[:, 5:6], MAGIC, ALU.mult, ALU.add),
                deps=[t_g, tok_r2, b.F(("cB2", i2))])
            b.v(lambda v, cbuf=cbuf: v.tensor_scalar(
                cbuf, cbuf, MAGIC, -(QA + 1.0), ALU.subtract, ALU.max))
            tvc = b.v(lambda v, cbuf=cbuf: v.tensor_scalar(
                cbuf, cbuf, QA, None, ALU.min))
            tss = b.s(lambda s, cbuf=cbuf, sbuf_=sbuf_: s.activation(
                sbuf_, cbuf, ACT.Sigmoid, scale=SC[:, 4:5]),
                deps=[tvc, b.F(("sigB2", i2))])
            b.v(lambda v, sbuf_=sbuf_, gp_buf=gp_buf: v.tensor_scalar(
                gp_buf[:], sbuf_, 65536.0, MAGIC, ALU.mult, ALU.add),
                deps=[tss])
            b.setF(("sigB2", i2), b.tok("VE"))
            b.v(lambda v, gp_buf=gp_buf: v.tensor_scalar(
                gp_buf[:], gp_buf[:], MAGIC, 0.0, ALU.subtract, ALU.max))
            b.v(lambda v, gp_buf=gp_buf: v.tensor_scalar(
                gp_buf[:], gp_buf[:], 65535.0, None, ALU.min))
            mb = mb_half[i2]
            tvm = b.v(lambda v, cbuf=cbuf, gp_buf=gp_buf, mb=mb:
                      v.tensor_tensor(mb, cbuf, gp_buf[:], ALU.mult),
                      deps=[b.F(("mB2", i2))])
            b.setF(("cB2", i2), tvm)
            b.setF(("gpB2", i2), tvm)
            b.v(lambda v, mb=mb: v.tensor_reduce(
                tmp128, mb, axis=AX.X, op=ALU.max, apply_absolute_value=True))
            b.v(lambda v: v.tensor_max(macc, macc, tmp128))
            tg = b.sp(d_m[oc * 128:(oc + 1) * 128, tt * 512:(tt + 1) * 512],
                      mb, deps=[tvm], slot=f"Sctx{i2}")
            b.setF(("mB2", i2), tg)
    tok_p2a = b.tok("VE")
    b.barrier()

    t1 = finalize_partial_max(macc, 0, tok_p2a)
    tcc3 = allreduce_max(t1)
    b.v(lambda v: v.tensor_tensor(
        rowb[0:1, 16:17], rowb[0:1, 8:9], rowb[0:1, 20:21], ALU.mult),
        deps=[tcc3])
    b.v(lambda v: v.tensor_scalar(
        rowb[0:1, 16:17], rowb[0:1, 16:17], 1.0 / QA, 1e-12,
        ALU.mult, ALU.max))
    b.v(lambda v: v.reciprocal(rowb[0:1, 21:22], rowb[0:1, 16:17]))
    tv = b.v(lambda v: v.tensor_tensor(
        rowb[0:1, 17:18], rowb[0:1, 20:21], rowb[0:1, 21:22], ALU.mult))
    tok_r3 = bcast_scalars(16, 2, 7, tv)

    for oc in range(24):
        for tt in range(8):
            i2 = (oc * 8 + tt) % 2
            m_buf = stage0 if i2 == 0 else stage1
            u_buf = tbufA[:, 0:512] if i2 == 0 else tbufA[:, 512:1024]
            t_m = b.ld(m_buf[:],
                       d_m[oc * 128:(oc + 1) * 128, tt * 512:(tt + 1) * 512],
                       deps=[b.F(("m2B", i2))], slot=f"Lstg{i2}")
            t_u = b.ld(u_buf,
                       d_up[oc * 128:(oc + 1) * 128, tt * 512:(tt + 1) * 512],
                       deps=[b.F(("u2B", i2))], slot=f"LtbA{i2}")
            c3buf = tbufB[:, 0:512] if i2 == 0 else tbufB[:, 512:1024]
            b.v(lambda v, m_buf=m_buf, c3buf=c3buf: v.tensor_scalar(
                c3buf, m_buf[:], SC[:, 8:9], MAGIC, ALU.mult, ALU.add),
                deps=[t_m, tok_r3, b.F(("c3B", i2))])
            b.setF(("m2B", i2), b.tok("VE"))
            b.v(lambda v, c3buf=c3buf: v.tensor_scalar(
                c3buf, c3buf, MAGIC, -(QA + 1.0), ALU.subtract, ALU.max))
            b.v(lambda v, c3buf=c3buf: v.tensor_scalar(
                c3buf, c3buf, QA, None, ALU.min))
            b.v(lambda v, u_buf=u_buf: v.tensor_scalar(
                u_buf, u_buf, SC[:, 3:4], MAGIC, ALU.mult, ALU.add),
                deps=[t_u])
            b.v(lambda v, u_buf=u_buf: v.tensor_scalar(
                u_buf, u_buf, MAGIC, -(QA + 1.0), ALU.subtract, ALU.max))
            b.v(lambda v, u_buf=u_buf: v.tensor_scalar(
                u_buf, u_buf, QA, None, ALU.min))
            mb = mb_half[i2]
            tvm = b.v(lambda v, c3buf=c3buf, u_buf=u_buf, mb=mb:
                      v.tensor_tensor(mb, c3buf, u_buf, ALU.mult),
                      deps=[b.F(("mm2B", i2))])
            b.setF(("c3B", i2), tvm)
            b.setF(("u2B", i2), tvm)
            b.v(lambda v, mb=mb: v.tensor_reduce(
                tmp128, mb, axis=AX.X, op=ALU.max, apply_absolute_value=True))
            b.v(lambda v: v.tensor_max(mmacc, mmacc, tmp128))
            tg = b.sp(d_mm[oc * 128:(oc + 1) * 128, tt * 512:(tt + 1) * 512],
                      mb, deps=[tvm], slot=f"Sctx{i2}")
            b.setF(("mm2B", i2), tg)
    tok_p2b = b.tok("VE")
    b.barrier()

    t1 = finalize_partial_max(mmacc, 0, tok_p2b)
    tcc4 = allreduce_max(t1)
    b.v(lambda v: v.tensor_tensor(
        rowb[0:1, 22:23], SC[0:1, 7:8], SC[0:1, 2:3], ALU.mult),
        deps=[tcc4])
    b.v(lambda v: v.tensor_tensor(
        rowb[0:1, 16:17], rowb[0:1, 8:9], rowb[0:1, 22:23], ALU.mult))
    b.v(lambda v: v.tensor_scalar(
        rowb[0:1, 16:17], rowb[0:1, 16:17], 1.0 / QA, 1e-12,
        ALU.mult, ALU.max))
    b.v(lambda v: v.reciprocal(rowb[0:1, 23:24], rowb[0:1, 16:17]))
    tv = b.v(lambda v: v.tensor_tensor(
        rowb[0:1, 17:18], rowb[0:1, 22:23], rowb[0:1, 23:24], ALU.mult))
    tok_r4 = bcast_scalars(16, 2, 9, tv)

    b.barrier()
    TT256 = 16
    for tt in range(TT256):
        t0 = tt * 256
        oq_toks = []
        for ic in range(24):
            i2 = ic % 2
            mmb = stage0 if i2 == 0 else stage1
            t_mm = b.ld(mmb[:, 0:256],
                        d_mm[ic * 128:(ic + 1) * 128, t0:t0 + 256],
                        deps=[b.F(("mmC", i2))], slot=f"Lstg{i2}")
            b.v(lambda v, mmb=mmb: v.tensor_scalar(
                mmb[:, 256:512], mmb[:, 0:256], SC[:, 10:11], MAGIC,
                ALU.mult, ALU.add),
                deps=[t_mm, tok_r4])
            b.v(lambda v, mmb=mmb: v.tensor_scalar(
                mmb[:, 256:512], mmb[:, 256:512], MAGIC, -(QA + 1.0),
                ALU.subtract, ALU.max))
            b.v(lambda v, mmb=mmb: v.tensor_scalar(
                mmb[:, 0:256], mmb[:, 256:512], QA, None, ALU.min),
                deps=[b.F("bigB_B")])
            b.v(lambda v, mmb=mmb, ic=ic: v.tensor_copy(
                bigB_bf[:, 512 * ic: 512 * ic + 256], mmb[:, 0:256]))
            oq_toks.append(b.v(lambda v, mmb=mmb, ic=ic: v.tensor_tensor(
                bigB_bf[:, 512 * ic + 256: 512 * ic + 512], mmb[:, 0:256],
                bigB_bf[:, 512 * ic: 512 * ic + 256], ALU.subtract)))
            b.setF(("mmC", i2), b.tok("VE"))
        t_oq = oq_toks[-1]
        x1_toks = []
        for oc2 in range(8):
            sl = oc2 % 2
            twA = b.ld(wsA_bf[:, 0:3072].rearrange("p (a m) -> p a m",
                                                    m=256),
                       d_wdownT[oc2, 0:12].rearrange("a p m -> p a m"),
                       deps=[b.F(("wsC", 0)), tok_wprep], slot="Lws0")
            twB = b.ld(wsB_bf[:, 0:3072].rearrange("p (a m) -> p a m",
                                                   m=256),
                       d_wdownT[oc2, 12:24].rearrange("a p m -> p a m"),
                       deps=[b.F(("wsC", 1)), tok_wprep], slot="Lws1")
            pp = ps0 if sl == 0 else ps1
            mm_tok = None
            for ic in range(24):
                wbf_ = wsA_bf if ic < 12 else wsB_bf
                icl = ic if ic < 12 else ic - 12
                wh = wbf_[:, icl * 256: icl * 256 + 128]
                wl = wbf_[:, icl * 256 + 128: icl * 256 + 256]
                rh = bigB_bf[:, 512 * ic: 512 * ic + 256]
                rl_ = bigB_bf[:, 512 * ic + 256: 512 * ic + 512]
                b.t(lambda t, pp=pp, wh=wh, rh=rh, ic=ic: t.matmul(
                    pp[:, 0:256], wh, rh, start=(ic == 0), stop=False),
                    deps=[twA if ic < 12 else twB, oq_toks[ic],
                          b.F(("psC", sl))], inc=False)
                b.t(lambda t, pp=pp, wh=wh, rl_=rl_: t.matmul(
                    pp[:, 0:256], wh, rl_, start=False, stop=False),
                    inc=False)
                mm_tok = b.t(lambda t, pp=pp, wl=wl, rh=rh, ic=ic: t.matmul(
                    pp[:, 0:256], wl, rh, start=False, stop=(ic == 23)),
                    inc=(ic == 23))
            b.setF(("wsC", 0), mm_tok)
            b.setF(("wsC", 1), mm_tok)
            gb = tbufA[:, 0:256] if sl == 0 else tbufA[:, 256:512]
            tsv = b.s(lambda s, pp=pp, gb=gb: s.activation(
                gb, pp[:, 0:256], ACT.Silu, scale=SC[:, 9:10]),
                deps=[mm_tok, b.F(("gC", sl))])
            b.setF(("psC", sl), tsv)
            xb = tbufB[:, 0:256] if sl == 0 else tbufB[:, 256:512]
            sb_ = tbufB[:, 512:768] if sl == 0 else tbufB[:, 768:1024]
            t_x = b.ld(xb, d_xfm[oc2 * 128:(oc2 + 1) * 128, t0:t0 + 256],
                       deps=[b.F(("xC", sl))], slot=f"LtbB{sl}")
            t_s = b.ld(sb_, d_satt[oc2 * 128:(oc2 + 1) * 128, t0:t0 + 256],
                       deps=[b.F(("sC", sl)), tok_a2], slot=f"LtbB{2 + sl}")
            sgb = tbufA[:, 512 + sl * 256: 512 + (sl + 1) * 256]
            tv = b.v(lambda v, sb_=sb_, gb=gb, sgb=sgb: v.tensor_tensor(
                sgb, sb_, gb, ALU.mult), deps=[tsv, t_s])
            b.setF(("gC", sl), tv)
            b.setF(("sC", sl), tv)
            tvx1 = b.v(lambda v, xb=xb, sgb=sgb, oc2=oc2: v.tensor_tensor(
                bigB[:, 6144 + oc2 * 256: 6144 + (oc2 + 1) * 256],
                xb, sgb, ALU.add),
                deps=[t_x, b.F(("x1sp", oc2))])
            b.setF(("xC", sl), tvx1)
            x1_toks.append(tvx1)
            tgx = b.sp(d_x1[oc2 * 128:(oc2 + 1) * 128, t0:t0 + 256],
                       bigB[:, 6144 + oc2 * 256: 6144 + (oc2 + 1) * 256],
                       deps=[tvx1], slot=f"Sx1{oc2 % 4}")
            b.setF(("x1sp", oc2), tgx)
        tok_x1 = x1_toks[-1]
        mm_tok = None
        for kc in range(DC):
            x1c = bigB[:, 6144 + kc * 256: 6144 + (kc + 1) * 256]
            mm_tok = b.t(lambda t, x1c=x1c, kc=kc: t.matmul(
                pst[0:1, 0:256], ones128, x1c,
                start=(kc == 0), stop=(kc == DC - 1)),
                deps=[tok_x1, b.F("pstB")], inc=(kc == DC - 1))
        for kc in range(DC):
            x1c = bigB[:, 6144 + kc * 256: 6144 + (kc + 1) * 256]
            i2 = kc % 2
            sqb = stage0 if i2 == 0 else stage1
            tsq = b.s(lambda s, x1c=x1c, sqb=sqb: s.activation(
                sqb[:, 0:256], x1c, ACT.Square),
                deps=[tok_x1, b.F(("sqB", i2))])
            mm_tok = b.t(lambda t, sqb=sqb, kc=kc: t.matmul(
                pst[0:1, 256:512], ones128, sqb[:, 0:256],
                start=(kc == 0), stop=(kc == DC - 1)),
                deps=[tsq], inc=True)
            b.setF(("sqB", i2), mm_tok)
        tv = b.v(lambda v: v.tensor_scalar(
            rb_m, pst[0:1, 0:256], 1.0 / D, None, ALU.mult), deps=[mm_tok])
        tvq = b.v(lambda v: v.tensor_scalar(
            rb_msq, pst[0:1, 256:512], 1.0 / D, None, ALU.mult))
        b.setF("pstB", tvq)
        b.v(lambda v: v.tensor_tensor(rb_m2, rb_m, rb_m, ALU.mult))
        tvv = b.v(lambda v: v.tensor_tensor(rb_var, rb_msq, rb_m2,
                                            ALU.subtract))
        tsv = b.s(lambda s: s.activation(rb_vstd, rb_var, ACT.Sqrt,
                                         bias=rowb[0:1, 32:33]),
                  deps=[tvv])
        tvr = b.v(lambda v: v.reciprocal(rb_rstd, rb_vstd), deps=[tsv])
        tbm = b.t(lambda t: t.matmul(psp[:, 0:256], ones_1, rb_m,
                                     start=True, stop=True),
                  deps=[tv, b.F("pspB")])
        tbr = b.t(lambda t: t.matmul(psp[:, 256:512], ones_1, rb_rstd,
                                     start=True, stop=True), deps=[tvr])
        xn1_toks = []
        for kc in range(DC):
            x1c = bigB[:, 6144 + kc * 256: 6144 + (kc + 1) * 256]
            i2 = kc % 2
            df = stage0 if i2 == 0 else stage1
            b.v(lambda v, x1c=x1c, df=df: v.tensor_tensor(
                df[:, 256:512], x1c, psp[:, 0:256], ALU.subtract),
                deps=[tbm, tbr, b.F(("dfB", i2))])
            b.v(lambda v, df=df: v.tensor_tensor(
                df[:, 0:256], df[:, 256:512], psp[:, 256:512], ALU.mult))
            b.v(lambda v, df=df, kc=kc, t0=t0: v.tensor_copy(
                xn1_hi(kc, t0, 256), df[:, 0:256]))
            tvx = b.v(lambda v, df=df, kc=kc, t0=t0: v.tensor_tensor(
                xn1_lo(kc, t0, 256), df[:, 0:256], xn1_hi(kc, t0, 256),
                ALU.subtract))
            b.setF(("dfB", i2), tvx)
            xn1_toks.append(tvx)
        b.setF("pspB", xn1_toks[-1])
        b.setF("bigB_B", [xn1_toks[-1], b.tok("SE"), b.tok("TE")]
               + b.all_dma_tok())
        b.setF(("mmC", 0), [b.tok("SE"), b.tok("VE")])
        b.setF(("mmC", 1), [b.tok("SE"), b.tok("VE")])
    tok_xn1 = b.tok("VE")
    tok_x1sp = b.all_dma_tok()

    if upto < 7:
        return finish(nc, b)
    b.barrier()
    for ec in range(DC):
        tw = []
        for wi, dT in enumerate((d_WgT, d_WvT, d_WdT)):
            dst = (wsA_bf[:, 0:2048], wsA_bf[:, 2048:4096],
                   wsB_bf[:, 0:2048])[wi]
            tw.append(b.ld(dst.rearrange("p (a m) -> p a m", m=256),
                           dT[ec].rearrange("a p m -> p a m"),
                           deps=[b.F("wsC2"), tok_wprep],
                           slot=("Lws0", "Lws0", "Lws1")[wi]))
        for tt in range(8):
            ppl = (ps0, ps1, ps2)
            mts = []
            for wi, pp in enumerate(ppl):
                mm_tok = None
                for kc in range(DC):
                    wsl = (wsA_bf[:, 0:2048], wsA_bf[:, 2048:4096],
                           wsB_bf[:, 0:2048])[wi]
                    wh = wsl[:, kc * 256: kc * 256 + 128]
                    wl = wsl[:, kc * 256 + 128: kc * 256 + 256]
                    rh = xn1_hi(kc, tt * 512, 512)
                    rl_ = xn1_lo(kc, tt * 512, 512)
                    b.t(lambda t, pp=pp, wh=wh, rh=rh, kc=kc: t.matmul(
                        pp[:], wh, rh, start=(kc == 0), stop=False),
                        deps=[tw[1] if wi < 2 else tw[2], tok_xn1,
                              b.F(("psC2", wi))], inc=False)
                    b.t(lambda t, pp=pp, wh=wh, rl_=rl_: t.matmul(
                        pp[:], wh, rl_, start=False, stop=False), inc=False)
                    mm_tok = b.t(lambda t, pp=pp, wl=wl, rh=rh, kc=kc:
                                 t.matmul(pp[:], wl, rh, start=False,
                                          stop=(kc == DC - 1)),
                                 inc=(kc == DC - 1))
                mts.append(mm_tok)
            ts1 = b.s(lambda s, ec=ec: s.activation(
                tbufA[:, 0:512], ps0[:], ACT.Sigmoid,
                bias=sb_bg[:, ec:ec + 1]),
                deps=[mts[0], tsl, b.F("tbAC")])
            ts2 = b.s(lambda s, ec=ec: s.activation(
                tbufA[:, 512:1024], ps1[:], ACT.Tanh,
                bias=sb_bv[:, ec:ec + 1]), deps=[mts[1]])
            b.setF(("psC2", 0), ts1)
            b.setF(("psC2", 1), ts2)
            tvxs = b.v(lambda v, tt=tt: v.tensor_tensor(
                bigB[:, tt * 512:(tt + 1) * 512],
                tbufA[:, 0:512], tbufA[:, 512:1024], ALU.mult),
                deps=[ts2, b.F("bigB_C")])
            ts3 = b.s(lambda s, ec=ec: s.activation(
                tbufB[:, 0:512], ps2[:], ACT.Sigmoid,
                bias=sb_bd[:, ec:ec + 1]), deps=[mts[2], b.F("tbBC")])
            b.setF(("psC2", 2), ts3)
            tva = b.v(lambda v, tt=tt: v.tensor_scalar(
                bigB[:, 4096 + tt * 512: 4096 + (tt + 1) * 512],
                tbufB[:, 0:512], 0.998, 0.001, ALU.mult, ALU.add),
                deps=[ts3])
            b.setF("tbAC", tvxs)
            b.setF("tbBC", tva)
        tvscan = b.v(lambda v: v.tensor_tensor_scan(
            bigB[:, 0:4096], bigB[:, 4096:8192], bigB[:, 0:4096],
            0.0, ALU.mult, ALU.add))
        tg = b.sp(d_s1[ec * 128:(ec + 1) * 128, :], bigB[:, 0:4096],
                  deps=[tvscan], slot="Ss1")
        b.setF("bigB_C", tg)
        b.setF("wsC2", b.tok("TE"))
    tok_c = b.all_dma_tok()

    if upto < 8:
        return finish(nc, b)
    b.barrier()
    for tt in range(TT256):
        t0 = tt * 256
        mm_tok = None
        for kc in range(DC):
            wh = rw_bf[:, kc * 8: kc * 8 + 8]
            wl = rw_bf[:, 64 + kc * 8: 64 + kc * 8 + 8]
            rh = xn1_hi(kc, t0, 256)
            rl_ = xn1_lo(kc, t0, 256)
            b.t(lambda t, wh=wh, rh=rh, kc=kc: t.matmul(
                pst[0:8, 0:256], wh, rh, start=(kc == 0), stop=False),
                deps=[tsl, tok_xn1, b.F("pstD")], inc=False)
            b.t(lambda t, wh=wh, rl_=rl_: t.matmul(
                pst[0:8, 0:256], wh, rl_, start=False, stop=False),
                inc=False)
            mm_tok = b.t(lambda t, wl=wl, rh=rh, kc=kc: t.matmul(
                pst[0:8, 0:256], wl, rh, start=False,
                stop=(kc == DC - 1)), inc=(kc == DC - 1))
        tv8 = b.v(lambda v: v.tensor_copy(stage0[0:8, 0:256],
                                          pst[0:8, 0:256]),
                  deps=[mm_tok, b.F("st0D")])
        b.setF("pstD", tv8)
        w_toks = []
        for sc in range(2):
            pr = probsA if sc == 0 else probsB

            def evR(dst, tt, pr=pr):
                return b.v(lambda v: v.tensor_copy(pr, dst), deps=[tt])
            tvp = do_transpose(stage0[0:8, sc * 128:(sc + 1) * 128], 128, 8,
                               ident[0:8, 0:8], evR, deps=[tv8])
            b.v(lambda v, pr=pr: v.reduce_max(rowm, pr, axis=AX.X))
            b.v(lambda v: v.tensor_scalar(nrowm, rowm, -1.0, None, ALU.mult))
            b.v(lambda v: v.engine_nop())
            tvn = b.v(lambda v: v.engine_nop())
            tse = b.s(lambda s, pr=pr: s.activation(
                r_eq, pr, ACT.Exp, bias=nrowm, accum_out=lsum), deps=[tvn])
            b.v(lambda v: v.reciprocal(rlsum, lsum), deps=[tse])
            b.v(lambda v: v.engine_nop())
            b.v(lambda v: v.engine_nop())
            b.v(lambda v: v.tensor_scalar(r_q, r_eq, rlsum, None, ALU.mult))
            b.v(lambda v: v.reduce_max(r_m1, r_q, axis=AX.X))
            b.v(lambda v: v.engine_nop())
            b.v(lambda v: v.engine_nop())
            b.v(lambda v: v.tensor_scalar(r_eq, r_q, r_m1, None,
                                          ALU.is_equal))
            b.v(lambda v: v.scalar_tensor_tensor(
                r_eq, r_eq, -1e30, r_q, ALU.mult, ALU.add))
            b.v(lambda v: v.reduce_max(r_m2, r_eq, axis=AX.X))
            b.v(lambda v: v.tensor_tensor(r_den, r_m1, r_m2, ALU.add))
            b.v(lambda v: v.engine_nop())
            b.v(lambda v: v.reciprocal(r_rden, r_den))
            b.v(lambda v: v.engine_nop())
            b.v(lambda v: v.engine_nop())
            b.v(lambda v: v.tensor_tensor(r_a1, r_m1, r_rden, ALU.mult))
            b.v(lambda v: v.tensor_tensor(r_a2, r_m2, r_rden, ALU.mult))
            w_toks.append(b.v(lambda v, sc=sc: v.tensor_tensor(
                w_col[:, sc:sc + 1], r_a1, r_a2, ALU.add)))
        b.setF("st0D", w_toks[-1])
        def evW(dst, tt):
            return b.v(lambda v: v.tensor_copy(stage1[0:2, 0:128], dst),
                       deps=[tt, b.F("st1D")])
        tvw = do_transpose(w_col, 2, 128, ident[:], evW, deps=[w_toks[-1]])
        tgw = b.sp(rb_wrow, stage1[0:2, 0:128], deps=[tvw], slot="Swrow")
        b.setF("st1D", tgw)
        tbw = b.t(lambda t: t.matmul(psp[:, 0:256], ones_1, rb_wrow,
                                     start=True, stop=True),
                  deps=[tgw, b.F("pspD")])
        for ocf in range(32):
            sl = ocf % 2
            wbf = wsA_bf if sl == 0 else wsB_bf
            t_wh = b.ld(wbf[:, 0:1024].rearrange("p (c m) -> p c m", m=128),
                        d_w1h[:, ocf * 128:(ocf + 1) * 128].rearrange(
                            "(c p) m -> p c m", p=128),
                        deps=[b.F(("wsD", sl))], slot=f"Lws{sl}")
            t_w = b.ld(wbf[:, 1024:2048].rearrange("p (c m) -> p c m", m=128),
                       d_w1l[:, ocf * 128:(ocf + 1) * 128].rearrange(
                           "(c p) m -> p c m", p=128),
                       deps=[b.F(("wsD", sl))], slot=f"Lws{sl}l")
            pp = ps0 if sl == 0 else ps1
            mm_tok = None
            for kc in range(DC):
                wh = wbf[:, kc * 128: kc * 128 + 128]
                wl = wbf[:, 1024 + kc * 128: 1024 + kc * 128 + 128]
                rh = xn1_hi(kc, t0, 256)
                rl_ = xn1_lo(kc, t0, 256)
                b.t(lambda t, pp=pp, wh=wh, rh=rh, kc=kc: t.matmul(
                    pp[:, 0:256], wh, rh, start=(kc == 0), stop=False),
                    deps=[t_wh, t_w, tok_xn1, b.F(("psD", sl))], inc=False)
                b.t(lambda t, pp=pp, wh=wh, rl_=rl_: t.matmul(
                    pp[:, 0:256], wh, rl_, start=False, stop=False),
                    inc=False)
                mm_tok = b.t(lambda t, pp=pp, wl=wl, rh=rh, kc=kc: t.matmul(
                    pp[:, 0:256], wl, rh, start=False, stop=(kc == DC - 1)),
                    inc=(kc == DC - 1))
            b.setF(("wsD", sl), mm_tok)
            gf = tbufA[:, 0:256] if sl == 0 else tbufA[:, 256:512]
            tsv = b.s(lambda s, pp=pp, ocf=ocf, gf=gf: s.activation(
                gf, pp[:, 0:256],
                ACT.Gelu_apprx_tanh, bias=sb_b1[:, ocf:ocf + 1]),
                deps=[mm_tok, tok_c, b.F("bigB_D"), b.F(("gD", sl))])
            b.setF(("psD", sl), tsv)
            b.v(lambda v, gf=gf, ocf=ocf: v.tensor_copy(
                bigB_bf[:, 512 * ocf: 512 * ocf + 256], gf), deps=[tsv])
            tvl = b.v(lambda v, gf=gf, ocf=ocf: v.tensor_tensor(
                bigB_bf[:, 512 * ocf + 256: 512 * ocf + 512], gf,
                bigB_bf[:, 512 * ocf: 512 * ocf + 256], ALU.subtract))
            b.setF(("gD", sl), tvl)
        tok_h1 = [b.tok("SE"), b.tok("VE")]
        fin_toks = []
        pend = [None]

        def flush_fin():
            if pend[0] is None:
                return
            oc2p, fb, tvf_p = pend[0]
            pend[0] = None
            for tc in range(2):
                ot = qa if tc == 0 else ctxtm

                def evF(dst, tt, oc2p=oc2p, ot=ot, tc=tc):
                    return b.v(lambda v: v.tensor_copy(
                        ot[:, oc2p * 128:(oc2p + 1) * 128], dst),
                        deps=[tt, b.F(("otD", tc))])
                tvo = do_transpose(fb[:, tc * 128:(tc + 1) * 128], 128, 128,
                                   ident[:], evF, deps=[tvf_p])
                fin_toks.append(tvo)
            b.setF(("finD", oc2p % 2), b.tok("TE"))

        for oc2 in range(8):
            mm_tok = None
            for half in range(2):
                wbf = wsA_bf if half == 0 else wsB_bf
                t_wh2 = b.ld(
                    wbf[:, 0:2048].rearrange("p (c m) -> p c m", m=128),
                    d_w2h[half * 2048:(half + 1) * 2048,
                          oc2 * 128:(oc2 + 1) * 128].rearrange(
                        "(c p) m -> p c m", p=128),
                    deps=[b.F(("wsD", half)), b.F(("wsD2", half))],
                    slot=f"Lws{half}")
                t_w = b.ld(wbf[:, 2048:4096].rearrange(
                               "p (c m) -> p c m", m=128),
                           d_w2l[half * 2048:(half + 1) * 2048,
                                 oc2 * 128:(oc2 + 1) * 128].rearrange(
                               "(c p) m -> p c m", p=128),
                           deps=[b.F(("wsD", half)), b.F(("wsD2", half))],
                           slot=f"Lws{half}l")
                for icq in range(16):
                    ic = half * 16 + icq
                    wh = wbf[:, icq * 128: icq * 128 + 128]
                    wl = wbf[:, 2048 + icq * 128: 2048 + icq * 128 + 128]
                    rh = bigB_bf[:, 512 * ic: 512 * ic + 256]
                    rl_ = bigB_bf[:, 512 * ic + 256: 512 * ic + 512]
                    b.t(lambda t, wh=wh, rh=rh, ic=ic: t.matmul(
                        ps2[:, 0:256], wh, rh, start=(ic == 0), stop=False),
                        deps=[t_wh2, t_w, tok_h1, b.F("ps2D")], inc=False)
                    b.t(lambda t, wh=wh, rl_=rl_: t.matmul(
                        ps2[:, 0:256], wh, rl_, start=False, stop=False),
                        inc=False)
                    mm_tok = b.t(lambda t, wl=wl, rh=rh, ic=ic: t.matmul(
                        ps2[:, 0:256], wl, rh, start=False, stop=(ic == 31)),
                        inc=(icq == 15))
                b.setF(("wsD2", half), mm_tok)
                b.setF(("wsD", half), mm_tok)
            flush_fin()
            tse = b.s(lambda s, oc2=oc2: s.activation(
                tbufA[:, 0:256], ps2[:, 0:256], ACT.Identity,
                bias=sb_b2[:, oc2:oc2 + 1]), deps=[mm_tok, b.F("tbAD")])
            b.setF("ps2D", tse)
            tv = b.v(lambda v: v.tensor_tensor(
                tbufA[:, 256:512], tbufA[:, 0:256], psp[:, 0:256], ALU.mult),
                deps=[tse, tbw])
            tsg = b.s(lambda s: s.activation(
                tbufA[:, 512:768], tbufA[:, 256:512], ACT.Silu), deps=[tv])
            t_s1 = b.ld(tbufB[:, 0:256],
                        d_s1[oc2 * 128:(oc2 + 1) * 128, t0:t0 + 256],
                        deps=[b.F("tbBD"), tok_c], slot="LtbB0")
            t_x1 = b.ld(tbufB[:, 256:512],
                        d_x1[oc2 * 128:(oc2 + 1) * 128, t0:t0 + 256],
                        deps=[tok_x1sp], slot="LtbB1")
            tv = b.v(lambda v: v.tensor_tensor(
                tbufA[:, 768:1024], tbufB[:, 0:256], tbufA[:, 512:768],
                ALU.mult), deps=[tsg, t_s1])
            fb = tbufB[:, 512:768] if oc2 % 2 == 0 else tbufB[:, 768:1024]
            tvf = b.v(lambda v, fb=fb: v.tensor_tensor(
                fb, tbufB[:, 256:512], tbufA[:, 768:1024],
                ALU.add), deps=[t_x1, b.F(("finD", oc2 % 2))])
            b.setF("tbAD", tvf)
            pend[0] = (oc2, fb, tvf)
            b.setF("tbBD", b.tok("VE"))
        flush_fin()
        b.setF("bigB_D", b.tok("TE"))
        tg0 = b.sp(d_out[t0:t0 + 128, :], qa[:], deps=[fin_toks[-1]],
                   slot="Sout0")
        tg1 = b.sp(d_out[t0 + 128:t0 + 256, :], ctxtm[:],
                   deps=[fin_toks[-1]], slot="Sout1")
        b.setF(("otD", 0), tg0)
        b.setF(("otD", 1), tg1)
        b.setF("pspD", b.tok("VE"))

    return finish(nc, b)


def finish(nc, b):
    for s in list(b.cnt):
        if s in ("TE", "VE", "SE", "CC", "GP", "LD"):
            continue
        if b.cnt[s] > 0:
            b.emit("gpsimd",
                   lambda e, h=b.sems[s], v=b.cnt[s]: e.wait_ge(h, v))
    b.emit("gpsimd", lambda e, h=b.sems["GP"], v=b.cnt["GP"]: e.wait_ge(h, v))

    with nc.Block() as block:
        @block.sync
        def _(eng):
            for f in b.prog["sync"]:
                f(eng)

        @block.tensor
        def _(eng):
            for f in b.prog["tensor"]:
                f(eng)

        @block.vector
        def _(eng):
            for f in b.prog["vector"]:
                f(eng)

        @block.scalar
        def _(eng):
            for f in b.prog["scalar"]:
                f(eng)

        @block.gpsimd
        def _(eng):
            for f in b.prog["gpsimd"]:
                f(eng)

    return nc


_NC_CACHE = {}


def kernel(**inputs):
    if "main" not in _NC_CACHE:
        _NC_CACHE["main"] = build_nc(dbg=False)
    nc = _NC_CACHE["main"]
    f32 = np.float32

    def r2(a, cols):
        return np.ascontiguousarray(
            np.asarray(a, dtype=f32).reshape(cols, 128).T)

    base = {
        "wdec": np.ascontiguousarray(inputs["L0_attn_w_dec_attn"], dtype=f32),
        "wout": np.ascontiguousarray(inputs["L0_attn_w_out"], dtype=f32),
        "bout": r2(inputs["L0_attn_b_out"], 8),
        "wgate": np.ascontiguousarray(inputs["L0_ffn_w_gate"], dtype=f32),
        "wup": np.ascontiguousarray(inputs["L0_ffn_w_up"], dtype=f32),
        "wdown": np.ascontiguousarray(inputs["L0_ffn_w_down"], dtype=f32),
        "Wg": np.ascontiguousarray(inputs["L1_attn_Wg"], dtype=f32),
        "Wv": np.ascontiguousarray(inputs["L1_attn_Wv"], dtype=f32),
        "Wd": np.ascontiguousarray(inputs["L1_attn_Wd"], dtype=f32),
        "bg": r2(inputs["L1_attn_bg"], 8),
        "bv": r2(inputs["L1_attn_bv"], 8),
        "bd": r2(inputs["L1_attn_bd"], 8),
        "rw": np.ascontiguousarray(inputs["L1_ffn_router_w"], dtype=f32),
        "w1": np.ascontiguousarray(inputs["L1_ffn_w1"], dtype=f32),
        "b1": r2(inputs["L1_ffn_b1"], 32),
        "w2": np.ascontiguousarray(inputs["L1_ffn_w2"], dtype=f32),
        "b2": r2(inputs["L1_ffn_b2"], 8),
    }
    x_full = np.asarray(inputs["L0_attn_hidden_states"], dtype=f32)
    in_maps = []
    for bb in range(8):
        m = dict(base)
        m["x"] = np.ascontiguousarray(x_full[bb])
        in_maps.append(m)
    res = run_bass_kernel_spmd(nc, in_maps, core_ids=list(range(8)))
    return np.stack([np.asarray(res.results[bb]["out"]) for bb in range(8)],
                    axis=0)
